# version 01
# baseline (speedup 1.0000x reference)
import contextlib
import numpy as np
import ml_dtypes
import concourse.bass as bass
import concourse.mybir as mybir
from concourse.bass_core_rust import IndirectOffsetOnAxis
from concourse.bass_utils import run_bass_kernel_spmd

F32 = mybir.dt.float32
BF16 = mybir.dt.bfloat16
I32 = mybir.dt.int32
AF = mybir.ActivationFunctionType
ALU = mybir.AluOpType
BF = ml_dtypes.bfloat16
f32 = np.float32

NC = 8
B = 4
S = 8192
SL = S // NC
D = 1024
H = 16
DK = 64
NKC = D // 128
NFC = 3 * D // 128
NTT = SL // 128
LN_EPS = 1e-5

_cache = {}
PP_STOP = 'full'


class Sched:

    ENGINES = ("sync", "vector", "scalar", "tensor", "gpsimd")
    SERIAL = False

    def __init__(self, nc):
        self.nc = nc
        self.steps = []
        if Sched.SERIAL:
            self._g = nc.semaphore("g_serial").__enter__()

    def add(self, engine, fn, waits=(), incs=()):
        if Sched.SERIAL:
            if not hasattr(self, "_gv"):
                self._gv = 0
            waits = ((self._g, self._gv),)
            is_dma_engine = engine in ("sync", "gpsimd")
            amt = 16 if is_dma_engine else 1
            self._gv += amt
            incs = ((self._g, amt),)
            fn0 = fn
            def fn(e, fn0=fn0, engine=engine, is_dma=is_dma_engine):
                inst = fn0(e)
                if inst is None:
                    if is_dma:
                        inst = e.dma_start(out=self._nopt[0:1, 0:1], in_=self._nopt[0:1, 1:2])
                    elif engine == "scalar":
                        inst = e.mul(self._nopt[0:1, 0:1], self._nopt[0:1, 0:1], 1.0)
                    else:
                        inst = e.memset(self._nopt[0:1, 0:1], 0)
                return inst
        self.steps.append((engine, fn, tuple(waits), tuple(incs)))

    def run(self):
        per = {k: [] for k in self.ENGINES}
        for st in self.steps:
            per[st[0]].append(st)
        with self.nc.Block() as blk:
            def mk(eng_steps):
                def body(e):
                    for _, fn, waits, incs in eng_steps:
                        for sem, val in waits:
                            e.wait_ge(sem, val)
                        inst = fn(e)
                        if incs:
                            assert inst is not None
                            for sem, amt in incs:
                                inst = inst.then_inc(sem, amt)
                return body
            for name in self.ENGINES:
                if per[name]:
                    getattr(blk, name)(mk(per[name]))


class Counter:
    __slots__ = ("v",)
    def __init__(self):
        self.v = 0
    def inc(self, n=1):
        self.v += n
        return self.v


def _wchunks_T(w):
    out_f, in_d = w.shape
    r = w.reshape(out_f // 128, 128, in_d // 128, 128)
    return np.ascontiguousarray(r.transpose(2, 0, 3, 1))


def _wchunks(w):
    in_d, out_f = w.shape
    r = w.reshape(in_d // 128, 128, out_f // 128, 128)
    return np.ascontiguousarray(r.transpose(0, 2, 1, 3))


def emit_weight_cast(sch, sem_ld, sem_cast, nw, src_ap_fn, wstg, wbf, nst):
    ld, ca = Counter(), Counter()
    for i in range(nw):
        waits = [(sem_cast, i - nst + 1)] if i >= nst else []
        def loadf(e, i=i):
            return e.dma_start(out=wstg[:, (i % nst) * 128:(i % nst + 1) * 128],
                               in_=src_ap_fn(i))
        sch.add("sync", loadf, waits=waits, incs=[(sem_ld, 16)])
        def castf(e, i=i):
            return e.tensor_copy(out=wbf[:, i * 128:(i + 1) * 128],
                                 in_=wstg[:, (i % nst) * 128:(i % nst + 1) * 128])
        sch.add("vector", castf, waits=[(sem_ld, 16 * (i + 1))], incs=[(sem_cast, 1)])
    return nw


def build_qkv0():
    if "qkv0" in _cache:
        return _cache["qkv0"]
    nc = bass.Bass("TRN2", target_bir_lowering=False, debug=False)
    x = nc.dram_tensor("x", [B, SL, D], F32, kind="ExternalInput").ap()
    wq = nc.dram_tensor("wq", [NKC, NFC, 128, 128], F32, kind="ExternalInput").ap()
    ident = nc.dram_tensor("ident", [128, 128], BF16, kind="ExternalInput").ap()
    bd64 = nc.dram_tensor("bd64", [128, 2], BF16, kind="ExternalInput").ap()
    sel2 = nc.dram_tensor("sel2", [2, 128], F32, kind="ExternalInput").ap()
    qkv_o = nc.dram_tensor("qkv_o", [B, NFC, 128, SL], BF16, kind="ExternalOutput").ap()

    NW = NKC * NFC
    NST = 4
    es = contextlib.ExitStack()
    with es:
        wstg = es.enter_context(nc.sbuf_tensor("wstg", [128, NST * 128], F32))
        wbf = es.enter_context(nc.sbuf_tensor("wbf", [128, NW * 128], BF16))
        idt = es.enter_context(nc.sbuf_tensor("idt", [128, 128], BF16))
        bdt = es.enter_context(nc.sbuf_tensor("bdt", [128, 2], BF16))
        sl2 = es.enter_context(nc.sbuf_tensor("sl2", [2, 128], F32))
        xtok = es.enter_context(nc.sbuf_tensor("xtok", [128, 2 * D], F32))
        xtokb = es.enter_context(nc.sbuf_tensor("xtokb", [128, 2 * D], BF16))
        xfm = es.enter_context(nc.sbuf_tensor("xfm", [128, NKC * SL], BF16))
        qkvs = es.enter_context(nc.sbuf_tensor("qkvs", [128, NFC * SL], BF16))
        ksq = es.enter_context(nc.sbuf_tensor("ksq", [128, SL], BF16))
        mrow = es.enter_context(nc.sbuf_tensor("mrow", [2, 3 * SL], F32))
        pst0 = es.enter_context(nc.psum_tensor("pst0", [128, 512], BF16))
        pst1 = es.enter_context(nc.psum_tensor("pst1", [128, 512], BF16))
        psm = es.enter_context(nc.psum_tensor("psm", [128, 2 * 512], F32))
        pks = es.enter_context(nc.psum_tensor("pks", [128, 1024], F32))
        pkb = es.enter_context(nc.psum_tensor("pkb", [128, 1024], F32))
        s_ld = es.enter_context(nc.semaphore("s_ld"))
        s_wslot = [es.enter_context(nc.semaphore(f"s_wslot{j}")) for j in range(4)]
        s_xslot = [es.enter_context(nc.semaphore(f"s_xslot{j}")) for j in range(2)]
        s_cast = es.enter_context(nc.semaphore("s_cast"))
        s_ldx = es.enter_context(nc.semaphore("s_ldx"))
        s_xc = es.enter_context(nc.semaphore("s_xc"))
        s_tp = es.enter_context(nc.semaphore("s_tp"))
        s_ev = es.enter_context(nc.semaphore("s_ev"))
        s_mm = es.enter_context(nc.semaphore("s_mm"))
        s_qe = es.enter_context(nc.semaphore("s_qe"))
        s_kst = es.enter_context(nc.semaphore("s_kst"))
        s_krow = es.enter_context(nc.semaphore("s_krow"))
        s_kbc = es.enter_context(nc.semaphore("s_kbc"))
        s_kn = es.enter_context(nc.semaphore("s_kn"))
        s_st = es.enter_context(nc.semaphore("s_st"))

        sch = Sched(nc)
        sch._nopt = wstg
        sch.add("sync", lambda e: e.dma_start(out=idt[:], in_=ident[:, :]), incs=[(s_ld, 16)])
        sch.add("sync", lambda e: e.dma_start(out=bdt[:], in_=bd64[:, :]), incs=[(s_ld, 16)])
        sch.add("sync", lambda e: e.dma_start(out=sl2[:], in_=sel2[:, :]), incs=[(s_ld, 16)])
        ldc = Counter(); ldc.inc(3)
        cac = Counter()
        for i in range(NW):
            waits = [(s_cast, i - NST + 1)] if i >= NST else []
            sch.add("sync",
                    (lambda e, i=i: e.dma_start(out=wstg[:, (i % NST) * 128:(i % NST + 1) * 128],
                                                in_=wq[i // NFC, i % NFC, :, :])),
                    waits=waits, incs=[(s_wslot[i % NST], 16)])
            sch.add("vector",
                    (lambda e, i=i: e.tensor_copy(out=wbf[:, i * 128:(i + 1) * 128],
                                                  in_=wstg[:, (i % NST) * 128:(i % NST + 1) * 128])),
                    waits=[(s_wslot[i % NST], 16 * (i // NST + 1))], incs=[(s_cast, 1)])

        ldx = Counter(); xc = Counter(); tp = Counter(); ev = Counter()
        mm = Counter(); qe = Counter(); kst = Counter(); krow = Counter()
        kbc = Counter(); kn = Counter(); st = Counter()

        for b in range(B):
            for t in range(NTT):
                i = b * NTT + t
                waits = [(s_xc, i - 1)] if i >= 2 else []
                sch.add("sync",
                        (lambda e, b=b, t=t, i=i: e.dma_start(
                            out=xtok[:, (i % 2) * D:(i % 2 + 1) * D],
                            in_=x[b, t * 128:(t + 1) * 128, :])),
                        waits=waits, incs=[(s_xslot[i % 2], 16)])
                ldx.inc()
                cw = [(s_xslot[i % 2], 16 * (i // 2 + 1))]
                if i >= 2:
                    cw.append((s_tp, 8 * (i - 1)))
                sch.add("vector",
                        (lambda e, i=i: e.tensor_copy(
                            out=xtokb[:, (i % 2) * D:(i % 2 + 1) * D],
                            in_=xtok[:, (i % 2) * D:(i % 2 + 1) * D])),
                        waits=cw, incs=[(s_xc, 1)])
                xc.inc()
                for kc in range(NKC):
                    j = i * NKC + kc
                    waits = [(s_xc, xc.v)]
                    if j >= 2:
                        waits.append((s_ev, j - 1))
                    sch.add("tensor",
                            (lambda e, i=i, kc=kc, j=j: e.transpose(
                                out=(pst0 if j % 2 == 0 else pst1)[:, 0:128],
                                in_=xtokb[:, (i % 2) * D + kc * 128:(i % 2) * D + (kc + 1) * 128],
                                identity=idt[:])),
                            waits=waits, incs=[(s_tp, 1)])
                    tp.inc()
                    sch.add("vector",
                            (lambda e, t=t, kc=kc, j=j: e.tensor_copy(
                                out=xfm[:, kc * SL + t * 128: kc * SL + (t + 1) * 128],
                                in_=(pst0 if j % 2 == 0 else pst1)[:, 0:128])),
                            waits=[(s_tp, tp.v)], incs=[(s_ev, 1)])
                    ev.inc()

            for fc in range(NFC):
                for blk2 in range(SL // 512):
                    m = (b * NFC + fc) * (SL // 512) + blk2
                    waits = [(s_ev, ev.v), (s_cast, NW)]
                    if m >= 1:
                        waits.append((s_qe, m))
                    def mmf(e, fc=fc, blk2=blk2, m=m):
                        last = None
                        for kc in range(NKC):
                            last = e.matmul(
                                out=psm[:, (m % 2) * 512:(m % 2) * 512 + 512],
                                lhsT=wbf[:, (kc * NFC + fc) * 128:(kc * NFC + fc + 1) * 128],
                                rhs=xfm[:, kc * SL + blk2 * 512: kc * SL + blk2 * 512 + 512],
                                start=(kc == 0), stop=(kc == NKC - 1))
                        return last
                    sch.add("tensor", mmf, waits=waits, incs=[(s_mm, 1)])
                    mm.inc()
                    sch.add("vector",
                            (lambda e, fc=fc, blk2=blk2, m=m: e.tensor_copy(
                                out=qkvs[:, fc * SL + blk2 * 512: fc * SL + blk2 * 512 + 512],
                                in_=psm[:, (m % 2) * 512:(m % 2) * 512 + 512])),
                            waits=[(s_mm, mm.v)], incs=[(s_qe, 1)])
                    qe.inc()

            for ik in range(8):
                fc = 8 + ik
                koff = fc * SL
                sch.add("scalar",
                        (lambda e, koff=koff: e.activation(out=ksq[:, 0:SL], in_=qkvs[:, koff:koff + SL],
                                                           func=AF.Square)),
                        waits=[(s_qe, qe.v), (s_kn, kn.v)], incs=[(s_kst, 1)])
                kst.inc()
                for blk2 in range(SL // 512):
                    so = blk2 * 512
                    def stf(e, koff=koff, so=so):
                        e.matmul(out=pks[0:2, 0:512], lhsT=bdt[:, :],
                                 rhs=qkvs[:, koff + so: koff + so + 512], start=True, stop=True)
                        return e.matmul(out=pks[0:2, 512:1024], lhsT=bdt[:, :],
                                        rhs=ksq[:, so:so + 512], start=True, stop=True)
                    sch.add("tensor", stf,
                            waits=[(s_kst, kst.v), (s_ld, 48), (s_krow, krow.v)], incs=[(s_kst, 1)])
                    kst.inc()
                    def rowf(e):
                        e.tensor_scalar_mul(out=mrow[0:2, 0:512], in0=pks[0:2, 0:512], scalar1=1.0 / 64)
                        e.tensor_scalar_mul(out=mrow[0:2, 512:1024], in0=pks[0:2, 512:1024], scalar1=1.0 / 64)
                        e.tensor_mul(out=mrow[0:2, 1024:1536], in0=mrow[0:2, 0:512], in1=mrow[0:2, 0:512])
                        return e.tensor_sub(out=mrow[0:2, 512:1024], in0=mrow[0:2, 512:1024],
                                            in1=mrow[0:2, 1024:1536])
                    sch.add("vector", rowf, waits=[(s_kst, kst.v), (s_kbc, kbc.v)], incs=[(s_krow, 1)])
                    krow.inc()
                    def actf(e):
                        e.activation(out=mrow[0:2, 512:1024], in_=mrow[0:2, 512:1024],
                                     func=AF.Ln, bias=np.float32(LN_EPS))
                        return e.activation(out=mrow[0:2, 512:1024], in_=mrow[0:2, 512:1024],
                                            func=AF.Exp, scale=np.float32(-0.5))
                    sch.add("scalar", actf, waits=[(s_krow, krow.v)], incs=[(s_krow, 1)])
                    krow.inc()
                    def bcf(e):
                        e.matmul(out=pkb[:, 0:512], lhsT=sl2[:, :],
                                 rhs=mrow[0:2, 0:512], start=True, stop=True)
                        return e.matmul(out=pkb[:, 512:1024], lhsT=sl2[:, :],
                                        rhs=mrow[0:2, 512:1024], start=True, stop=True)
                    w = [(s_krow, krow.v)]
                    if kbc.v:
                        w.append((s_kn, kn.v))
                    sch.add("tensor", bcf, waits=w, incs=[(s_kbc, 1)])
                    kbc.inc()
                    def nrmf(e, koff=koff, so=so):
                        sl_ = slice(koff + so, koff + so + 512)
                        e.tensor_sub(out=ksq[:, so:so + 512], in0=qkvs[:, sl_], in1=pkb[:, 0:512])
                        return e.tensor_mul(out=qkvs[:, sl_], in0=ksq[:, so:so + 512], in1=pkb[:, 512:1024])
                    sch.add("vector", nrmf, waits=[(s_kbc, kbc.v)], incs=[(s_kn, 1)])
                    kn.inc()

            for fc in range(NFC):
                w = [(s_qe, qe.v)] if fc < 8 or fc >= 16 else [(s_kn, kn.v)]
                sch.add("sync",
                        (lambda e, b=b, fc=fc: e.dma_start(out=qkv_o[b, fc, :, :],
                                                           in_=qkvs[:, fc * SL:(fc + 1) * SL])),
                        waits=w, incs=[(s_st, 16)])
                st.inc()
            sch.add("vector", lambda e: None, waits=[(s_st, 16 * st.v)])

        sch.add("sync", lambda e: None, waits=[(s_st, 16 * st.v)])
        sch.run()
    return nc


def _prep_qkv0(inputs):
    Wq = inputs["L0_attn_W_qkv"].astype(f32)
    wq = _wchunks_T(Wq)
    ident = np.eye(128, dtype=BF)
    bd64 = np.zeros((128, 2), dtype=BF)
    bd64[0:64, 0] = 1
    bd64[64:128, 1] = 1
    sel2 = np.zeros((2, 128), dtype=f32)
    sel2[0, 0:64] = 1
    sel2[1, 64:128] = 1
    x = inputs["L0_attn_x"].astype(f32)
    maps = []
    for c in range(NC):
        xs = np.ascontiguousarray(x[:, c * SL:(c + 1) * SL, :])
        maps.append({"x": xs, "wq": wq, "ident": ident, "bd64": bd64, "sel2": sel2})
    return maps


def run_qkv0(inputs):
    nc = build_qkv0()
    maps = _prep_qkv0(inputs)
    res = run_bass_kernel_spmd(nc, maps, core_ids=list(range(NC)))
    out = np.concatenate([r["qkv_o"] for r in res.results], axis=3)
    return out


def kernel(**inputs):
    x0 = inputs["L0_attn_x"].astype(f32)
    qkv_hs = run_qkv0(inputs)
    y_hs = run_scan(inputs, qkv_hs)
    x1_tok, _ = run_postproj("l0", inputs["L0_attn_W_o"].astype(f32),
                             y_hs, x0, True)
    moe_all = run_moe(inputs, x1_tok)
    x2_tok = run_combine(inputs, moe_all, x1_tok)
    qk, v = run_qkv1(inputs, x2_tok)
    o_hs = run_attn1(qk, v)
    x3_tok, x3_fm = run_postproj("l1", inputs["L1_attn_w_o"].astype(f32),
                                 o_hs, x2_tok, False)
    out = run_ffn1(inputs, x3_fm, x3_tok)
    return np.ascontiguousarray(out.astype(np.float32))


TSC = 1024
NCH = S // TSC


def build_scan():
    nc = bass.Bass("TRN2", target_bir_lowering=False, debug=False)
    q_in = nc.dram_tensor("q_in", [B, 128, S], BF16, kind="ExternalInput").ap()
    k_in = nc.dram_tensor("k_in", [B, 128, S], BF16, kind="ExternalInput").ap()
    v_in = nc.dram_tensor("v_in", [B, 128, S], BF16, kind="ExternalInput").ap()
    wltm_l = nc.dram_tensor("wltm_l", [2, 128, 32], F32, kind="ExternalInput").ap()
    wltm_T = nc.dram_tensor("wltm_T", [2, 64, 64], F32, kind="ExternalInput").ap()
    pcol = nc.dram_tensor("pcol", [128, 16], F32, kind="ExternalInput").ap()
    vbc = nc.dram_tensor("vbc", [64, 128, 128], BF16, kind="ExternalInput").ap()
    y_o = nc.dram_tensor("y_o", [B, 128, S], BF16, kind="ExternalOutput").ap()

    es = contextlib.ExitStack()
    with es:
        kr = [es.enter_context(nc.sbuf_tensor(f"kr{h}", [128, S], BF16)) for h in range(2)]
        qr = [es.enter_context(nc.sbuf_tensor(f"qr{h}", [128, S], BF16)) for h in range(2)]
        vs = es.enter_context(nc.sbuf_tensor("vs", [128, S], BF16))
        ysb = es.enter_context(nc.sbuf_tensor("ysb", [128, S], BF16))
        vbct = es.enter_context(nc.sbuf_tensor("vbct", [128, 64 * 128], BF16))
        a_sb = es.enter_context(nc.sbuf_tensor("a_sb", [128, TSC], BF16))
        g_sb = es.enter_context(nc.sbuf_tensor("g_sb", [128, TSC], BF16))
        z_sb = es.enter_context(nc.sbuf_tensor("z_sb", [128, 2 * TSC], BF16))
        state = [es.enter_context(nc.sbuf_tensor(f"state{h}", [128, 32], BF16)) for h in range(2)]
        selc = [es.enter_context(nc.sbuf_tensor(f"selc{h}", [128, 32 * 64], BF16)) for h in range(2)]
        wTb = [es.enter_context(nc.sbuf_tensor(f"wTb{h}", [64, 64], BF16)) for h in range(2)]
        Rt = es.enter_context(nc.sbuf_tensor("Rt", [128, 64], F32))
        coef = es.enter_context(nc.sbuf_tensor("coef", [128, 64], F32))
        gw = es.enter_context(nc.sbuf_tensor("gw", [128, 32], F32))
        gt1 = es.enter_context(nc.sbuf_tensor("gt1", [128, 32], F32))
        gt2 = es.enter_context(nc.sbuf_tensor("gt2", [128, 32], F32))
        gt3 = es.enter_context(nc.sbuf_tensor("gt3", [128, 32], F32))
        gvn = es.enter_context(nc.sbuf_tensor("gvn", [128, 32], F32))
        gch = es.enter_context(nc.sbuf_tensor("gch", [128, 32], F32))
        gsp = es.enter_context(nc.sbuf_tensor("gsp", [128, 32], F32))
        pct = es.enter_context(nc.sbuf_tensor("pct", [128, 16], F32))
        pc1 = es.enter_context(nc.sbuf_tensor("pc1", [128, 2], F32))
        wTf = es.enter_context(nc.sbuf_tensor("wTf", [64, 64], F32))
        v_ps = [es.enter_context(nc.psum_tensor(f"v_ps{j}", [128, TSC], F32)) for j in range(2)]
        y_ps = [es.enter_context(nc.psum_tensor(f"y_ps{j}", [128, TSC], F32)) for j in range(2)]
        s_g = es.enter_context(nc.semaphore("s_g"))
        s_ldb = es.enter_context(nc.semaphore("s_ldb"))
        s_vb = es.enter_context(nc.semaphore("s_vb"))
        s_am = es.enter_context(nc.semaphore("s_am"))
        s_z = es.enter_context(nc.semaphore("s_z"))
        s_ym = es.enter_context(nc.semaphore("s_ym"))
        s_ye = es.enter_context(nc.semaphore("s_ye"))
        s_yst = es.enter_context(nc.semaphore("s_yst"))

        sch = Sched(nc)
        sch._nopt = gw

        gv = Counter()

        def chain(eng, fn):
            w = [(s_g, gv.v)]
            amt = 16 if eng in ("sync", "gpsimd") else 1
            gv.inc(amt)
            sch.add(eng, fn, waits=w, incs=[(s_g, amt)])

        chain("sync", lambda e: e.dma_start(out=pct[:], in_=pcol[:, :]))
        chain("sync", lambda e: e.dma_start(out=vbct[:].rearrange("p (n m) -> p n m", m=128), in_=vbc.rearrange("n p m -> p n m")))
        for hl in range(2):
            R_sl = slice(hl * 32, hl * 32 + 32)
            chain("sync", lambda e, hl=hl: e.dma_start(out=gw[:], in_=wltm_l[hl, :, :]))
            chain("sync", lambda e, hl=hl: e.dma_start(out=wTf[:], in_=wltm_T[hl, :, :]))
            chain("vector", lambda e, hl=hl: e.tensor_copy(out=wTb[hl][:], in_=wTf[:]))
            chain("vector", lambda e, hl=hl: e.tensor_scalar(
                out=gvn[:], in0=gw[:], scalar1=pct[:, 0 + hl:1 + hl], scalar2=pct[:, 2 + hl:3 + hl],
                op0=ALU.add, op1=ALU.subtract))
            chain("scalar", lambda e: e.activation(out=gt1[:], in_=gvn[:], func=AF.Abs))
            chain("scalar", lambda e: e.activation(out=gt1[:], in_=gt1[:], func=AF.Exp,
                                                   scale=np.float32(-1.0)))
            chain("scalar", lambda e: e.activation(out=gt2[:], in_=gt1[:], func=AF.Ln,
                                                   bias=np.float32(1.0)))
            chain("vector", lambda e: e.tensor_scalar_max(out=gt3[:], in0=gvn[:], scalar1=0.0))
            chain("vector", lambda e: e.tensor_add(out=gsp[:], in0=gt3[:], in1=gt2[:]))
            chain("vector", lambda e, hl=hl: e.tensor_scalar_mul(
                out=gch[:], in0=gsp[:], scalar1=pct[:, 4 + hl:5 + hl]))
            chain("vector", lambda e, hl=hl: e.reciprocal(out=pc1[:, 0:1], in_=pct[:, 8 + hl:9 + hl]))
            chain("vector", lambda e: e.tensor_scalar_mul(out=pc1[:, 1:2], in0=pc1[:, 0:1], scalar1=-1.0))
            chain("scalar", lambda e, R_sl=R_sl: e.activation(out=Rt[:, R_sl], in_=gch[:],
                                                              func=AF.Exp, scale=pc1[:, 1:2]))
            chain("scalar", lambda e: e.activation(out=gt2[:], in_=gvn[:], func=AF.Sigmoid))
            chain("vector", lambda e, hl=hl: e.scalar_tensor_tensor(
                out=gt3[:], in0=gsp[:], scalar=pct[:, 6 + hl:7 + hl], in1=gt2[:],
                op0=ALU.mult, op1=ALU.mult))
            chain("vector", lambda e, hl=hl: e.tensor_single_scalar(
                out=gt1[:], in_=gch[:], scalar=pct[:, 14 + hl:15 + hl], op=ALU.subtract))
            chain("scalar", lambda e, hl=hl: e.activation(out=gt1[:], in_=gt1[:], func=AF.Tanh,
                                                          scale=pct[:, 12 + hl:13 + hl]))
            chain("vector", lambda e, hl=hl, R_sl=R_sl: e.scalar_tensor_tensor(
                out=coef[:, R_sl], in0=gt1[:], scalar=pct[:, 10 + hl:11 + hl], in1=gt3[:],
                op0=ALU.mult, op1=ALU.mult))
            chain("vector", lambda e, hl=hl: e.memset(selc[hl][:], 0))
            for i in range(32):
                chain("vector", lambda e, hl=hl, i=i: e.tensor_copy(
                    out=selc[hl][0:64, i * 64 + 2 * i: i * 64 + 2 * i + 1],
                    in_=coef[0:64, hl * 32 + i: hl * 32 + i + 1]))
                chain("vector", lambda e, hl=hl, i=i: e.tensor_copy(
                    out=selc[hl][64:128, i * 64 + 2 * i + 1: i * 64 + 2 * i + 2],
                    in_=coef[64:128, hl * 32 + i: hl * 32 + i + 1]))
        GATES_DONE = gv.v

        vb = Counter(); am = Counter(); zz = Counter(); ym = Counter()
        ye = Counter(); yst = Counter(); ldb = Counter()
        gi = Counter()
        cg = Counter()

        for b in range(B):
            lw = []
            if b > 0:
                lw = [(s_am, am.v), (s_ym, ym.v), (s_vb, vb.v)]
            def ldf(e, b=b):
                for hl in range(2):
                    rows = slice(hl * 64, hl * 64 + 64)
                    e.dma_start(out=kr[hl][0:64, :], in_=k_in[b, rows, :])
                    e.dma_start(out=kr[hl][64:128, :], in_=k_in[b, rows, :])
                    e.dma_start(out=qr[hl][0:64, :], in_=q_in[b, rows, :])
                    e.dma_start(out=qr[hl][64:128, :], in_=q_in[b, rows, :])
                return e.dma_start(out=vs[:, :], in_=v_in[b, :, :])
            def ldf2(e, b=b):
                insts = []
                for hl in range(2):
                    rows = slice(hl * 64, hl * 64 + 64)
                    insts.append(e.dma_start(out=kr[hl][0:64, :], in_=k_in[b, rows, :]))
                    insts.append(e.dma_start(out=kr[hl][64:128, :], in_=k_in[b, rows, :]))
                    insts.append(e.dma_start(out=qr[hl][0:64, :], in_=q_in[b, rows, :]))
                    insts.append(e.dma_start(out=qr[hl][64:128, :], in_=q_in[b, rows, :]))
                insts.append(e.dma_start(out=vs[:, :], in_=v_in[b, :, :]))
                for inst in insts:
                    inst.then_inc(s_ldb, 16)
                return None
            sch.add("sync", ldf2, waits=lw)
            ldb.inc(9)

            for hl in range(2):
                sch.add("vector", (lambda e, hl=hl: e.memset(state[hl][:], 0)),
                        waits=[(s_g, GATES_DONE)])
                for ch in range(NCH):
                    ccols = slice(ch * TSC, (ch + 1) * TSC)
                    cgi = cg.v
                    for i in range(32):
                        g = gi.v
                        tw = [(s_g, GATES_DONE), (s_ldb, 16 * ldb.v)]
                        if g >= 2:
                            tw.append((s_am, g - 1))
                        def vbf(e, hl=hl, i=i, ch=ch, g=g):
                            last = None
                            for q2 in range(TSC // 512):
                                last = e.matmul(
                                    out=v_ps[g % 2][:, q2 * 512:(q2 + 1) * 512],
                                    lhsT=vbct[:, (hl * 32 + i) * 128:(hl * 32 + i + 1) * 128],
                                    rhs=vs[:, ch * TSC + q2 * 512: ch * TSC + (q2 + 1) * 512],
                                    start=True, stop=True)
                            return last
                        sch.add("tensor", vbf, waits=tw, incs=[(s_vb, 1)])
                        vb.inc()
                        sch.add("vector",
                                (lambda e, hl=hl, ccols=ccols, g=g: e.tensor_mul(
                                    out=a_sb[:], in0=kr[hl][:, ccols], in1=v_ps[g % 2][:])),
                                waits=[(s_vb, vb.v)], incs=[(s_am, 1)])
                        am.inc()
                        def scf(e, hl=hl, i=i):
                            return e.tensor_tensor_scan(
                                out=g_sb[:], data0=Rt[:, hl * 32 + i: hl * 32 + i + 1].broadcast_to([128, TSC]),
                                data1=a_sb[:], initial=state[hl][:, i:i + 1],
                                op0=ALU.mult, op1=ALU.add)
                        sch.add("vector", scf)
                        sch.add("vector", (lambda e, hl=hl, i=i: e.tensor_copy(
                            out=state[hl][:, i:i + 1], in_=g_sb[:, TSC - 1:TSC])))
                        zwaits = []
                        if g >= 2:
                            zwaits.append((s_ym, g - 1))
                        sch.add("vector",
                                (lambda e, hl=hl, ccols=ccols, g=g: e.tensor_mul(
                                    out=z_sb[:, (g % 2) * TSC:(g % 2 + 1) * TSC],
                                    in0=g_sb[:], in1=qr[hl][:, ccols])),
                                waits=zwaits, incs=[(s_z, 1)])
                        zz.inc()
                        def ymf(e, hl=hl, i=i, ch=ch, g=g, cgi=cgi):
                            last = None
                            for q2 in range(TSC // 512):
                                if i == 0:
                                    e.matmul(out=y_ps[cgi % 2][hl * 64:hl * 64 + 64, q2 * 512:(q2 + 1) * 512],
                                             lhsT=wTb[hl][:, :],
                                             rhs=qr[hl][0:64, ch * TSC + q2 * 512: ch * TSC + (q2 + 1) * 512],
                                             start=True, stop=False)
                                last = e.matmul(
                                    out=y_ps[cgi % 2][hl * 64:hl * 64 + 64, q2 * 512:(q2 + 1) * 512],
                                    lhsT=selc[hl][:, i * 64:(i + 1) * 64],
                                    rhs=z_sb[:, (g % 2) * TSC + q2 * 512:(g % 2) * TSC + (q2 + 1) * 512],
                                    start=False, stop=(i == 31))
                            return last
                        yw = [(s_z, zz.v)]
                        if i == 0 and cgi >= 2:
                            yw.append((s_ye, cgi - 1))
                        sch.add("tensor", ymf, waits=yw, incs=[(s_ym, 1)])
                        ym.inc()
                        gi.inc()
                    ew = [(s_ym, ym.v)]
                    if b > 0 or hl > 0 or ch > 0:
                        pass
                    if yst.v:
                        ew.append((s_yst, 16 * yst.v))
                    sch.add("vector",
                            (lambda e, hl=hl, ccols=ccols, cgi=cgi: e.tensor_copy(
                                out=ysb[hl * 64:hl * 64 + 64, ccols],
                                in_=y_ps[cgi % 2][hl * 64:hl * 64 + 64, :])),
                            waits=ew, incs=[(s_ye, 1)])
                    ye.inc()
                    cg.inc()
            sch.add("sync", (lambda e, b=b: e.dma_start(out=y_o[b, :, :], in_=ysb[:, :])),
                    waits=[(s_ye, ye.v)], incs=[(s_yst, 16)])
            yst.inc()

        sch.add("sync", lambda e: None, waits=[(s_yst, 16 * yst.v)])
        sch.run()
    return nc


def _prep_scan(inputs, qkv_hs):
    WL = inputs["L0_attn_W_LTM"].astype(f32)
    pv = [inputs["L0_attn_V_gs"], inputs["L0_attn_V_T0"], inputs["L0_attn_beta_tau"],
          inputs["L0_attn_beta_gm"], inputs["L0_attn_C_ch"], inputs["L0_attn_gamma"]]
    alpha = float(inputs["L0_attn_alpha_ppd"][0]); ic = float(inputs["L0_attn_IC_threshold"][0])
    vbc = np.zeros((64, 128, 128), dtype=BF)
    for hl in range(2):
        for i in range(32):
            for dlt in range(2):
                p = hl * 64 + 2 * i + dlt
                for ee in range(64):
                    vbc[hl * 32 + i, p, dlt * 64 + ee] = 1
    maps = []
    for c in range(NC):
        h0 = 2 * c
        wl = np.empty((2, 128, 32), f32)
        wt = np.empty((2, 64, 64), f32)
        for hl in range(2):
            Wh = WL[h0 + hl]
            wt[hl] = Wh.T
            for dlt in range(2):
                wl[hl, dlt * 64:(dlt + 1) * 64, :] = Wh[dlt::2, :].T
        pcol = np.zeros((128, 16), f32)
        for r in range(6):
            for hl in range(2):
                pcol[:, 2 * r + hl] = pv[r][h0 + hl]
        pcol[:, 12] = alpha; pcol[:, 13] = alpha
        pcol[:, 14] = ic; pcol[:, 15] = ic
        maps.append({
            "q_in": np.ascontiguousarray(qkv_hs[:, c]),
            "k_in": np.ascontiguousarray(qkv_hs[:, 8 + c]),
            "v_in": np.ascontiguousarray(qkv_hs[:, 16 + c]),
            "wltm_l": wl, "wltm_T": wt, "pcol": pcol, "vbc": vbc,
        })
    return maps


def run_scan(inputs, qkv_hs):
    nc = build_scan()
    maps = _prep_scan(inputs, qkv_hs)
    res = run_bass_kernel_spmd(nc, maps, core_ids=list(range(NC)))
    return np.stack([r["y_o"] for r in res.results], axis=0)


def build_postproj(tag):
    nc = bass.Bass("TRN2", target_bir_lowering=False, debug=False)
    yo = nc.dram_tensor("yo", [NKC, B, 128, SL], BF16, kind="ExternalInput").ap()
    wo = nc.dram_tensor("wo", [NKC, NKC, 128, 128], F32, kind="ExternalInput").ap()
    xr = nc.dram_tensor("xr", [B, SL, D], F32, kind="ExternalInput").ap()
    identb = nc.dram_tensor("identb", [128, 128], BF16, kind="ExternalInput").ap()
    xout_t = nc.dram_tensor("xout_t", [B, SL, D], F32, kind="ExternalOutput").ap()
    xout_fm = nc.dram_tensor("xout_fm", [B, NKC, 128, SL], BF16, kind="ExternalOutput").ap()

    NST = 4
    es = contextlib.ExitStack()
    with es:
        wstg = es.enter_context(nc.sbuf_tensor("wstg", [128, NST * 128], F32))
        wbf = es.enter_context(nc.sbuf_tensor("wbf", [128, 64 * 128], BF16))
        idtb = es.enter_context(nc.sbuf_tensor("idtb", [128, 128], BF16))
        yosb = es.enter_context(nc.sbuf_tensor("yosb", [128, NKC * SL], BF16))
        xtk = es.enter_context(nc.sbuf_tensor("xtk", [128, NTT * D], F32))
        osb = es.enter_context(nc.sbuf_tensor("osb", [128, NKC * SL], BF16))
        xres = es.enter_context(nc.sbuf_tensor("xres", [128, NTT * D], F32))
        xsq = es.enter_context(nc.sbuf_tensor("xsq", [128, D], F32))
        xnb = es.enter_context(nc.sbuf_tensor("xnb", [128, NTT * D], BF16))
        fmout = es.enter_context(nc.sbuf_tensor("fmout", [128, NKC * SL], BF16))
        scol = es.enter_context(nc.sbuf_tensor("scol", [128, 16], F32))
        o_ps = es.enter_context(nc.psum_tensor("o_ps", [128, 1024], F32))
        pstb = [es.enter_context(nc.psum_tensor(f"pstb{j}", [128, 512], BF16)) for j in range(2)]
        pstc = [es.enter_context(nc.psum_tensor(f"pstc{j}", [128, 512], BF16)) for j in range(2)]
        s_wslot = [es.enter_context(nc.semaphore(f"s_wsl{j}")) for j in range(NST)]
        s_cast = es.enter_context(nc.semaphore("s_cast"))
        s_ldy = es.enter_context(nc.semaphore("s_ldy"))
        s_ldx = es.enter_context(nc.semaphore("s_ldx"))
        s_mm = es.enter_context(nc.semaphore("s_mm"))
        s_oe = es.enter_context(nc.semaphore("s_oe"))
        s_tp = es.enter_context(nc.semaphore("s_tp"))
        s_ra = es.enter_context(nc.semaphore("s_ra"))
        s_sq = es.enter_context(nc.semaphore("s_sq"))
        s_nrm = es.enter_context(nc.semaphore("s_nrm"))
        s_tb = es.enter_context(nc.semaphore("s_tb"))
        s_fme = es.enter_context(nc.semaphore("s_fme"))
        s_st = es.enter_context(nc.semaphore("s_st"))

        sch = Sched(nc)
        sch._nopt = wstg
        NW = NKC * NKC

        sch.add("sync", lambda e: e.dma_start(out=idtb[:], in_=identb[:, :]), incs=[(s_ldx, 16)])
        for i in range(NW):
            waits = [(s_cast, i - NST + 1)] if i >= NST else []
            sch.add("sync",
                    (lambda e, i=i: e.dma_start(out=wstg[:, (i % NST) * 128:(i % NST + 1) * 128],
                                                in_=wo[i // NKC, i % NKC, :, :])),
                    waits=waits, incs=[(s_wslot[i % NST], 16)])
            sch.add("vector",
                    (lambda e, i=i: e.tensor_copy(out=wbf[:, i * 128:(i + 1) * 128],
                                                  in_=wstg[:, (i % NST) * 128:(i % NST + 1) * 128])),
                    waits=[(s_wslot[i % NST], 16 * (i // NST + 1))], incs=[(s_cast, 1)])

        ldy = Counter(); ldx = Counter(); mm = Counter(); oe = Counter()
        tp = Counter(); ra = Counter(); sq = Counter(); nrm = Counter()
        tb = Counter(); fme = Counter(); st = Counter()

        for b in range(B):
            lw = [(s_mm, mm.v)] if b else []
            def ldyf(e, b=b):
                for kc in range(NKC):
                    e.dma_start(out=yosb[:, kc * SL:(kc + 1) * SL],
                                in_=yo[kc, b, :, :]).then_inc(s_ldy, 16)
                return None
            sch.add("sync", ldyf, waits=lw)
            ldy.inc(NKC)
            lw2 = [(s_ra, ra.v)] if b else []
            def ldxf(e, b=b):
                for t in range(NTT):
                    e.dma_start(out=xtk[:, t * D:(t + 1) * D],
                                in_=xr[b, t * 128:(t + 1) * 128, :]).then_inc(s_ldx, 16)
                return None
            sch.add("sync", ldxf, waits=lw2)
            ldx.inc(NTT)

            for mc in range(NKC):
                for blk in range(2):
                    m = (b * NKC + mc) * 2 + blk
                    w = [(s_ldy, 16 * ldy.v), (s_cast, NW)]
                    if m >= 2:
                        w.append((s_oe, m - 1))
                    def mmf(e, mc=mc, blk=blk, m=m):
                        last = None
                        for kc in range(NKC):
                            last = e.matmul(
                                out=o_ps[:, (m % 2) * 512:(m % 2) * 512 + 512],
                                lhsT=wbf[:, (kc * NKC + mc) * 128:(kc * NKC + mc + 1) * 128],
                                rhs=yosb[:, kc * SL + blk * 512: kc * SL + blk * 512 + 512],
                                start=(kc == 0), stop=(kc == NKC - 1))
                        return last
                    sch.add("tensor", mmf, waits=w, incs=[(s_mm, 1)])
                    mm.inc()
                    sch.add("vector",
                            (lambda e, mc=mc, blk=blk, m=m: e.tensor_copy(
                                out=osb[:, mc * SL + blk * 512: mc * SL + blk * 512 + 512],
                                in_=o_ps[:, (m % 2) * 512:(m % 2) * 512 + 512])),
                            waits=[(s_mm, mm.v)], incs=[(s_oe, 1)])
                    oe.inc()

            for t in range(NTT if PP_STOP != 'mm' else 0):
                for mc in range(NKC):
                    j = (b * NTT + t) * NKC + mc
                    w = [(s_oe, oe.v), (s_ldx, 16 * ldx.v), (s_ldx, 16)]
                    if j >= 2:
                        w.append((s_ra, j - 1))
                    sch.add("tensor",
                            (lambda e, t=t, mc=mc, j=j: e.transpose(
                                out=pstb[j % 2][:, 0:128],
                                in_=osb[:, mc * SL + t * 128: mc * SL + (t + 1) * 128],
                                identity=idtb[:])),
                            waits=w, incs=[(s_tp, 1)])
                    tp.inc()
                    sch.add("vector",
                            (lambda e, t=t, mc=mc, j=j: e.scalar_tensor_tensor(
                                out=xres[:, t * D + mc * 128: t * D + (mc + 1) * 128],
                                in0=xtk[:, t * D + mc * 128: t * D + (mc + 1) * 128],
                                scalar=1.0, in1=pstb[j % 2][:, 0:128],
                                op0=ALU.mult, op1=ALU.add)),
                            waits=[(s_tp, tp.v)], incs=[(s_ra, 1)])
                    ra.inc()
                if PP_STOP == 'ra':
                    continue
                def L1(e, t=t):
                    return e.reduce_sum(out=scol[:, 0:1], in_=xres[:, t * D:(t + 1) * D],
                                        axis=mybir.AxisListType.X)
                sch.add("vector", L1, waits=[(s_ra, ra.v)], incs=[(s_nrm, 1)])
                nrm.inc()
                def L2(e, t=t):
                    e.tensor_mul(out=xsq[:, :], in0=xres[:, t * D:(t + 1) * D],
                                 in1=xres[:, t * D:(t + 1) * D])
                    return e.reduce_sum(out=scol[:, 1:2], in_=xsq[:, :],
                                        axis=mybir.AxisListType.X)
                sch.add("vector", L2, waits=[(s_nrm, nrm.v)], incs=[(s_nrm, 1)])
                nrm.inc()
                sch.add("vector", (lambda e: e.tensor_scalar_mul(
                    out=scol[:, 2:3], in0=scol[:, 0:1], scalar1=1.0 / D)),
                    waits=[(s_nrm, nrm.v)], incs=[(s_nrm, 1)])
                nrm.inc()
                sch.add("vector", (lambda e: e.tensor_scalar_mul(
                    out=scol[:, 3:4], in0=scol[:, 1:2], scalar1=1.0 / D)),
                    waits=[(s_nrm, nrm.v)], incs=[(s_nrm, 1)])
                nrm.inc()
                sch.add("vector", (lambda e: e.tensor_mul(
                    out=scol[:, 4:5], in0=scol[:, 2:3], in1=scol[:, 2:3])),
                    waits=[(s_nrm, nrm.v)], incs=[(s_nrm, 1)])
                nrm.inc()
                sch.add("vector", (lambda e: e.tensor_sub(
                    out=scol[:, 5:6], in0=scol[:, 3:4], in1=scol[:, 4:5])),
                    waits=[(s_nrm, nrm.v)], incs=[(s_nrm, 1)])
                nrm.inc()
                sch.add("scalar", (lambda e: e.activation(
                    out=scol[:, 6:7], in_=scol[:, 5:6], func=AF.Ln, bias=np.float32(LN_EPS))),
                    waits=[(s_nrm, nrm.v)], incs=[(s_nrm, 1)])
                nrm.inc()
                sch.add("scalar", (lambda e: e.activation(
                    out=scol[:, 7:8], in_=scol[:, 6:7], func=AF.Exp, scale=np.float32(-0.5))),
                    waits=[(s_nrm, nrm.v)], incs=[(s_nrm, 1)])
                nrm.inc()
                def nzf(e, t=t):
                    e.tensor_scalar(out=xres[:, t * D:(t + 1) * D], in0=xres[:, t * D:(t + 1) * D],
                                    scalar1=scol[:, 2:3], scalar2=scol[:, 7:8],
                                    op0=ALU.subtract, op1=ALU.mult)
                    return e.tensor_copy(out=xnb[:, t * D:(t + 1) * D], in_=xres[:, t * D:(t + 1) * D])
                sch.add("vector", nzf, waits=[(s_nrm, nrm.v)], incs=[(s_nrm, 1)])
                nrm.inc()
            if PP_STOP not in ('mm',):
                sch.add("sync",
                        (lambda e, b=b: e.dma_start(
                            out=xout_t[b].rearrange("(t p) d -> p t d", p=128),
                            in_=xres[:].rearrange("p (t d) -> p t d", d=D))),
                        waits=[(s_nrm, nrm.v), (s_ra, ra.v)], incs=[(s_st, 16)])
                st.inc()
            for t in range(NTT if PP_STOP == 'full' else 0):
                for mc in range(NKC):
                    j2 = (b * NTT + t) * NKC + mc
                    w = [(s_nrm, nrm.v), (s_ldx, 16)]
                    if j2 >= 2:
                        w.append((s_fme, j2 - 1))
                    sch.add("tensor",
                            (lambda e, t=t, mc=mc, j2=j2: e.transpose(
                                out=pstc[j2 % 2][:, 0:128],
                                in_=xnb[:, t * D + mc * 128: t * D + (mc + 1) * 128],
                                identity=idtb[:])),
                            waits=w, incs=[(s_tb, 1)])
                    tb.inc()
                    sch.add("vector",
                            (lambda e, t=t, mc=mc, j2=j2: e.tensor_copy(
                                out=fmout[:, mc * SL + t * 128: mc * SL + (t + 1) * 128],
                                in_=pstc[j2 % 2][:, 0:128])),
                            waits=[(s_tb, tb.v)], incs=[(s_fme, 1)])
                    fme.inc()
            def stf2(e, b=b):
                src = fmout if PP_STOP == 'full' else osb
                for kc in range(NKC):
                    e.dma_start(out=xout_fm[b, kc, :, :],
                                in_=src[:, kc * SL:(kc + 1) * SL]).then_inc(s_st, 16)
                return None
            sch.add("sync", stf2, waits=([(s_fme, fme.v)] if PP_STOP == 'full' else [(s_oe, oe.v)]))
            st.inc(NKC)
            sch.add("vector", lambda e: None, waits=[(s_st, 16 * st.v)])

        sch.add("sync", lambda e: None, waits=[(s_st, 16 * st.v)])
        sch.run()
    return nc


def _ident_consts():
    return {
        "identf": np.eye(128, dtype=f32),
        "identb": np.eye(128, dtype=BF),
        "onesc": np.ones((128, 1), f32),
        "ones1": np.ones((1, 128), f32),
    }


def run_postproj(tag, w, y_hs, xr_tok, transpose_w):
    key = "pp_" + tag
    if key not in _cache:
        _cache[key] = build_postproj(tag)
    nc = _cache[key]
    wch = _wchunks_T(w) if transpose_w else _wchunks(w)
    ident = np.eye(128, dtype=BF)
    maps = []
    for c in range(NC):
        sl = slice(c * SL, (c + 1) * SL)
        maps.append({"yo": np.ascontiguousarray(y_hs[:, :, :, sl]), "wo": wch,
                     "identb": ident, "xr": np.ascontiguousarray(xr_tok[:, sl, :])})
    res = run_bass_kernel_spmd(nc, maps, core_ids=list(range(NC)))
    xt = np.concatenate([r["xout_t"] for r in res.results], axis=1)
    xfm = np.concatenate([r["xout_fm"] for r in res.results], axis=3)
    return xt, xfm


EC = 2048
FF = 4096


def build_moe():
    nc = bass.Bass("TRN2", target_bir_lowering=False, debug=False)
    xall = nc.dram_tensor("xall", [S, D], F32, kind="ExternalInput").ap()
    idxc = nc.dram_tensor("idxc", [128, 16], I32, kind="ExternalInput").ap()
    w1 = nc.dram_tensor("w1", [NKC, 32, 128, 128], F32, kind="ExternalInput").ap()
    w2 = nc.dram_tensor("w2", [32, NKC, 128, 128], F32, kind="ExternalInput").ap()
    b1 = nc.dram_tensor("b1", [128, 32], F32, kind="ExternalInput").ap()
    b2 = nc.dram_tensor("b2", [128, 8], F32, kind="ExternalInput").ap()
    cwe = nc.dram_tensor("cwe", [S, 1], F32, kind="ExternalInput").ap()
    identb = nc.dram_tensor("identb", [128, 128], BF16, kind="ExternalInput").ap()
    hbounce = nc.dram_tensor("hbounce", [32, 128, EC], BF16, kind="Internal")
    moe_o = nc.dram_tensor("moe_o", [EC, D], BF16, kind="ExternalOutput").ap()
    hb_ap = hbounce.ap()

    NST = 4
    es = contextlib.ExitStack()
    with es:
        wstg = es.enter_context(nc.sbuf_tensor("wstg", [128, NST * 128], F32))
        w1b = es.enter_context(nc.sbuf_tensor("w1b", [128, 256 * 128], BF16))
        idxt = es.enter_context(nc.sbuf_tensor("idxt", [128, 16], I32))
        idt = es.enter_context(nc.sbuf_tensor("idt", [128, 128], BF16))
        b1t = es.enter_context(nc.sbuf_tensor("b1t", [128, 32], F32))
        b2t = es.enter_context(nc.sbuf_tensor("b2t", [128, 8], F32))
        wct = es.enter_context(nc.sbuf_tensor("wct", [128, 16], F32))
        xg32 = es.enter_context(nc.sbuf_tensor("xg32", [128, 2 * D], F32))
        xgb = es.enter_context(nc.sbuf_tensor("xgb", [128, 2 * D], BF16))
        xgfm = es.enter_context(nc.sbuf_tensor("xgfm", [128, NKC * EC], BF16))
        hev = es.enter_context(nc.sbuf_tensor("hev", [128, 6 * 512], F32))
        hbt = es.enter_context(nc.sbuf_tensor("hbt", [128, EC], BF16))
        h2 = es.enter_context(nc.sbuf_tensor("h2", [128, 32 * 512], BF16))
        osb = es.enter_context(nc.sbuf_tensor("osb", [128, NKC * 512], BF16))
        otm = es.enter_context(nc.sbuf_tensor("otm", [128, 2 * D], BF16))
        ps_m = es.enter_context(nc.psum_tensor("ps_m", [128, 1024], F32))
        ps_t0 = es.enter_context(nc.psum_tensor("ps_t0", [128, 512], BF16))
        ps_t1 = es.enter_context(nc.psum_tensor("ps_t1", [128, 512], BF16))
        s_wslot = [es.enter_context(nc.semaphore(f"s_wsl{j}")) for j in range(NST)]
        s_cast = es.enter_context(nc.semaphore("s_cast"))
        s_ld = es.enter_context(nc.semaphore("s_ld"))
        s_gat = es.enter_context(nc.semaphore("s_gat"))
        s_gc = es.enter_context(nc.semaphore("s_gc"))
        s_gtp = es.enter_context(nc.semaphore("s_gtp"))
        s_gte = es.enter_context(nc.semaphore("s_gte"))
        s_mm = es.enter_context(nc.semaphore("s_mm"))
        s_ge = es.enter_context(nc.semaphore("s_ge"))
        s_hst = es.enter_context(nc.semaphore("s_hst"))
        s_h2 = es.enter_context(nc.semaphore("s_h2"))
        s_mm2 = es.enter_context(nc.semaphore("s_mm2"))
        s_oe = es.enter_context(nc.semaphore("s_oe"))
        s_otp = es.enter_context(nc.semaphore("s_otp"))
        s_ote = es.enter_context(nc.semaphore("s_ote"))
        s_st = es.enter_context(nc.semaphore("s_st"))

        sch = Sched(nc)
        sch._nopt = wstg

        sch.add("sync", lambda e: e.dma_start(out=idt[:], in_=identb[:, :]), incs=[(s_ld, 16)])
        sch.add("sync", lambda e: e.dma_start(out=idxt[:], in_=idxc[:, :]), incs=[(s_ld, 16)])
        sch.add("sync", lambda e: e.dma_start(out=b1t[:], in_=b1[:, :]), incs=[(s_ld, 16)])
        sch.add("sync", lambda e: e.dma_start(out=b2t[:], in_=b2[:, :]), incs=[(s_ld, 16)])
        NW1 = 256
        for i in range(NW1):
            waits = [(s_cast, i - NST + 1)] if i >= NST else []
            sch.add("sync",
                    (lambda e, i=i: e.dma_start(out=wstg[:, (i % NST) * 128:(i % NST + 1) * 128],
                                                in_=w1[i // 32, i % 32, :, :])),
                    waits=waits, incs=[(s_wslot[i % NST], 16)])
            sch.add("vector",
                    (lambda e, i=i: e.tensor_copy(out=w1b[:, i * 128:(i + 1) * 128],
                                                  in_=wstg[:, (i % NST) * 128:(i % NST + 1) * 128])),
                    waits=[(s_wslot[i % NST], 16 * (i // NST + 1))], incs=[(s_cast, 1)])

        gat = Counter(); gc = Counter(); gtp = Counter(); gte = Counter()
        mm = Counter(); ge = Counter(); hst = Counter(); h2c = Counter()
        mm2 = Counter(); oe = Counter(); otp = Counter(); ote = Counter(); st = Counter()

        for g in range(16):
            w = [(s_ld, 64)]
            if g >= 2:
                w.append((s_gc, 2 * (g - 1)))
            def gf(e, g=g):
                e.indirect_dma_start(
                    out=xg32[:, (g % 2) * D:(g % 2 + 1) * D], out_offset=None,
                    in_=xall[0:128, :], in_offset=IndirectOffsetOnAxis(idxt[:, g:g + 1], 0)
                ).then_inc(s_gat, 16)
                e.indirect_dma_start(
                    out=wct[:, g:g + 1], out_offset=None,
                    in_=cwe[0:128, :], in_offset=IndirectOffsetOnAxis(idxt[:, g:g + 1], 0)
                ).then_inc(s_gat, 16)
                return None
            sch.add("gpsimd", gf, waits=w)
            gat.inc()
            sch.add("vector",
                    (lambda e, g=g: e.tensor_copy(out=xgb[:, (g % 2) * D:(g % 2 + 1) * D],
                                                  in_=xg32[:, (g % 2) * D:(g % 2 + 1) * D])),
                    waits=[(s_gat, 32 * gat.v)], incs=[(s_gc, 1)])
            gc.inc()
            for kc in range(NKC):
                j = g * NKC + kc
                tw = [(s_gc, gc.v)]
                if j >= 2:
                    tw.append((s_gte, j - 1))
                sch.add("tensor",
                        (lambda e, g=g, kc=kc, j=j: e.transpose(
                            out=(ps_t0 if j % 2 == 0 else ps_t1)[:, 0:128],
                            in_=xgb[:, (g % 2) * D + kc * 128:(g % 2) * D + (kc + 1) * 128],
                            identity=idt[:])),
                        waits=tw, incs=[(s_gtp, 1)])
                gtp.inc()
                sch.add("vector",
                        (lambda e, g=g, kc=kc, j=j: e.tensor_copy(
                            out=xgfm[:, kc * EC + g * 128: kc * EC + (g + 1) * 128],
                            in_=(ps_t0 if j % 2 == 0 else ps_t1)[:, 0:128])),
                        waits=[(s_gtp, gtp.v)], incs=[(s_gte, 1)])
                gte.inc()
            sch.add("vector", (lambda e: None), waits=[(s_gte, gte.v)])
            sch.add("vector", (lambda e: e.memset(hev[0:1, 0:1], 0)), incs=[(s_gc, 1)])
            gc.inc()

        for mc in range(32):
            for blk in range(EC // 512):
                m = mc * 4 + blk
                w = [(s_gte, gte.v), (s_cast, NW1)]
                if m >= 1:
                    w.append((s_ge, 6 * m))
                def mmf(e, mc=mc, blk=blk, m=m):
                    last = None
                    for kc in range(NKC):
                        last = e.matmul(
                            out=ps_m[:, (m % 2) * 512:(m % 2) * 512 + 512],
                            lhsT=w1b[:, (kc * 32 + mc) * 128:(kc * 32 + mc + 1) * 128],
                            rhs=xgfm[:, kc * EC + blk * 512: kc * EC + (blk + 1) * 512],
                            start=(kc == 0), stop=(kc == NKC - 1))
                    return last
                sch.add("tensor", mmf, waits=w, incs=[(s_mm, 1)])
                mm.inc()
                def g1(e, mc=mc, m=m):
                    return e.tensor_scalar_add(out=hev[:, 0:512], in0=ps_m[:, (m % 2) * 512:(m % 2) * 512 + 512],
                                               scalar1=b1t[:, mc:mc + 1])
                sch.add("vector", g1, waits=[(s_mm, mm.v)], incs=[(s_ge, 1)])
                def g2(e):
                    return e.tensor_mul(out=hev[:, 512:1024], in0=hev[:, 0:512], in1=hev[:, 0:512])
                sch.add("vector", g2, waits=[(s_ge, ge.v + 1)], incs=[(s_ge, 1)])
                def g3(e):
                    return e.tensor_scalar(out=hev[:, 1024:1536], in0=hev[:, 512:1024],
                                           scalar1=0.044715, scalar2=1.0, op0=ALU.mult, op1=ALU.add)
                sch.add("vector", g3, waits=[(s_ge, ge.v + 2)], incs=[(s_ge, 1)])
                def g4(e):
                    return e.tensor_mul(out=hev[:, 1536:2048], in0=hev[:, 1024:1536], in1=hev[:, 0:512])
                sch.add("vector", g4, waits=[(s_ge, ge.v + 3)], incs=[(s_ge, 1)])
                def g5(e):
                    return e.activation(out=hev[:, 2048:2560], in_=hev[:, 1536:2048],
                                        func=AF.Tanh, scale=np.float32(0.7978845608028654))
                sch.add("scalar", g5, waits=[(s_ge, ge.v + 4)], incs=[(s_ge, 1)])
                def g6(e, blk=blk):
                    e.tensor_scalar(out=hev[:, 2560:3072], in0=hev[:, 2048:2560],
                                    scalar1=1.0, scalar2=0.5, op0=ALU.add, op1=ALU.mult)
                    return e.tensor_mul(out=hbt[:, blk * 512:(blk + 1) * 512],
                                        in0=hev[:, 2560:3072], in1=hev[:, 0:512])
                sch.add("vector", g6, waits=[(s_ge, ge.v + 5)], incs=[(s_ge, 1)])
                ge.inc(6)
            sch.add("sync",
                    (lambda e, mc=mc: e.dma_start(out=hb_ap[mc, :, :], in_=hbt[:, :])),
                    waits=[(s_ge, ge.v)], incs=[(s_hst, 16)])
            hst.inc()
            sch.add("vector", (lambda e: None), waits=[(s_hst, 16 * hst.v)])

        NW2 = 256
        for i in range(NW2):
            waits = [(s_mm, mm.v)] if i < NST else [(s_cast, NW1 + i - NST + 1)]
            sch.add("sync",
                    (lambda e, i=i: e.dma_start(out=wstg[:, (i % NST) * 128:(i % NST + 1) * 128],
                                                in_=w2[i // NKC, i % NKC, :, :])),
                    waits=waits, incs=[(s_wslot[i % NST], 16 * 1)])
            sch.add("vector",
                    (lambda e, i=i: e.tensor_copy(out=w1b[:, i * 128:(i + 1) * 128],
                                                  in_=wstg[:, (i % NST) * 128:(i % NST + 1) * 128])),
                    waits=[(s_wslot[i % NST], 16 * ((NW1 + i) // NST + 1))], incs=[(s_cast, 1)])

        for blk in range(EC // 512):
            def ldh(e, blk=blk):
                for kc2 in range(32):
                    e.dma_start(out=h2[:, kc2 * 512:(kc2 + 1) * 512],
                                in_=hb_ap[kc2, :, blk * 512:(blk + 1) * 512]).then_inc(s_h2, 16)
                return None
            sch.add("sync", ldh, waits=[(s_hst, 16 * hst.v), (s_mm2, mm2.v)])
            h2c.inc(32)
            for mc2 in range(NKC):
                m = blk * NKC + mc2
                w = [(s_h2, 16 * h2c.v), (s_cast, NW1 + NW2)]
                if m >= 2:
                    w.append((s_oe, m - 1))
                def mmf2(e, mc2=mc2, m=m):
                    last = None
                    for kc2 in range(32):
                        last = e.matmul(
                            out=ps_m[:, (m % 2) * 512:(m % 2) * 512 + 512],
                            lhsT=w1b[:, (kc2 * NKC + mc2) * 128:(kc2 * NKC + mc2 + 1) * 128],
                            rhs=h2[:, kc2 * 512:(kc2 + 1) * 512],
                            start=(kc2 == 0), stop=(kc2 == 31))
                    return last
                sch.add("tensor", mmf2, waits=w, incs=[(s_mm2, 1)])
                mm2.inc()
                def oef(e, mc2=mc2, m=m):
                    return e.tensor_scalar_add(out=osb[:, mc2 * 512:(mc2 + 1) * 512],
                                               in0=ps_m[:, (m % 2) * 512:(m % 2) * 512 + 512],
                                               scalar1=b2t[:, mc2:mc2 + 1])
                ow = [(s_mm2, mm2.v)]
                if blk >= 1:
                    ow.append((s_otp, 32 * blk))
                sch.add("vector", oef, waits=ow, incs=[(s_oe, 1)])
                oe.inc()
            for tg in range(4):
                for mc2 in range(NKC):
                    j = (blk * 4 + tg) * NKC + mc2
                    w = [(s_oe, oe.v), (s_ld, 64), (s_gte, 128), (s_gat, 16 * 32)]
                    if j >= 2:
                        w.append((s_ote, j - 1))
                    sch.add("tensor",
                            (lambda e, tg=tg, mc2=mc2, j=j: e.transpose(
                                out=(ps_t0 if j % 2 == 0 else ps_t1)[:, 0:128],
                                in_=osb[:, mc2 * 512 + tg * 128: mc2 * 512 + (tg + 1) * 128],
                                identity=idt[:])),
                            waits=w, incs=[(s_otp, 1)])
                    otp.inc()
                    ewx = [(s_otp, otp.v)]
                    if (blk * 4 + tg) >= 2:
                        ewx.append((s_st, 16 * (blk * 4 + tg - 1)))
                    sch.add("vector",
                            (lambda e, tg=tg, mc2=mc2, j=j, blk=blk: e.tensor_scalar_mul(
                                out=otm[:, (j % 16 // 8) * D + mc2 * 128:(j % 16 // 8) * D + (mc2 + 1) * 128],
                                in0=(ps_t0 if j % 2 == 0 else ps_t1)[:, 0:128],
                                scalar1=wct[:, blk * 4 + tg: blk * 4 + tg + 1])),
                            waits=ewx, incs=[(s_ote, 1)])
                    ote.inc()
                def stf(e, blk=blk, tg=tg):
                    j0 = (blk * 4 + tg) * NKC
                    half = (j0 % 16) // 8
                    return e.dma_start(out=moe_o[(blk * 4 + tg) * 128:(blk * 4 + tg + 1) * 128, :],
                                       in_=otm[:, half * D:(half + 1) * D])
                sch.add("sync", stf, waits=[(s_ote, ote.v)], incs=[(s_st, 16)])
                st.inc()

        sch.add("sync", lambda e: None, waits=[(s_st, 16 * st.v)])
        sch.run()
    return nc


def run_moe(inputs, x1_tok):
    if "moe" not in _cache:
        _cache["moe"] = build_moe()
    nc = _cache["moe"]
    W1 = inputs["L0_ffn_W1"].astype(f32)
    W2 = inputs["L0_ffn_W2"].astype(f32)
    b1 = inputs["L0_ffn_b1"].astype(f32)
    b2 = inputs["L0_ffn_b2"].astype(f32)
    cw = inputs["L0_ffn_combine_weights"].astype(f32)
    ti = inputs["L0_ffn_top_idx"].astype(np.int32)
    ident = np.eye(128, dtype=BF)
    xall = np.ascontiguousarray(x1_tok[0])
    maps = []
    for c in range(NC):
        maps.append({
            "xall": xall,
            "idxc": np.ascontiguousarray(ti[:, c].reshape(16, 128).T),
            "w1": _wchunks(W1[c]), "w2": _wchunks(W2[c]),
            "b1": np.ascontiguousarray(b1[c].reshape(32, 128).T),
            "b2": np.ascontiguousarray(b2[c].reshape(8, 128).T),
            "cwe": np.ascontiguousarray(cw[:, c:c + 1]),
            "identb": ident,
        })
    res = run_bass_kernel_spmd(nc, maps, core_ids=list(range(NC)))
    return np.stack([r["moe_o"] for r in res.results], axis=0)


def build_combine():
    nc = bass.Bass("TRN2", target_bir_lowering=False, debug=False)
    moe = nc.dram_tensor("moe", [8, EC, D], BF16, kind="ExternalInput").ap()
    tokq = nc.dram_tensor("tokq", [128, 128], I32, kind="ExternalInput").ap()
    xr = nc.dram_tensor("xr", [B, SL, D], F32, kind="ExternalInput").ap()
    iotac = nc.dram_tensor("iotac", [128, SL], F32, kind="ExternalInput").ap()
    xout_t = nc.dram_tensor("xout_t", [B, SL, D], F32, kind="ExternalOutput").ap()

    es = contextlib.ExitStack()
    with es:
        toki = es.enter_context(nc.sbuf_tensor("toki", [128, 128], I32))
        tokf = es.enter_context(nc.sbuf_tensor("tokf", [128, 128], F32))
        iot = es.enter_context(nc.sbuf_tensor("iot", [128, SL], F32))
        mtile = [es.enter_context(nc.sbuf_tensor(f"mt{j}", [128, SL], BF16)) for j in range(2)]
        moet = [es.enter_context(nc.sbuf_tensor(f"moet{j}", [128, D], BF16)) for j in range(2)]
        xtk = es.enter_context(nc.sbuf_tensor("xtk", [128, NTT * D], F32))
        xres = es.enter_context(nc.sbuf_tensor("xres", [128, NTT * D], F32))
        xsq = es.enter_context(nc.sbuf_tensor("xsq", [128, D], F32))
        scol = es.enter_context(nc.sbuf_tensor("scol", [128, 16], F32))
        pg = [es.enter_context(nc.psum_tensor(f"pg{j}", [128, 1024], F32)) for j in range(4)]
        s_ld = es.enter_context(nc.semaphore("s_ld"))
        s_ldx = es.enter_context(nc.semaphore("s_ldx"))
        s_mload = es.enter_context(nc.semaphore("s_mload"))
        s_mb = es.enter_context(nc.semaphore("s_mb"))
        s_mm = es.enter_context(nc.semaphore("s_mm"))
        s_ra = es.enter_context(nc.semaphore("s_ra"))
        s_nrm = es.enter_context(nc.semaphore("s_nrm"))
        s_st = es.enter_context(nc.semaphore("s_st"))

        sch = Sched(nc)
        sch._nopt = tokf

        sch.add("sync", lambda e: e.dma_start(out=toki[:], in_=tokq[:, :]), incs=[(s_ld, 16)])
        sch.add("sync", lambda e: e.dma_start(out=iot[:], in_=iotac[:, :]), incs=[(s_ld, 16)])
        sch.add("vector", (lambda e: e.tensor_copy(out=tokf[:], in_=toki[:])),
                waits=[(s_ld, 32)], incs=[(s_mb, 1)])
        mbv = Counter(); mbv.inc()

        ldx = Counter(); mload = Counter(); mm = Counter(); ra = Counter()
        nrm = Counter(); st = Counter()

        def ln_chain(tile_getter, store_fn, extra_w):
            pass

        for b in range(B):
            def ldxf(e, b=b):
                for t in range(NTT):
                    e.dma_start(out=xtk[:, t * D:(t + 1) * D],
                                in_=xr[b, t * 128:(t + 1) * 128, :]).then_inc(s_ldx, 16)
                return None
            sch.add("sync", ldxf, waits=([(s_nrm, nrm.v)] if b else []))
            ldx.inc(NTT)

            if b == 0:
                for grp in range(2):
                    for cc in range(128):
                        ci = grp * 128 + cc
                        w = []
                        if ci >= 2:
                            w.append((s_mm, 8 * (ci - 1)))
                        sch.add("sync",
                                (lambda e, cc=cc, ci=ci: e.dma_start(
                                    out=moet[ci % 2][:], in_=moe[cc // 16, (cc % 16) * 128:(cc % 16 + 1) * 128, :])),
                                waits=w, incs=[(s_mload, 16)])
                        mload.inc()
                        mw = [(s_ld, 32)]
                        if ci >= 2:
                            mw.append((s_mm, 8 * (ci - 1)))
                        sch.add("vector",
                                (lambda e, cc=cc, ci=ci: e.tensor_single_scalar(
                                    out=mtile[ci % 2][:], in_=iot[:],
                                    scalar=tokf[:, cc:cc + 1], op=ALU.is_equal)),
                                waits=mw, incs=[(s_mb, 1)])
                        mbv.inc()
                        def mmf(e, grp=grp, cc=cc, ci=ci):
                            last = None
                            for ntl in range(4):
                                nt = grp * 4 + ntl
                                for fh in range(2):
                                    last = e.matmul(
                                        out=pg[ntl][:, fh * 512:(fh + 1) * 512],
                                        lhsT=mtile[ci % 2][:, nt * 128:(nt + 1) * 128],
                                        rhs=moet[ci % 2][:, fh * 512:(fh + 1) * 512],
                                        start=(cc == 0), stop=(cc == 127))
                            return last
                        mmw = [(s_mload, 16 * mload.v), (s_mb, mbv.v)]
                        if grp == 1 and cc == 0:
                            mmw.append((s_ra, 4))
                        sch.add("tensor", mmf, waits=mmw, incs=[(s_mm, 8)])
                        mm.inc(8)
                    for ntl in range(4):
                        t = grp * 4 + ntl
                        sch.add("vector",
                                (lambda e, t=t, ntl=ntl: e.tensor_add(
                                    out=xres[:, t * D:(t + 1) * D],
                                    in0=xtk[:, t * D:(t + 1) * D], in1=pg[ntl][:])),
                                waits=[(s_mm, mm.v), (s_ldx, 16 * ldx.v)], incs=[(s_ra, 1)])
                        ra.inc()
                        _emit_ln(sch, nc, xres, xsq, scol, t, s_ra, ra.v, s_nrm, nrm, AF, ALU)
                        def stf(e, b=b, t=t):
                            return e.dma_start(
                                out=xout_t[b, t * 128:(t + 1) * 128, :],
                                in_=xres[:, t * D:(t + 1) * D])
                        sch.add("sync", stf, waits=[(s_nrm, nrm.v)], incs=[(s_st, 16)])
                        st.inc()
            else:
                for t in range(NTT):
                    sch.add("vector",
                            (lambda e, t=t: e.tensor_copy(
                                out=xres[:, t * D:(t + 1) * D], in_=xtk[:, t * D:(t + 1) * D])),
                            waits=[(s_ldx, 16 * ldx.v), (s_st, 16 * st.v)], incs=[(s_ra, 1)])
                    ra.inc()
                    _emit_ln(sch, nc, xres, xsq, scol, t, s_ra, ra.v, s_nrm, nrm, AF, ALU)
                    def stf(e, b=b, t=t):
                        return e.dma_start(
                            out=xout_t[b, t * 128:(t + 1) * 128, :],
                            in_=xres[:, t * D:(t + 1) * D])
                    sch.add("sync", stf, waits=[(s_nrm, nrm.v)], incs=[(s_st, 16)])
                    st.inc()

        sch.add("sync", lambda e: None, waits=[(s_st, 16 * st.v)])
        sch.run()
    return nc


def _emit_ln(sch, nc, xres, xsq, scol, t, s_prev, prev_v, s_nrm, nrm, AF, ALU):
    sch.add("vector", (lambda e, t=t: e.reduce_sum(
        out=scol[:, 0:1], in_=xres[:, t * D:(t + 1) * D], axis=mybir.AxisListType.X)),
        waits=[(s_prev, prev_v)], incs=[(s_nrm, 1)])
    nrm.inc()
    def L2(e, t=t):
        e.tensor_mul(out=xsq[:, :], in0=xres[:, t * D:(t + 1) * D],
                     in1=xres[:, t * D:(t + 1) * D])
        return e.reduce_sum(out=scol[:, 1:2], in_=xsq[:, :], axis=mybir.AxisListType.X)
    sch.add("vector", L2, waits=[(s_nrm, nrm.v)], incs=[(s_nrm, 1)])
    nrm.inc()
    sch.add("vector", (lambda e: e.tensor_scalar_mul(
        out=scol[:, 2:3], in0=scol[:, 0:1], scalar1=1.0 / D)),
        waits=[(s_nrm, nrm.v)], incs=[(s_nrm, 1)])
    nrm.inc()
    sch.add("vector", (lambda e: e.tensor_scalar_mul(
        out=scol[:, 3:4], in0=scol[:, 1:2], scalar1=1.0 / D)),
        waits=[(s_nrm, nrm.v)], incs=[(s_nrm, 1)])
    nrm.inc()
    sch.add("vector", (lambda e: e.tensor_mul(
        out=scol[:, 4:5], in0=scol[:, 2:3], in1=scol[:, 2:3])),
        waits=[(s_nrm, nrm.v)], incs=[(s_nrm, 1)])
    nrm.inc()
    sch.add("vector", (lambda e: e.tensor_sub(
        out=scol[:, 5:6], in0=scol[:, 3:4], in1=scol[:, 4:5])),
        waits=[(s_nrm, nrm.v)], incs=[(s_nrm, 1)])
    nrm.inc()
    sch.add("scalar", (lambda e: e.activation(
        out=scol[:, 6:7], in_=scol[:, 5:6], func=AF.Ln, bias=np.float32(LN_EPS))),
        waits=[(s_nrm, nrm.v)], incs=[(s_nrm, 1)])
    nrm.inc()
    sch.add("scalar", (lambda e: e.activation(
        out=scol[:, 7:8], in_=scol[:, 6:7], func=AF.Exp, scale=np.float32(-0.5))),
        waits=[(s_nrm, nrm.v)], incs=[(s_nrm, 1)])
    nrm.inc()
    sch.add("vector", (lambda e, t=t: e.tensor_scalar(
        out=xres[:, t * D:(t + 1) * D], in0=xres[:, t * D:(t + 1) * D],
        scalar1=scol[:, 2:3], scalar2=scol[:, 7:8],
        op0=ALU.subtract, op1=ALU.mult)),
        waits=[(s_nrm, nrm.v)], incs=[(s_nrm, 1)])
    nrm.inc()


def run_combine(inputs, moe_all, x1_tok):
    if "combine" not in _cache:
        _cache["combine"] = build_combine()
    nc = _cache["combine"]
    ti = inputs["L0_ffn_top_idx"].astype(np.int32)
    flat = np.ascontiguousarray(ti.T.reshape(-1))
    tokq = np.ascontiguousarray(flat.reshape(128, 128).T)
    maps = []
    for c in range(NC):
        sl = slice(c * SL, (c + 1) * SL)
        iota = np.broadcast_to(np.arange(c * SL, (c + 1) * SL, dtype=f32)[None, :],
                               (128, SL)).copy()
        maps.append({"moe": moe_all, "tokq": tokq,
                     "xr": np.ascontiguousarray(x1_tok[:, sl, :]), "iotac": iota})
    res = run_bass_kernel_spmd(nc, maps, core_ids=list(range(NC)))
    return np.concatenate([r["xout_t"] for r in res.results], axis=1)


def build_qkv1():
    nc = bass.Bass("TRN2", target_bir_lowering=False, debug=False)
    xr = nc.dram_tensor("xr", [B, SL, D], F32, kind="ExternalInput").ap()
    wqkv = nc.dram_tensor("wqkv", [3, NKC, NKC, 128, 128], F32, kind="ExternalInput").ap()
    identb = nc.dram_tensor("identb", [128, 128], BF16, kind="ExternalInput").ap()
    qk_o = nc.dram_tensor("qk_o", [2, B, NKC, 128, SL], BF16, kind="ExternalOutput").ap()
    v_o = nc.dram_tensor("v_o", [B, SL, D], BF16, kind="ExternalOutput").ap()

    NST = 4
    NW = 3 * NKC * NKC
    es = contextlib.ExitStack()
    with es:
        wstg = es.enter_context(nc.sbuf_tensor("wstg", [128, NST * 128], F32))
        wb = es.enter_context(nc.sbuf_tensor("wb", [128, NW * 128], BF16))
        idt = es.enter_context(nc.sbuf_tensor("idt", [128, 128], BF16))
        xtk = es.enter_context(nc.sbuf_tensor("xtk", [128, 2 * D], F32))
        xtb = es.enter_context(nc.sbuf_tensor("xtb", [128, 2 * D], BF16))
        xfm = es.enter_context(nc.sbuf_tensor("xfm", [128, NKC * SL], BF16))
        qksb = es.enter_context(nc.sbuf_tensor("qksb", [128, 4 * 512], BF16))
        vsb = es.enter_context(nc.sbuf_tensor("vsb", [128, 2 * D], BF16))
        pst0 = es.enter_context(nc.psum_tensor("pst0", [128, 512], BF16))
        pst1 = es.enter_context(nc.psum_tensor("pst1", [128, 512], BF16))
        ps_qk = es.enter_context(nc.psum_tensor("ps_qk", [128, 1024], F32))
        ps_v = es.enter_context(nc.psum_tensor("ps_v", [128, 1024], F32))
        s_wslot = [es.enter_context(nc.semaphore(f"s_wsl{j}")) for j in range(NST)]
        s_cast = es.enter_context(nc.semaphore("s_cast"))
        s_ld = es.enter_context(nc.semaphore("s_ld"))
        s_xslot = [es.enter_context(nc.semaphore(f"s_xsl{j}")) for j in range(2)]
        s_xc = es.enter_context(nc.semaphore("s_xc"))
        s_tp = es.enter_context(nc.semaphore("s_tp"))
        s_tpe = es.enter_context(nc.semaphore("s_tpe"))
        s_mm = es.enter_context(nc.semaphore("s_mm"))
        s_qe = es.enter_context(nc.semaphore("s_qe"))
        s_mv = es.enter_context(nc.semaphore("s_mv"))
        s_ve = es.enter_context(nc.semaphore("s_ve"))
        s_st = es.enter_context(nc.semaphore("s_st"))

        sch = Sched(nc)
        sch._nopt = wstg

        sch.add("sync", lambda e: e.dma_start(out=idt[:], in_=identb[:, :]), incs=[(s_ld, 16)])
        for i in range(NW):
            waits = [(s_cast, i - NST + 1)] if i >= NST else []
            sch.add("sync",
                    (lambda e, i=i: e.dma_start(out=wstg[:, (i % NST) * 128:(i % NST + 1) * 128],
                                                in_=wqkv[i // 64, (i % 64) // NKC, i % NKC, :, :])),
                    waits=waits, incs=[(s_wslot[i % NST], 16)])
            sch.add("vector",
                    (lambda e, i=i: e.tensor_copy(out=wb[:, i * 128:(i + 1) * 128],
                                                  in_=wstg[:, (i % NST) * 128:(i % NST + 1) * 128])),
                    waits=[(s_wslot[i % NST], 16 * (i // NST + 1))], incs=[(s_cast, 1)])

        ldx = Counter(); xc = Counter(); tp = Counter(); tpe = Counter()
        mm = Counter(); qe = Counter(); mv = Counter(); ve = Counter(); st = Counter()

        for b in range(B):
            for t in range(NTT):
                i = b * NTT + t
                w = [(s_xc, i - 1)] if i >= 2 else []
                sch.add("sync",
                        (lambda e, b=b, t=t, i=i: e.dma_start(
                            out=xtk[:, (i % 2) * D:(i % 2 + 1) * D],
                            in_=xr[b, t * 128:(t + 1) * 128, :])),
                        waits=w, incs=[(s_xslot[i % 2], 16)])
                ldx.inc()
                cw = [(s_xslot[i % 2], 16 * (i // 2 + 1))]
                if i >= 2:
                    cw.append((s_tp, 8 * (i - 1)))
                sch.add("vector",
                        (lambda e, i=i: e.tensor_copy(
                            out=xtb[:, (i % 2) * D:(i % 2 + 1) * D],
                            in_=xtk[:, (i % 2) * D:(i % 2 + 1) * D])),
                        waits=cw, incs=[(s_xc, 1)])
                xc.inc()
                for kc in range(NKC):
                    j = i * NKC + kc
                    tw = [(s_xc, xc.v), (s_ld, 16)]
                    if j >= 2:
                        tw.append((s_tpe, j - 1))
                    sch.add("tensor",
                            (lambda e, i=i, kc=kc, j=j: e.transpose(
                                out=(pst0 if j % 2 == 0 else pst1)[:, 0:128],
                                in_=xtb[:, (i % 2) * D + kc * 128:(i % 2) * D + (kc + 1) * 128],
                                identity=idt[:])),
                            waits=tw, incs=[(s_tp, 1)])
                    tp.inc()
                    sch.add("vector",
                            (lambda e, t=t, kc=kc, j=j: e.tensor_copy(
                                out=xfm[:, kc * SL + t * 128: kc * SL + (t + 1) * 128],
                                in_=(pst0 if j % 2 == 0 else pst1)[:, 0:128])),
                            waits=[(s_tp, tp.v)], incs=[(s_tpe, 1)])
                    tpe.inc()

            for t2 in range(2):
                for mc in range(NKC):
                    for blk in range(2):
                        m = ((b * 2 + t2) * NKC + mc) * 2 + blk
                        w = [(s_tpe, tpe.v), (s_cast, NW)]
                        if m >= 2:
                            w.append((s_qe, m - 1))
                        def mmf(e, t2=t2, mc=mc, blk=blk, m=m):
                            last = None
                            for kc in range(NKC):
                                last = e.matmul(
                                    out=ps_qk[:, (m % 2) * 512:(m % 2) * 512 + 512],
                                    lhsT=wb[:, (t2 * 64 + kc * NKC + mc) * 128:(t2 * 64 + kc * NKC + mc + 1) * 128],
                                    rhs=xfm[:, kc * SL + blk * 512: kc * SL + blk * 512 + 512],
                                    start=(kc == 0), stop=(kc == NKC - 1))
                            return last
                        sch.add("tensor", mmf, waits=w, incs=[(s_mm, 1)])
                        mm.inc()
                        ew = [(s_mm, mm.v)]
                        if m >= 4:
                            ew.append((s_st, 16 * (st.v - 3)))
                        sch.add("vector",
                                (lambda e, m=m: e.tensor_copy(
                                    out=qksb[:, (m % 4) * 512:(m % 4 + 1) * 512],
                                    in_=ps_qk[:, (m % 2) * 512:(m % 2) * 512 + 512])),
                                waits=ew, incs=[(s_qe, 1)])
                        qe.inc()
                        sch.add("sync",
                                (lambda e, t2=t2, b=b, mc=mc, blk=blk, m=m: e.dma_start(
                                    out=qk_o[t2, b, mc, :, blk * 512:blk * 512 + 512],
                                    in_=qksb[:, (m % 4) * 512:(m % 4 + 1) * 512])),
                                waits=[(s_qe, qe.v)], incs=[(s_st, 16)])
                        st.inc()

            for t in range(NTT):
                for fh in range(2):
                    m = (b * NTT + t) * 2 + fh
                    w = [(s_tpe, tpe.v), (s_cast, NW)]
                    if m >= 2:
                        w.append((s_ve, m - 1))
                    def mvf(e, t=t, fh=fh, m=m):
                        last = None
                        for kc in range(NKC):
                            last = e.matmul(
                                out=ps_v[:, (m % 2) * 512:(m % 2) * 512 + 512],
                                lhsT=xfm[:, kc * SL + t * 128: kc * SL + (t + 1) * 128],
                                rhs=wb[:, (2 * 64 + kc * NKC + fh * 4) * 128:(2 * 64 + kc * NKC + fh * 4 + 4) * 128],
                                start=(kc == 0), stop=(kc == NKC - 1))
                        return last
                    sch.add("tensor", mvf, waits=w, incs=[(s_mv, 1)])
                    mv.inc()
                    ew = [(s_mv, mv.v)]
                    if m >= 4:
                        ew.append((s_st, 16 * st.v - 16 * 3))
                    sch.add("vector",
                            (lambda e, m=m, fh=fh: e.tensor_copy(
                                out=vsb[:, (m % 4) * 512:(m % 4 + 1) * 512],
                                in_=ps_v[:, (m % 2) * 512:(m % 2) * 512 + 512])),
                            waits=ew, incs=[(s_ve, 1)])
                    ve.inc()
                    sch.add("sync",
                            (lambda e, b=b, t=t, fh=fh, m=m: e.dma_start(
                                out=v_o[b, t * 128:(t + 1) * 128, fh * 512:fh * 512 + 512],
                                in_=vsb[:, (m % 4) * 512:(m % 4 + 1) * 512])),
                            waits=[(s_ve, ve.v)], incs=[(s_st, 16)])
                    st.inc()

        sch.add("sync", lambda e: None, waits=[(s_st, 16 * st.v)])
        sch.run()
    return nc


def run_qkv1(inputs, x2_tok):
    if "qkv1" not in _cache:
        _cache["qkv1"] = build_qkv1()
    nc = _cache["qkv1"]
    wq = _wchunks(inputs["L1_attn_w_q"].astype(f32))
    wk = _wchunks(inputs["L1_attn_w_k"].astype(f32))
    wv = _wchunks(inputs["L1_attn_w_v"].astype(f32))
    wqkv = np.stack([wq, wk, wv])
    ident = np.eye(128, dtype=BF)
    maps = []
    for c in range(NC):
        sl = slice(c * SL, (c + 1) * SL)
        maps.append({"xr": np.ascontiguousarray(x2_tok[:, sl, :]), "wqkv": wqkv, "identb": ident})
    res = run_bass_kernel_spmd(nc, maps, core_ids=list(range(NC)))
    qk = np.concatenate([r["qk_o"] for r in res.results], axis=4)
    v = np.concatenate([r["v_o"] for r in res.results], axis=1)
    return qk, v


CSH = 20.0


def build_attn1():
    nc = bass.Bass("TRN2", target_bir_lowering=False, debug=False)
    q_in = nc.dram_tensor("q_in", [B, 128, S], BF16, kind="ExternalInput").ap()
    k_in = nc.dram_tensor("k_in", [B, 128, S], BF16, kind="ExternalInput").ap()
    v_in = nc.dram_tensor("v_in", [2, B, S, DK], BF16, kind="ExternalInput").ap()
    sel64 = nc.dram_tensor("sel64", [65, 64], BF16, kind="ExternalInput").ap()
    o_o = nc.dram_tensor("o_o", [B, 128, S], BF16, kind="ExternalOutput").ap()

    NQB = S // 512
    NKCH = S // 128
    es = contextlib.ExitStack()
    with es:
        qsb = es.enter_context(nc.sbuf_tensor("qsb", [128, S], BF16))
        ksb = es.enter_context(nc.sbuf_tensor("ksb", [128, S], BF16))
        vsb = es.enter_context(nc.sbuf_tensor("vsb", [128, NKCH * 65], BF16))
        psb = [es.enter_context(nc.sbuf_tensor(f"psb{j}", [128, 1024], BF16)) for j in range(2)]
        osb = es.enter_context(nc.sbuf_tensor("osb", [128, S], BF16))
        oev = es.enter_context(nc.sbuf_tensor("oev", [128, 2 * 512], BF16))
        slt = es.enter_context(nc.sbuf_tensor("slt", [65, 64], BF16))
        sps = [es.enter_context(nc.psum_tensor(f"sps{j}", [128, 1024], F32)) for j in range(2)]
        ops = [es.enter_context(nc.psum_tensor(f"ops{j}", [128, 512], F32)) for j in range(2)]
        lbp = [es.enter_context(nc.psum_tensor(f"lbp{j}", [64, 512], F32)) for j in range(2)]
        s_ld = es.enter_context(nc.semaphore("s_ld"))
        s_vs = es.enter_context(nc.semaphore("s_vs"))
        s_smm = es.enter_context(nc.semaphore("s_smm"))
        s_ex = es.enter_context(nc.semaphore("s_ex"))
        s_pv = es.enter_context(nc.semaphore("s_pv"))
        s_oe = es.enter_context(nc.semaphore("s_oe"))
        s_rc = es.enter_context(nc.semaphore("s_rc"))
        s_bc = es.enter_context(nc.semaphore("s_bc"))
        s_on = es.enter_context(nc.semaphore("s_on"))
        s_st = es.enter_context(nc.semaphore("s_st"))

        sch = Sched(nc)
        sch._nopt = slt

        sch.add("sync", lambda e: e.dma_start(out=slt[:], in_=sel64[:, :]), incs=[(s_ld, 16)])

        ld = Counter(); vs = Counter(); smm = Counter(); ex = Counter()
        pv = Counter(); oe = Counter(); rc = Counter(); bc = Counter()
        on = Counter(); st = Counter()
        prg = Counter()
        qg = Counter()

        for b in range(B):
            lw = []
            if b:
                lw = [(s_smm, smm.v), (s_pv, pv.v)]
            def ldf(e, b=b):
                e.dma_start(out=qsb[:], in_=q_in[b, :, :]).then_inc(s_ld, 16)
                e.dma_start(out=ksb[:], in_=k_in[b, :, :]).then_inc(s_ld, 16)
                return None
            sch.add("sync", ldf, waits=lw)
            ld.inc(2)
            for hl in range(2):
                hr = slice(hl * 64, hl * 64 + 64)
                vw = [(s_pv, pv.v)] if (b or hl) else []
                sch.add("sync",
                        (lambda e, b=b, hl=hl: e.dma_start(
                            out=vsb[:].rearrange("p (c j) -> p c j", j=65)[:, :, 0:64],
                            in_=v_in[hl, b].rearrange("(c p) j -> p c j", p=128))),
                        waits=vw, incs=[(s_vs, 16)])
                vs.inc()
                sch.add("vector",
                        (lambda e: e.memset(
                            vsb[:].rearrange("p (c j) -> p c j", j=65)[:, :, 64:65], 1.0)),
                        waits=[(s_vs, 17 * (2 * b + hl) + 16)], incs=[(s_vs, 1)])
                vs.inc()

                for qb in range(NQB):
                    qgi = qg.v
                    for pr in range(NKCH // 2):
                        prg_i = prg.v
                        w = [(s_ld, 16 + 32 * (b + 1))]
                        if prg_i >= 2:
                            w.append((s_ex, prg_i - 1))
                        def smf(e, hl=hl, qb=qb, pr=pr, prg_i=prg_i):
                            hr = slice(hl * 64, hl * 64 + 64)
                            last = None
                            for half in range(2):
                                kc = pr * 2 + half
                                last = e.matmul(
                                    out=sps[prg_i % 2][:, half * 512:(half + 1) * 512],
                                    lhsT=ksb[hr, kc * 128:(kc + 1) * 128],
                                    rhs=qsb[hr, qb * 512:(qb + 1) * 512],
                                    start=True, stop=True)
                            return last
                        sch.add("tensor", smf, waits=w, incs=[(s_smm, 1)])
                        smm.inc()
                        w = [(s_smm, smm.v)]
                        if prg_i >= 2:
                            w.append((s_pv, 2 * (prg_i - 1)))
                        sch.add("scalar",
                                (lambda e, prg_i=prg_i: e.activation(
                                    out=psb[prg_i % 2][:], in_=sps[prg_i % 2][:],
                                    func=AF.Exp, scale=np.float32(0.125), bias=np.float32(-CSH))),
                                waits=w, incs=[(s_ex, 1)])
                        ex.inc()
                        for half in range(2):
                            kc = pr * 2 + half
                            w = [(s_ex, ex.v), (s_vs, 17 * (2 * b + hl + 1))]
                            if pr == 0 and half == 0 and qgi >= 2:
                                w.append((s_oe, qgi - 1))
                            def pvf(e, kc=kc, half=half, prg_i=prg_i, qgi=qgi):
                                return e.matmul(
                                    out=ops[qgi % 2][0:65, :],
                                    lhsT=vsb[:, kc * 65:(kc + 1) * 65],
                                    rhs=psb[prg_i % 2][:, half * 512:(half + 1) * 512],
                                    start=(kc == 0), stop=(kc == NKCH - 1))
                            sch.add("tensor", pvf, waits=w, incs=[(s_pv, 1)])
                            pv.inc()
                        prg.inc()
                    ew = [(s_pv, pv.v)]
                    if qgi >= 2:
                        ew.append((s_on, qgi - 1))
                    sch.add("vector",
                            (lambda e, qgi=qgi: e.tensor_copy(
                                out=oev[0:65, (qgi % 2) * 512:(qgi % 2) * 512 + 512],
                                in_=ops[qgi % 2][0:65, :])),
                            waits=ew, incs=[(s_oe, 1)])
                    oe.inc()
                    def rcf(e, qgi=qgi):
                        with nc.allow_low_precision(reason="softmax 1/l in bf16"):
                            return e.reciprocal(
                                out=oev[64:65, (qgi % 2) * 512:(qgi % 2) * 512 + 512],
                                in_=oev[64:65, (qgi % 2) * 512:(qgi % 2) * 512 + 512])
                    sch.add("vector", rcf, waits=[(s_oe, oe.v)], incs=[(s_rc, 1)])
                    rc.inc()
                    w = [(s_rc, rc.v), (s_ld, 16)]
                    if qgi >= 2:
                        w.append((s_on, qgi - 1))
                    sch.add("tensor",
                            (lambda e, qgi=qgi: e.matmul(
                                out=lbp[qgi % 2][:, :], lhsT=slt[:, :],
                                rhs=oev[0:65, (qgi % 2) * 512:(qgi % 2) * 512 + 512],
                                start=True, stop=True)),
                            waits=w, incs=[(s_bc, 1)])
                    bc.inc()
                    sch.add("vector",
                            (lambda e, hl=hl, qb=qb, qgi=qgi: e.tensor_mul(
                                out=osb[hl * 64:hl * 64 + 64, qb * 512:(qb + 1) * 512],
                                in0=oev[0:64, (qgi % 2) * 512:(qgi % 2) * 512 + 512],
                                in1=lbp[qgi % 2][:, :])),
                            waits=[(s_bc, bc.v)], incs=[(s_on, 1)])
                    on.inc()
                    qg.inc()
            sch.add("sync", (lambda e, b=b: e.dma_start(out=o_o[b, :, :], in_=osb[:])),
                    waits=[(s_on, on.v)], incs=[(s_st, 16)])
            st.inc()
            sch.add("vector", lambda e: None, waits=[(s_st, 16 * st.v)])

        sch.add("sync", lambda e: None, waits=[(s_st, 16 * st.v)])
        sch.run()
    return nc


def run_attn1(qk_hs, v_full):
    if "attn1" not in _cache:
        _cache["attn1"] = build_attn1()
    nc = _cache["attn1"]
    sel = np.zeros((65, 64), dtype=BF)
    sel[64, :] = 1
    maps = []
    for c in range(NC):
        vv = np.stack([np.ascontiguousarray(v_full[:, :, (2 * c + hl) * 64:(2 * c + hl + 1) * 64])
                       for hl in range(2)])
        maps.append({"q_in": np.ascontiguousarray(qk_hs[0, :, c]),
                     "k_in": np.ascontiguousarray(qk_hs[1, :, c]),
                     "v_in": vv, "sel64": sel})
    res = run_bass_kernel_spmd(nc, maps, core_ids=list(range(NC)))
    return np.stack([r["o_o"] for r in res.results], axis=0)


def build_ffn1():
    nc = bass.Bass("TRN2", target_bir_lowering=False, debug=False)
    xfm_in = nc.dram_tensor("xfm_in", [B, NKC, 128, SL], BF16, kind="ExternalInput").ap()
    xr = nc.dram_tensor("xr", [B, SL, D], F32, kind="ExternalInput").ap()
    wfc = nc.dram_tensor("wfc", [NKC, 32, 128, 128], F32, kind="ExternalInput").ap()
    wpj = nc.dram_tensor("wpj", [32, NKC, 128, 128], F32, kind="ExternalInput").ap()
    identb = nc.dram_tensor("identb", [128, 128], BF16, kind="ExternalInput").ap()
    out_t = nc.dram_tensor("out_t", [B, SL, D], F32, kind="ExternalOutput").ap()

    NST = 4
    NW = 512
    es = contextlib.ExitStack()
    with es:
        wstg = es.enter_context(nc.sbuf_tensor("wstg", [128, NST * 128], F32))
        wfb = es.enter_context(nc.sbuf_tensor("wfb", [128, 256 * 128], BF16))
        wpb = es.enter_context(nc.sbuf_tensor("wpb", [128, 256 * 128], BF16))
        idt = es.enter_context(nc.sbuf_tensor("idt", [128, 128], BF16))
        xfm = es.enter_context(nc.sbuf_tensor("xfm", [128, NKC * 512], BF16))
        hb = es.enter_context(nc.sbuf_tensor("hb", [128, 32 * 512], BF16))
        hrl = es.enter_context(nc.sbuf_tensor("hrl", [128, 512], BF16))
        yt = es.enter_context(nc.sbuf_tensor("yt", [128, 512], F32))
        mem = es.enter_context(nc.sbuf_tensor("mem", [128, 512], F32))
        spk = es.enter_context(nc.sbuf_tensor("spk", [128, 512], F32))
        m1 = es.enter_context(nc.sbuf_tensor("m1", [128, 512], F32))
        accfm = es.enter_context(nc.sbuf_tensor("accfm", [128, NKC * 512], BF16))
        xres = es.enter_context(nc.sbuf_tensor("xres", [128, 4 * D], F32))
        xsq = es.enter_context(nc.sbuf_tensor("xsq", [128, D], F32))
        scol = es.enter_context(nc.sbuf_tensor("scol", [128, 16], F32))
        ps_h = es.enter_context(nc.psum_tensor("ps_h", [128, 1024], F32))
        ps_y = es.enter_context(nc.psum_tensor("ps_y", [128, 1024], F32))
        pst0 = es.enter_context(nc.psum_tensor("pst0", [128, 512], BF16))
        pst1 = es.enter_context(nc.psum_tensor("pst1", [128, 512], BF16))
        s_wslot = [es.enter_context(nc.semaphore(f"s_wsl{j}")) for j in range(NST)]
        s_cast = es.enter_context(nc.semaphore("s_cast"))
        s_ld = es.enter_context(nc.semaphore("s_ld"))
        s_ldx = es.enter_context(nc.semaphore("s_ldx"))
        s_ldr = es.enter_context(nc.semaphore("s_ldr"))
        s_mh = es.enter_context(nc.semaphore("s_mh"))
        s_he = es.enter_context(nc.semaphore("s_he"))
        s_my = es.enter_context(nc.semaphore("s_my"))
        s_sn = es.enter_context(nc.semaphore("s_sn"))
        s_tp = es.enter_context(nc.semaphore("s_tp"))
        s_ra = es.enter_context(nc.semaphore("s_ra"))
        s_nrm = es.enter_context(nc.semaphore("s_nrm"))
        s_st = es.enter_context(nc.semaphore("s_st"))

        sch = Sched(nc)
        sch._nopt = wstg

        sch.add("sync", lambda e: e.dma_start(out=idt[:], in_=identb[:, :]), incs=[(s_ld, 16)])
        for i in range(NW):
            waits = [(s_cast, i - NST + 1)] if i >= NST else []
            def ldw(e, i=i):
                if i < 256:
                    src = wfc[i // 32, i % 32, :, :]
                else:
                    j = i - 256
                    src = wpj[j // NKC, j % NKC, :, :]
                return e.dma_start(out=wstg[:, (i % NST) * 128:(i % NST + 1) * 128], in_=src)
            sch.add("sync", ldw, waits=waits, incs=[(s_wslot[i % NST], 16)])
            def cst(e, i=i):
                dst = wfb if i < 256 else wpb
                j = i if i < 256 else i - 256
                return e.tensor_copy(out=dst[:, j * 128:(j + 1) * 128],
                                     in_=wstg[:, (i % NST) * 128:(i % NST + 1) * 128])
            sch.add("vector", cst, waits=[(s_wslot[i % NST], 16 * (i // NST + 1))], incs=[(s_cast, 1)])

        ldx = Counter(); ldr = Counter(); mh = Counter(); he = Counter()
        my = Counter(); sn = Counter(); tp = Counter(); ra = Counter()
        nrm = Counter(); st = Counter()

        for b in range(B):
            for hf in range(2):
                hx = b * 2 + hf
                tp_h = tp.v
                st_h = st.v
                lw = [(s_mh, mh.v)] if hx else []
                def ldxf(e, b=b, hf=hf):
                    for kc in range(NKC):
                        e.dma_start(out=xfm[:, kc * 512:(kc + 1) * 512],
                                    in_=xfm_in[b, kc, :, hf * 512:(hf + 1) * 512]).then_inc(s_ldx, 16)
                    return None
                sch.add("sync", ldxf, waits=lw)
                ldx.inc(NKC)
                lw2 = [(s_ra, ra.v)] if hx else []
                def ldrf(e, b=b, hf=hf):
                    for t in range(4):
                        tt = hf * 4 + t
                        e.dma_start(out=xres[:, t * D:(t + 1) * D],
                                    in_=xr[b, tt * 128:(tt + 1) * 128, :]).then_inc(s_ldr, 16)
                    return None
                if hx:
                    lw2.append((s_st, 16 * st_h))
                sch.add("sync", ldrf, waits=lw2)
                ldr.inc(4)

                for mc in range(32):
                    m = hx * 32 + mc
                    w = [(s_ldx, 16 * ldx.v), (s_cast, 256)]
                    if m >= 2:
                        w.append((s_he, 2 * (m - 1)))
                    def mhf(e, mc=mc, m=m):
                        last = None
                        for kc in range(NKC):
                            last = e.matmul(
                                out=ps_h[:, (m % 2) * 512:(m % 2) * 512 + 512],
                                lhsT=wfb[:, (kc * 32 + mc) * 128:(kc * 32 + mc + 1) * 128],
                                rhs=xfm[:, kc * 512:(kc + 1) * 512],
                                start=(kc == 0), stop=(kc == NKC - 1))
                        return last
                    sch.add("tensor", mhf, waits=w, incs=[(s_mh, 1)])
                    mh.inc()
                    def ev1(e, m=m):
                        return e.tensor_scalar_max(out=hrl[:], in0=ps_h[:, (m % 2) * 512:(m % 2) * 512 + 512],
                                                   scalar1=0.0)
                    sch.add("vector", ev1, waits=[(s_mh, mh.v)], incs=[(s_he, 1)])
                    he.inc()
                    def ev2(e, mc=mc):
                        return e.tensor_mul(out=hb[:, mc * 512:(mc + 1) * 512], in0=hrl[:], in1=hrl[:])
                    sch.add("vector", ev2, waits=[(s_he, he.v)], incs=[(s_he, 1)])
                    he.inc()

                for mc2 in range(NKC):
                    m = hx * NKC + mc2
                    w = [(s_he, he.v), (s_cast, NW)]
                    if m >= 2:
                        w.append((s_sn, 19 * (m - 1)))
                    def myf(e, mc2=mc2, m=m):
                        last = None
                        for kc2 in range(32):
                            last = e.matmul(
                                out=ps_y[:, (m % 2) * 512:(m % 2) * 512 + 512],
                                lhsT=wpb[:, (kc2 * NKC + mc2) * 128:(kc2 * NKC + mc2 + 1) * 128],
                                rhs=hb[:, kc2 * 512:(kc2 + 1) * 512],
                                start=(kc2 == 0), stop=(kc2 == 31))
                        return last
                    sch.add("tensor", myf, waits=w, incs=[(s_my, 1)])
                    my.inc()
                    def snn(e, mc2=mc2, m=m):
                        ps = ps_y[:, (m % 2) * 512:(m % 2) * 512 + 512]
                        e.tensor_copy(out=yt[:], in_=ps)
                        e.tensor_copy(out=mem[:], in_=yt[:])
                        e.tensor_single_scalar(out=spk[:], in_=mem[:], scalar=1.0, op=ALU.is_gt)
                        e.tensor_copy(out=accfm[:, mc2 * 512:(mc2 + 1) * 512], in_=spk[:])
                        for it in range(4):
                            e.scalar_tensor_tensor(out=m1[:], in0=mem[:], scalar=0.9, in1=yt[:],
                                                   op0=ALU.mult, op1=ALU.add)
                            e.tensor_sub(out=mem[:], in0=m1[:], in1=spk[:])
                            e.tensor_single_scalar(out=spk[:], in_=mem[:], scalar=1.0, op=ALU.is_gt)
                            e.tensor_tensor(out=accfm[:, mc2 * 512:(mc2 + 1) * 512],
                                            in0=accfm[:, mc2 * 512:(mc2 + 1) * 512], in1=spk[:],
                                            op=ALU.add)
                        return e.tensor_copy(out=accfm[:, mc2 * 512:(mc2 + 1) * 512],
                                             in_=accfm[:, mc2 * 512:(mc2 + 1) * 512])
                    sch.add("vector", snn, waits=[(s_my, my.v), (s_tp, tp_h)], incs=[(s_sn, 19)])
                    sn.inc(19)

                for t in range(4):
                    tt = hf * 4 + t
                    for mc2 in range(NKC):
                        j = (hx * 4 + t) * NKC + mc2
                        w = [(s_sn, sn.v), (s_ld, 16), (s_ldr, 16 * ldr.v)]
                        if j >= 2:
                            w.append((s_ra, j - 1))
                        sch.add("tensor",
                                (lambda e, t=t, mc2=mc2, j=j: e.transpose(
                                    out=(pst0 if j % 2 == 0 else pst1)[:, 0:128],
                                    in_=accfm[:, mc2 * 512 + t * 128: mc2 * 512 + (t + 1) * 128],
                                    identity=idt[:])),
                                waits=w, incs=[(s_tp, 1)])
                        tp.inc()
                        sch.add("vector",
                                (lambda e, t=t, mc2=mc2, j=j: e.scalar_tensor_tensor(
                                    out=xres[:, t * D + mc2 * 128: t * D + (mc2 + 1) * 128],
                                    in0=(pst0 if j % 2 == 0 else pst1)[:, 0:128],
                                    scalar=0.2, in1=xres[:, t * D + mc2 * 128: t * D + (mc2 + 1) * 128],
                                    op0=ALU.mult, op1=ALU.add)),
                                waits=[(s_tp, tp.v)], incs=[(s_ra, 1)])
                        ra.inc()
                    _emit_ln(sch, nc, xres, xsq, scol, t, s_ra, ra.v, s_nrm, nrm, AF, ALU)
                    sch.add("sync",
                            (lambda e, b=b, tt=tt, t=t: e.dma_start(
                                out=out_t[b, tt * 128:(tt + 1) * 128, :],
                                in_=xres[:, t * D:(t + 1) * D])),
                            waits=[(s_nrm, nrm.v)], incs=[(s_st, 16)])
                    st.inc()

        sch.add("sync", lambda e: None, waits=[(s_st, 16 * st.v)])
        sch.run()
    return nc


def run_ffn1(inputs, x3_fm, x3_tok):
    if "ffn1" not in _cache:
        _cache["ffn1"] = build_ffn1()
    nc = _cache["ffn1"]
    wfc = _wchunks_T(inputs["L1_ffn_W_fc"].astype(f32))
    wpj = _wchunks_T(inputs["L1_ffn_W_proj"].astype(f32))
    ident = np.eye(128, dtype=BF)
    maps = []
    for c in range(NC):
        sl = slice(c * SL, (c + 1) * SL)
        maps.append({"xfm_in": np.ascontiguousarray(x3_fm[:, :, :, sl]),
                     "xr": np.ascontiguousarray(x3_tok[:, sl, :]),
                     "wfc": wfc, "wpj": wpj, "identb": ident})
    res = run_bass_kernel_spmd(nc, maps, core_ids=list(range(NC)))
    return np.concatenate([r["out_t"] for r in res.results], axis=1)


# version 02 (final)
# speedup vs baseline: 1.3883x; 1.3883x over previous
import contextlib
import numpy as np
import ml_dtypes
import concourse.bass as bass
import concourse.mybir as mybir
from concourse.bass_core_rust import IndirectOffsetOnAxis
from concourse.bass_utils import run_bass_kernel_spmd

F32 = mybir.dt.float32
BF16 = mybir.dt.bfloat16
I32 = mybir.dt.int32
AF = mybir.ActivationFunctionType
ALU = mybir.AluOpType
BF = ml_dtypes.bfloat16
f32 = np.float32

NC = 8
B = 4
S = 8192
SL = S // NC
D = 1024
H = 16
DK = 64
NKC = D // 128
NFC = 3 * D // 128
NTT = SL // 128
LN_EPS = 1e-5

_cache = {}
PP_STOP = 'full'


class Sched:

    ENGINES = ("sync", "vector", "scalar", "tensor", "gpsimd")
    SERIAL = False

    def __init__(self, nc):
        self.nc = nc
        self.steps = []
        if Sched.SERIAL:
            self._g = nc.semaphore("g_serial").__enter__()

    def add(self, engine, fn, waits=(), incs=()):
        if Sched.SERIAL:
            if not hasattr(self, "_gv"):
                self._gv = 0
            waits = ((self._g, self._gv),)
            is_dma_engine = engine in ("sync", "gpsimd")
            amt = 16 if is_dma_engine else 1
            self._gv += amt
            incs = ((self._g, amt),)
            fn0 = fn
            def fn(e, fn0=fn0, engine=engine, is_dma=is_dma_engine):
                inst = fn0(e)
                if inst is None:
                    if is_dma:
                        inst = e.dma_start(out=self._nopt[0:1, 0:1], in_=self._nopt[0:1, 1:2])
                    elif engine == "scalar":
                        inst = e.mul(self._nopt[0:1, 0:1], self._nopt[0:1, 0:1], 1.0)
                    else:
                        inst = e.memset(self._nopt[0:1, 0:1], 0)
                return inst
        self.steps.append((engine, fn, tuple(waits), tuple(incs)))

    def run(self):
        per = {k: [] for k in self.ENGINES}
        for st in self.steps:
            per[st[0]].append(st)
        with self.nc.Block() as blk:
            def mk(eng_steps):
                def body(e):
                    for _, fn, waits, incs in eng_steps:
                        for sem, val in waits:
                            e.wait_ge(sem, val)
                        inst = fn(e)
                        if incs:
                            assert inst is not None
                            for sem, amt in incs:
                                inst = inst.then_inc(sem, amt)
                return body
            for name in self.ENGINES:
                if per[name]:
                    getattr(blk, name)(mk(per[name]))


class Counter:
    __slots__ = ("v",)
    def __init__(self):
        self.v = 0
    def inc(self, n=1):
        self.v += n
        return self.v


def _wchunks_T(w):
    out_f, in_d = w.shape
    r = w.reshape(out_f // 128, 128, in_d // 128, 128)
    return np.ascontiguousarray(r.transpose(2, 0, 3, 1))


def _wchunks(w):
    in_d, out_f = w.shape
    r = w.reshape(in_d // 128, 128, out_f // 128, 128)
    return np.ascontiguousarray(r.transpose(0, 2, 1, 3))


def emit_weight_cast(sch, sem_ld, sem_cast, nw, src_ap_fn, wstg, wbf, nst):
    ld, ca = Counter(), Counter()
    for i in range(nw):
        waits = [(sem_cast, i - nst + 1)] if i >= nst else []
        def loadf(e, i=i):
            return e.dma_start(out=wstg[:, (i % nst) * 128:(i % nst + 1) * 128],
                               in_=src_ap_fn(i))
        sch.add("sync", loadf, waits=waits, incs=[(sem_ld, 16)])
        def castf(e, i=i):
            return e.tensor_copy(out=wbf[:, i * 128:(i + 1) * 128],
                                 in_=wstg[:, (i % nst) * 128:(i % nst + 1) * 128])
        sch.add("vector", castf, waits=[(sem_ld, 16 * (i + 1))], incs=[(sem_cast, 1)])
    return nw


def build_qkv0():
    if "qkv0" in _cache:
        return _cache["qkv0"]
    nc = bass.Bass("TRN2", target_bir_lowering=False, debug=False)
    x = nc.dram_tensor("x", [B, SL, D], F32, kind="ExternalInput").ap()
    wq = nc.dram_tensor("wq", [NKC, NFC, 128, 128], F32, kind="ExternalInput").ap()
    ident = nc.dram_tensor("ident", [128, 128], BF16, kind="ExternalInput").ap()
    bd64 = nc.dram_tensor("bd64", [128, 2], BF16, kind="ExternalInput").ap()
    sel2 = nc.dram_tensor("sel2", [2, 128], F32, kind="ExternalInput").ap()
    qkv_o = nc.dram_tensor("qkv_o", [B, NFC, 128, SL], BF16, kind="ExternalOutput").ap()

    NW = NKC * NFC
    NST = 4
    es = contextlib.ExitStack()
    with es:
        wstg = es.enter_context(nc.sbuf_tensor("wstg", [128, NST * 128], F32))
        wbf = es.enter_context(nc.sbuf_tensor("wbf", [128, NW * 128], BF16))
        idt = es.enter_context(nc.sbuf_tensor("idt", [128, 128], BF16))
        bdt = es.enter_context(nc.sbuf_tensor("bdt", [128, 2], BF16))
        sl2 = es.enter_context(nc.sbuf_tensor("sl2", [2, 128], F32))
        xtok = es.enter_context(nc.sbuf_tensor("xtok", [128, 2 * D], F32))
        xtokb = es.enter_context(nc.sbuf_tensor("xtokb", [128, 2 * D], BF16))
        xfm = es.enter_context(nc.sbuf_tensor("xfm", [128, NKC * SL], BF16))
        qkvs = es.enter_context(nc.sbuf_tensor("qkvs", [128, NFC * SL], BF16))
        ksq = es.enter_context(nc.sbuf_tensor("ksq", [128, SL], BF16))
        mrow = es.enter_context(nc.sbuf_tensor("mrow", [2, 3 * SL], F32))
        pst0 = es.enter_context(nc.psum_tensor("pst0", [128, 512], BF16))
        pst1 = es.enter_context(nc.psum_tensor("pst1", [128, 512], BF16))
        psm = es.enter_context(nc.psum_tensor("psm", [128, 2 * 512], F32))
        pks = es.enter_context(nc.psum_tensor("pks", [128, 1024], F32))
        pkb = es.enter_context(nc.psum_tensor("pkb", [128, 1024], F32))
        s_ld = es.enter_context(nc.semaphore("s_ld"))
        s_wslot = [es.enter_context(nc.semaphore(f"s_wslot{j}")) for j in range(4)]
        s_xslot = [es.enter_context(nc.semaphore(f"s_xslot{j}")) for j in range(2)]
        s_cast = es.enter_context(nc.semaphore("s_cast"))
        s_ldx = es.enter_context(nc.semaphore("s_ldx"))
        s_xc = es.enter_context(nc.semaphore("s_xc"))
        s_tp = es.enter_context(nc.semaphore("s_tp"))
        s_ev = es.enter_context(nc.semaphore("s_ev"))
        s_mm = es.enter_context(nc.semaphore("s_mm"))
        s_qe = es.enter_context(nc.semaphore("s_qe"))
        s_kst = es.enter_context(nc.semaphore("s_kst"))
        s_krow = es.enter_context(nc.semaphore("s_krow"))
        s_kbc = es.enter_context(nc.semaphore("s_kbc"))
        s_kn = es.enter_context(nc.semaphore("s_kn"))
        s_st = es.enter_context(nc.semaphore("s_st"))

        sch = Sched(nc)
        sch._nopt = wstg
        sch.add("sync", lambda e: e.dma_start(out=idt[:], in_=ident[:, :]), incs=[(s_ld, 16)])
        sch.add("sync", lambda e: e.dma_start(out=bdt[:], in_=bd64[:, :]), incs=[(s_ld, 16)])
        sch.add("sync", lambda e: e.dma_start(out=sl2[:], in_=sel2[:, :]), incs=[(s_ld, 16)])
        ldc = Counter(); ldc.inc(3)
        cac = Counter()
        for i in range(NW):
            waits = [(s_cast, i - NST + 1)] if i >= NST else []
            sch.add("sync",
                    (lambda e, i=i: e.dma_start(out=wstg[:, (i % NST) * 128:(i % NST + 1) * 128],
                                                in_=wq[i // NFC, i % NFC, :, :])),
                    waits=waits, incs=[(s_wslot[i % NST], 16)])
            sch.add("vector",
                    (lambda e, i=i: e.tensor_copy(out=wbf[:, i * 128:(i + 1) * 128],
                                                  in_=wstg[:, (i % NST) * 128:(i % NST + 1) * 128])),
                    waits=[(s_wslot[i % NST], 16 * (i // NST + 1))], incs=[(s_cast, 1)])

        ldx = Counter(); xc = Counter(); tp = Counter(); ev = Counter()
        mm = Counter(); qe = Counter(); kst = Counter(); krow = Counter()
        kbc = Counter(); kn = Counter(); st = Counter()

        for b in range(B):
            for t in range(NTT):
                i = b * NTT + t
                waits = [(s_xc, i - 1)] if i >= 2 else []
                sch.add("sync",
                        (lambda e, b=b, t=t, i=i: e.dma_start(
                            out=xtok[:, (i % 2) * D:(i % 2 + 1) * D],
                            in_=x[b, t * 128:(t + 1) * 128, :])),
                        waits=waits, incs=[(s_xslot[i % 2], 16)])
                ldx.inc()
                cw = [(s_xslot[i % 2], 16 * (i // 2 + 1))]
                if i >= 2:
                    cw.append((s_tp, 8 * (i - 1)))
                sch.add("vector",
                        (lambda e, i=i: e.tensor_copy(
                            out=xtokb[:, (i % 2) * D:(i % 2 + 1) * D],
                            in_=xtok[:, (i % 2) * D:(i % 2 + 1) * D])),
                        waits=cw, incs=[(s_xc, 1)])
                xc.inc()
                for kc in range(NKC):
                    j = i * NKC + kc
                    waits = [(s_xc, xc.v)]
                    if j >= 2:
                        waits.append((s_ev, j - 1))
                    sch.add("tensor",
                            (lambda e, i=i, kc=kc, j=j: e.transpose(
                                out=(pst0 if j % 2 == 0 else pst1)[:, 0:128],
                                in_=xtokb[:, (i % 2) * D + kc * 128:(i % 2) * D + (kc + 1) * 128],
                                identity=idt[:])),
                            waits=waits, incs=[(s_tp, 1)])
                    tp.inc()
                    sch.add("vector",
                            (lambda e, t=t, kc=kc, j=j: e.tensor_copy(
                                out=xfm[:, kc * SL + t * 128: kc * SL + (t + 1) * 128],
                                in_=(pst0 if j % 2 == 0 else pst1)[:, 0:128])),
                            waits=[(s_tp, tp.v)], incs=[(s_ev, 1)])
                    ev.inc()

            for fc in range(NFC):
                for blk2 in range(SL // 512):
                    m = (b * NFC + fc) * (SL // 512) + blk2
                    waits = [(s_ev, ev.v), (s_cast, NW)]
                    if m >= 1:
                        waits.append((s_qe, m))
                    def mmf(e, fc=fc, blk2=blk2, m=m):
                        last = None
                        for kc in range(NKC):
                            last = e.matmul(
                                out=psm[:, (m % 2) * 512:(m % 2) * 512 + 512],
                                lhsT=wbf[:, (kc * NFC + fc) * 128:(kc * NFC + fc + 1) * 128],
                                rhs=xfm[:, kc * SL + blk2 * 512: kc * SL + blk2 * 512 + 512],
                                start=(kc == 0), stop=(kc == NKC - 1))
                        return last
                    sch.add("tensor", mmf, waits=waits, incs=[(s_mm, 1)])
                    mm.inc()
                    sch.add("vector",
                            (lambda e, fc=fc, blk2=blk2, m=m: e.tensor_copy(
                                out=qkvs[:, fc * SL + blk2 * 512: fc * SL + blk2 * 512 + 512],
                                in_=psm[:, (m % 2) * 512:(m % 2) * 512 + 512])),
                            waits=[(s_mm, mm.v)], incs=[(s_qe, 1)])
                    qe.inc()

            for ik in range(8):
                fc = 8 + ik
                koff = fc * SL
                sch.add("scalar",
                        (lambda e, koff=koff: e.activation(out=ksq[:, 0:SL], in_=qkvs[:, koff:koff + SL],
                                                           func=AF.Square)),
                        waits=[(s_qe, qe.v), (s_kn, kn.v)], incs=[(s_kst, 1)])
                kst.inc()
                for blk2 in range(SL // 512):
                    so = blk2 * 512
                    def stf(e, koff=koff, so=so):
                        e.matmul(out=pks[0:2, 0:512], lhsT=bdt[:, :],
                                 rhs=qkvs[:, koff + so: koff + so + 512], start=True, stop=True)
                        return e.matmul(out=pks[0:2, 512:1024], lhsT=bdt[:, :],
                                        rhs=ksq[:, so:so + 512], start=True, stop=True)
                    sch.add("tensor", stf,
                            waits=[(s_kst, kst.v), (s_ld, 48), (s_krow, krow.v)], incs=[(s_kst, 1)])
                    kst.inc()
                    def rowf(e):
                        e.tensor_scalar_mul(out=mrow[0:2, 0:512], in0=pks[0:2, 0:512], scalar1=1.0 / 64)
                        e.tensor_scalar_mul(out=mrow[0:2, 512:1024], in0=pks[0:2, 512:1024], scalar1=1.0 / 64)
                        e.tensor_mul(out=mrow[0:2, 1024:1536], in0=mrow[0:2, 0:512], in1=mrow[0:2, 0:512])
                        return e.tensor_sub(out=mrow[0:2, 512:1024], in0=mrow[0:2, 512:1024],
                                            in1=mrow[0:2, 1024:1536])
                    sch.add("vector", rowf, waits=[(s_kst, kst.v), (s_kbc, kbc.v)], incs=[(s_krow, 1)])
                    krow.inc()
                    def actf(e):
                        e.activation(out=mrow[0:2, 512:1024], in_=mrow[0:2, 512:1024],
                                     func=AF.Ln, bias=np.float32(LN_EPS))
                        return e.activation(out=mrow[0:2, 512:1024], in_=mrow[0:2, 512:1024],
                                            func=AF.Exp, scale=np.float32(-0.5))
                    sch.add("scalar", actf, waits=[(s_krow, krow.v)], incs=[(s_krow, 1)])
                    krow.inc()
                    def bcf(e):
                        e.matmul(out=pkb[:, 0:512], lhsT=sl2[:, :],
                                 rhs=mrow[0:2, 0:512], start=True, stop=True)
                        return e.matmul(out=pkb[:, 512:1024], lhsT=sl2[:, :],
                                        rhs=mrow[0:2, 512:1024], start=True, stop=True)
                    w = [(s_krow, krow.v)]
                    if kbc.v:
                        w.append((s_kn, kn.v))
                    sch.add("tensor", bcf, waits=w, incs=[(s_kbc, 1)])
                    kbc.inc()
                    def nrmf(e, koff=koff, so=so):
                        sl_ = slice(koff + so, koff + so + 512)
                        e.tensor_sub(out=ksq[:, so:so + 512], in0=qkvs[:, sl_], in1=pkb[:, 0:512])
                        return e.tensor_mul(out=qkvs[:, sl_], in0=ksq[:, so:so + 512], in1=pkb[:, 512:1024])
                    sch.add("vector", nrmf, waits=[(s_kbc, kbc.v)], incs=[(s_kn, 1)])
                    kn.inc()

            for fc in range(NFC):
                w = [(s_qe, qe.v)] if fc < 8 or fc >= 16 else [(s_kn, kn.v)]
                sch.add("sync",
                        (lambda e, b=b, fc=fc: e.dma_start(out=qkv_o[b, fc, :, :],
                                                           in_=qkvs[:, fc * SL:(fc + 1) * SL])),
                        waits=w, incs=[(s_st, 16)])
                st.inc()
            sch.add("vector", lambda e: None, waits=[(s_st, 16 * st.v)])

        sch.add("sync", lambda e: None, waits=[(s_st, 16 * st.v)])
        sch.run()
    return nc


def _prep_qkv0(inputs):
    Wq = inputs["L0_attn_W_qkv"].astype(f32)
    wq = _wchunks_T(Wq)
    ident = np.eye(128, dtype=BF)
    bd64 = np.zeros((128, 2), dtype=BF)
    bd64[0:64, 0] = 1
    bd64[64:128, 1] = 1
    sel2 = np.zeros((2, 128), dtype=f32)
    sel2[0, 0:64] = 1
    sel2[1, 64:128] = 1
    x = inputs["L0_attn_x"].astype(f32)
    maps = []
    for c in range(NC):
        xs = np.ascontiguousarray(x[:, c * SL:(c + 1) * SL, :])
        maps.append({"x": xs, "wq": wq, "ident": ident, "bd64": bd64, "sel2": sel2})
    return maps


def run_qkv0(inputs):
    nc = build_qkv0()
    maps = _prep_qkv0(inputs)
    res = run_bass_kernel_spmd(nc, maps, core_ids=list(range(NC)))
    out = np.concatenate([r["qkv_o"] for r in res.results], axis=3)
    return out


def kernel(**inputs):
    x0 = inputs["L0_attn_x"].astype(f32)
    qkv_hs = run_qkv0(inputs)
    y_hs = run_scan(inputs, qkv_hs)
    x1_tok, _ = run_postproj("l0", inputs["L0_attn_W_o"].astype(f32),
                             y_hs, x0, True)
    moe_all = run_moe(inputs, x1_tok)
    x2_tok = run_combine(inputs, moe_all, x1_tok)
    qk, v = run_qkv1(inputs, x2_tok)
    o_hs = run_attn1(qk, v)
    x3_tok, x3_fm = run_postproj("l1", inputs["L1_attn_w_o"].astype(f32),
                                 o_hs, x2_tok, False)
    out = run_ffn1(inputs, x3_fm, x3_tok)
    return np.ascontiguousarray(out.astype(np.float32))


TSC = 512
NCH = S // TSC


def build_scan():
    nc = bass.Bass("TRN2", target_bir_lowering=False, debug=False)
    q_in = nc.dram_tensor("q_in", [B, 128, S], BF16, kind="ExternalInput").ap()
    k_in = nc.dram_tensor("k_in", [B, 128, S], BF16, kind="ExternalInput").ap()
    v_in = nc.dram_tensor("v_in", [B, 128, S], BF16, kind="ExternalInput").ap()
    wltm_l = nc.dram_tensor("wltm_l", [2, 128, 32], F32, kind="ExternalInput").ap()
    wltm_T = nc.dram_tensor("wltm_T", [2, 64, 64], F32, kind="ExternalInput").ap()
    pcol = nc.dram_tensor("pcol", [128, 16], F32, kind="ExternalInput").ap()
    vbc = nc.dram_tensor("vbc", [64, 128, 128], BF16, kind="ExternalInput").ap()
    y_o = nc.dram_tensor("y_o", [B, 128, S], BF16, kind="ExternalOutput").ap()

    es = contextlib.ExitStack()
    with es:
        kr = [es.enter_context(nc.sbuf_tensor(f"kr{h}", [128, S], BF16)) for h in range(2)]
        qr = [es.enter_context(nc.sbuf_tensor(f"qr{h}", [128, S], BF16)) for h in range(2)]
        vs = es.enter_context(nc.sbuf_tensor("vs", [128, S], BF16))
        ysb = es.enter_context(nc.sbuf_tensor("ysb", [128, S], BF16))
        vbct = es.enter_context(nc.sbuf_tensor("vbct", [128, 64 * 128], BF16))
        a_sb = es.enter_context(nc.sbuf_tensor("a_sb", [128, TSC], BF16))
        g_sb = es.enter_context(nc.sbuf_tensor("g_sb", [128, TSC], BF16))
        z_sb = es.enter_context(nc.sbuf_tensor("z_sb", [128, 2 * TSC], BF16))
        state = [es.enter_context(nc.sbuf_tensor(f"state{h}", [128, 32], BF16)) for h in range(2)]
        selc = [es.enter_context(nc.sbuf_tensor(f"selc{h}", [128, 32 * 64], BF16)) for h in range(2)]
        wTb = [es.enter_context(nc.sbuf_tensor(f"wTb{h}", [64, 64], BF16)) for h in range(2)]
        Rt = es.enter_context(nc.sbuf_tensor("Rt", [128, 64], F32))
        coef = es.enter_context(nc.sbuf_tensor("coef", [128, 64], F32))
        gw = es.enter_context(nc.sbuf_tensor("gw", [128, 32], F32))
        gt1 = es.enter_context(nc.sbuf_tensor("gt1", [128, 32], F32))
        gt2 = es.enter_context(nc.sbuf_tensor("gt2", [128, 32], F32))
        gt3 = es.enter_context(nc.sbuf_tensor("gt3", [128, 32], F32))
        gvn = es.enter_context(nc.sbuf_tensor("gvn", [128, 32], F32))
        gch = es.enter_context(nc.sbuf_tensor("gch", [128, 32], F32))
        gsp = es.enter_context(nc.sbuf_tensor("gsp", [128, 32], F32))
        pct = es.enter_context(nc.sbuf_tensor("pct", [128, 16], F32))
        pc1 = es.enter_context(nc.sbuf_tensor("pc1", [128, 2], F32))
        wTf = es.enter_context(nc.sbuf_tensor("wTf", [64, 64], F32))
        v_ps = [es.enter_context(nc.psum_tensor(f"v_ps{j}", [128, TSC], F32)) for j in range(4)]
        y_ps = [es.enter_context(nc.psum_tensor(f"y_ps{j}", [128, TSC], F32)) for j in range(2)]
        s_g = es.enter_context(nc.semaphore("s_g"))
        s_ldb = es.enter_context(nc.semaphore("s_ldb"))
        s_vb = es.enter_context(nc.semaphore("s_vb"))
        s_am = es.enter_context(nc.semaphore("s_am"))
        s_z = es.enter_context(nc.semaphore("s_z"))
        s_ym = es.enter_context(nc.semaphore("s_ym"))
        s_ye = es.enter_context(nc.semaphore("s_ye"))
        s_yst = es.enter_context(nc.semaphore("s_yst"))

        sch = Sched(nc)
        sch._nopt = gw

        gv = Counter()

        def chain(eng, fn):
            w = [(s_g, gv.v)]
            amt = 16 if eng in ("sync", "gpsimd") else 1
            gv.inc(amt)
            sch.add(eng, fn, waits=w, incs=[(s_g, amt)])

        chain("sync", lambda e: e.dma_start(out=pct[:], in_=pcol[:, :]))
        chain("sync", lambda e: e.dma_start(out=vbct[:].rearrange("p (n m) -> p n m", m=128), in_=vbc.rearrange("n p m -> p n m")))
        for hl in range(2):
            R_sl = slice(hl * 32, hl * 32 + 32)
            chain("sync", lambda e, hl=hl: e.dma_start(out=gw[:], in_=wltm_l[hl, :, :]))
            chain("sync", lambda e, hl=hl: e.dma_start(out=wTf[:], in_=wltm_T[hl, :, :]))
            chain("vector", lambda e, hl=hl: e.tensor_copy(out=wTb[hl][:], in_=wTf[:]))
            chain("vector", lambda e, hl=hl: e.tensor_scalar(
                out=gvn[:], in0=gw[:], scalar1=pct[:, 0 + hl:1 + hl], scalar2=pct[:, 2 + hl:3 + hl],
                op0=ALU.add, op1=ALU.subtract))
            chain("scalar", lambda e: e.activation(out=gt1[:], in_=gvn[:], func=AF.Abs))
            chain("scalar", lambda e: e.activation(out=gt1[:], in_=gt1[:], func=AF.Exp,
                                                   scale=np.float32(-1.0)))
            chain("scalar", lambda e: e.activation(out=gt2[:], in_=gt1[:], func=AF.Ln,
                                                   bias=np.float32(1.0)))
            chain("vector", lambda e: e.tensor_scalar_max(out=gt3[:], in0=gvn[:], scalar1=0.0))
            chain("vector", lambda e: e.tensor_add(out=gsp[:], in0=gt3[:], in1=gt2[:]))
            chain("vector", lambda e, hl=hl: e.tensor_scalar_mul(
                out=gch[:], in0=gsp[:], scalar1=pct[:, 4 + hl:5 + hl]))
            chain("vector", lambda e, hl=hl: e.reciprocal(out=pc1[:, 0:1], in_=pct[:, 8 + hl:9 + hl]))
            chain("vector", lambda e: e.tensor_scalar_mul(out=pc1[:, 1:2], in0=pc1[:, 0:1], scalar1=-1.0))
            chain("scalar", lambda e, R_sl=R_sl: e.activation(out=Rt[:, R_sl], in_=gch[:],
                                                              func=AF.Exp, scale=pc1[:, 1:2]))
            chain("scalar", lambda e: e.activation(out=gt2[:], in_=gvn[:], func=AF.Sigmoid))
            chain("vector", lambda e, hl=hl: e.scalar_tensor_tensor(
                out=gt3[:], in0=gsp[:], scalar=pct[:, 6 + hl:7 + hl], in1=gt2[:],
                op0=ALU.mult, op1=ALU.mult))
            chain("vector", lambda e, hl=hl: e.tensor_single_scalar(
                out=gt1[:], in_=gch[:], scalar=pct[:, 14 + hl:15 + hl], op=ALU.subtract))
            chain("scalar", lambda e, hl=hl: e.activation(out=gt1[:], in_=gt1[:], func=AF.Tanh,
                                                          scale=pct[:, 12 + hl:13 + hl]))
            chain("vector", lambda e, hl=hl, R_sl=R_sl: e.scalar_tensor_tensor(
                out=coef[:, R_sl], in0=gt1[:], scalar=pct[:, 10 + hl:11 + hl], in1=gt3[:],
                op0=ALU.mult, op1=ALU.mult))
            chain("vector", lambda e, hl=hl: e.memset(selc[hl][:], 0))
            for i in range(32):
                chain("vector", lambda e, hl=hl, i=i: e.tensor_copy(
                    out=selc[hl][0:64, i * 64 + 2 * i: i * 64 + 2 * i + 1],
                    in_=coef[0:64, hl * 32 + i: hl * 32 + i + 1]))
                chain("vector", lambda e, hl=hl, i=i: e.tensor_copy(
                    out=selc[hl][64:128, i * 64 + 2 * i + 1: i * 64 + 2 * i + 2],
                    in_=coef[64:128, hl * 32 + i: hl * 32 + i + 1]))
        GATES_DONE = gv.v

        vb = Counter(); am = Counter(); zz = Counter(); ym = Counter()
        ye = Counter(); yst = Counter(); ldb = Counter()
        gi = Counter()
        cg = Counter()

        LOOKAHEAD = 3

        def emit_vbf(g, hl, ch, i, ldv):
            tw = [(s_g, GATES_DONE), (s_ldb, 16 * ldv)]
            if g >= 4:
                tw.append((s_am, g - 3))
            def vbf(e, hl=hl, i=i, ch=ch, g=g):
                return e.matmul(
                    out=v_ps[g % 4][:, :],
                    lhsT=vbct[:, (hl * 32 + i) * 128:(hl * 32 + i + 1) * 128],
                    rhs=vs[:, ch * TSC:(ch + 1) * TSC],
                    start=True, stop=True)
            sch.add("tensor", vbf, waits=tw, incs=[(s_vb, 1)])
            vb.inc()

        for b in range(B):
            lw = []
            if b > 0:
                lw = [(s_am, am.v), (s_ym, ym.v), (s_vb, vb.v)]
            def ldf2(e, b=b):
                for hl in range(2):
                    rows = slice(hl * 64, hl * 64 + 64)
                    e.dma_start(out=kr[hl][0:64, :], in_=k_in[b, rows, :]).then_inc(s_ldb, 16)
                    e.dma_start(out=kr[hl][64:128, :], in_=k_in[b, rows, :]).then_inc(s_ldb, 16)
                    e.dma_start(out=qr[hl][0:64, :], in_=q_in[b, rows, :]).then_inc(s_ldb, 16)
                    e.dma_start(out=qr[hl][64:128, :], in_=q_in[b, rows, :]).then_inc(s_ldb, 16)
                e.dma_start(out=vs[:, :], in_=v_in[b, :, :]).then_inc(s_ldb, 16)
                return None
            sch.add("sync", ldf2, waits=lw)
            ldb.inc(9)
            ldv = ldb.v

            jobs = []
            for hl in range(2):
                for ch in range(NCH):
                    for i in range(32):
                        jobs.append((hl, ch, i))
            NJ = len(jobs)
            gi0 = gi.v
            cg0 = cg.v

            for hl in range(2):
                sch.add("vector", (lambda e, hl=hl: e.memset(state[hl][:], 0)),
                        waits=[(s_g, GATES_DONE)])
            for p in range(LOOKAHEAD):
                emit_vbf(gi0 + p, *jobs[p], ldv)

            for jx, (hl, ch, i) in enumerate(jobs):
                g = gi0 + jx
                ccols = slice(ch * TSC, (ch + 1) * TSC)
                sch.add("vector",
                        (lambda e, hl=hl, ccols=ccols, g=g: e.tensor_mul(
                            out=a_sb[:, 0:TSC], in0=kr[hl][:, ccols], in1=v_ps[g % 4][:, :])),
                        waits=[(s_vb, g + 1)], incs=[(s_am, 1)])
                am.inc()
                def scf(e, hl=hl, i=i):
                    return e.tensor_tensor_scan(
                        out=g_sb[:, 0:TSC],
                        data0=Rt[:, hl * 32 + i: hl * 32 + i + 1].broadcast_to([128, TSC]),
                        data1=a_sb[:, 0:TSC], initial=state[hl][:, i:i + 1],
                        op0=ALU.mult, op1=ALU.add)
                sch.add("vector", scf)
                sch.add("vector", (lambda e, hl=hl, i=i: e.tensor_copy(
                    out=state[hl][:, i:i + 1], in_=g_sb[:, TSC - 1:TSC])))
                zw = []
                if g >= 2:
                    zw.append((s_ym, g - 1))
                sch.add("vector",
                        (lambda e, hl=hl, ccols=ccols, g=g: e.tensor_mul(
                            out=z_sb[:, (g % 2) * TSC:(g % 2 + 1) * TSC],
                            in0=g_sb[:, 0:TSC], in1=qr[hl][:, ccols])),
                        waits=zw, incs=[(s_z, 1)])
                zz.inc()
                cgi = cg0 + (jx // 32)
                def ymf(e, hl=hl, i=i, ch=ch, g=g, cgi=cgi):
                    if i == 0:
                        e.matmul(out=y_ps[cgi % 2][hl * 64:hl * 64 + 64, :],
                                 lhsT=wTb[hl][:, :],
                                 rhs=qr[hl][0:64, ch * TSC:(ch + 1) * TSC],
                                 start=True, stop=False)
                    return e.matmul(
                        out=y_ps[cgi % 2][hl * 64:hl * 64 + 64, :],
                        lhsT=selc[hl][:, i * 64:(i + 1) * 64],
                        rhs=z_sb[:, (g % 2) * TSC:(g % 2 + 1) * TSC],
                        start=False, stop=(i == 31))
                yw = [(s_z, zz.v)]
                if i == 0 and cgi >= 2:
                    yw.append((s_ye, cgi - 1))
                sch.add("tensor", ymf, waits=yw, incs=[(s_ym, 1)])
                ym.inc()
                if jx + LOOKAHEAD < NJ:
                    emit_vbf(g + LOOKAHEAD, *jobs[jx + LOOKAHEAD], ldv)
                gi.inc()
                if i == 31:
                    ew = [(s_ym, ym.v)]
                    if yst.v:
                        ew.append((s_yst, 16 * yst.v))
                    sch.add("vector",
                            (lambda e, hl=hl, ccols=ccols, cgi=cgi: e.tensor_copy(
                                out=ysb[hl * 64:hl * 64 + 64, ccols],
                                in_=y_ps[cgi % 2][hl * 64:hl * 64 + 64, :])),
                            waits=ew, incs=[(s_ye, 1)])
                    ye.inc()
                    cg.inc()
            sch.add("sync", (lambda e, b=b: e.dma_start(out=y_o[b, :, :], in_=ysb[:, :])),
                    waits=[(s_ye, ye.v)], incs=[(s_yst, 16)])
            yst.inc()

        sch.add("sync", lambda e: None, waits=[(s_yst, 16 * yst.v)])
        sch.run()
    return nc


def _prep_scan(inputs, qkv_hs):
    WL = inputs["L0_attn_W_LTM"].astype(f32)
    pv = [inputs["L0_attn_V_gs"], inputs["L0_attn_V_T0"], inputs["L0_attn_beta_tau"],
          inputs["L0_attn_beta_gm"], inputs["L0_attn_C_ch"], inputs["L0_attn_gamma"]]
    alpha = float(inputs["L0_attn_alpha_ppd"][0]); ic = float(inputs["L0_attn_IC_threshold"][0])
    vbc = np.zeros((64, 128, 128), dtype=BF)
    for hl in range(2):
        for i in range(32):
            for dlt in range(2):
                p = hl * 64 + 2 * i + dlt
                for ee in range(64):
                    vbc[hl * 32 + i, p, dlt * 64 + ee] = 1
    maps = []
    for c in range(NC):
        h0 = 2 * c
        wl = np.empty((2, 128, 32), f32)
        wt = np.empty((2, 64, 64), f32)
        for hl in range(2):
            Wh = WL[h0 + hl]
            wt[hl] = Wh.T
            for dlt in range(2):
                wl[hl, dlt * 64:(dlt + 1) * 64, :] = Wh[dlt::2, :].T
        pcol = np.zeros((128, 16), f32)
        for r in range(6):
            for hl in range(2):
                pcol[:, 2 * r + hl] = pv[r][h0 + hl]
        pcol[:, 12] = alpha; pcol[:, 13] = alpha
        pcol[:, 14] = ic; pcol[:, 15] = ic
        maps.append({
            "q_in": np.ascontiguousarray(qkv_hs[:, c]),
            "k_in": np.ascontiguousarray(qkv_hs[:, 8 + c]),
            "v_in": np.ascontiguousarray(qkv_hs[:, 16 + c]),
            "wltm_l": wl, "wltm_T": wt, "pcol": pcol, "vbc": vbc,
        })
    return maps


def run_scan(inputs, qkv_hs):
    nc = build_scan()
    maps = _prep_scan(inputs, qkv_hs)
    res = run_bass_kernel_spmd(nc, maps, core_ids=list(range(NC)))
    return np.stack([r["y_o"] for r in res.results], axis=0)


def build_postproj(tag):
    nc = bass.Bass("TRN2", target_bir_lowering=False, debug=False)
    yo = nc.dram_tensor("yo", [NKC, B, 128, SL], BF16, kind="ExternalInput").ap()
    wo = nc.dram_tensor("wo", [NKC, NKC, 128, 128], F32, kind="ExternalInput").ap()
    xr = nc.dram_tensor("xr", [B, SL, D], F32, kind="ExternalInput").ap()
    identb = nc.dram_tensor("identb", [128, 128], BF16, kind="ExternalInput").ap()
    xout_t = nc.dram_tensor("xout_t", [B, SL, D], F32, kind="ExternalOutput").ap()
    xout_fm = nc.dram_tensor("xout_fm", [B, NKC, 128, SL], BF16, kind="ExternalOutput").ap()

    NST = 4
    es = contextlib.ExitStack()
    with es:
        wstg = es.enter_context(nc.sbuf_tensor("wstg", [128, NST * 128], F32))
        wbf = es.enter_context(nc.sbuf_tensor("wbf", [128, 64 * 128], BF16))
        idtb = es.enter_context(nc.sbuf_tensor("idtb", [128, 128], BF16))
        yosb = es.enter_context(nc.sbuf_tensor("yosb", [128, NKC * SL], BF16))
        xtk = es.enter_context(nc.sbuf_tensor("xtk", [128, NTT * D], F32))
        osb = es.enter_context(nc.sbuf_tensor("osb", [128, NKC * SL], BF16))
        xres = es.enter_context(nc.sbuf_tensor("xres", [128, NTT * D], F32))
        xsq = es.enter_context(nc.sbuf_tensor("xsq", [128, D], F32))
        xnb = es.enter_context(nc.sbuf_tensor("xnb", [128, NTT * D], BF16))
        fmout = es.enter_context(nc.sbuf_tensor("fmout", [128, NKC * SL], BF16))
        scol = es.enter_context(nc.sbuf_tensor("scol", [128, 16], F32))
        o_ps = es.enter_context(nc.psum_tensor("o_ps", [128, 1024], F32))
        pstb = [es.enter_context(nc.psum_tensor(f"pstb{j}", [128, 512], BF16)) for j in range(2)]
        pstc = [es.enter_context(nc.psum_tensor(f"pstc{j}", [128, 512], BF16)) for j in range(2)]
        s_wslot = [es.enter_context(nc.semaphore(f"s_wsl{j}")) for j in range(NST)]
        s_cast = es.enter_context(nc.semaphore("s_cast"))
        s_ldy = es.enter_context(nc.semaphore("s_ldy"))
        s_ldx = es.enter_context(nc.semaphore("s_ldx"))
        s_mm = es.enter_context(nc.semaphore("s_mm"))
        s_oe = es.enter_context(nc.semaphore("s_oe"))
        s_tp = es.enter_context(nc.semaphore("s_tp"))
        s_ra = es.enter_context(nc.semaphore("s_ra"))
        s_sq = es.enter_context(nc.semaphore("s_sq"))
        s_nrm = es.enter_context(nc.semaphore("s_nrm"))
        s_tb = es.enter_context(nc.semaphore("s_tb"))
        s_fme = es.enter_context(nc.semaphore("s_fme"))
        s_st = es.enter_context(nc.semaphore("s_st"))

        sch = Sched(nc)
        sch._nopt = wstg
        NW = NKC * NKC

        sch.add("sync", lambda e: e.dma_start(out=idtb[:], in_=identb[:, :]), incs=[(s_ldx, 16)])
        for i in range(NW):
            waits = [(s_cast, i - NST + 1)] if i >= NST else []
            sch.add("sync",
                    (lambda e, i=i: e.dma_start(out=wstg[:, (i % NST) * 128:(i % NST + 1) * 128],
                                                in_=wo[i // NKC, i % NKC, :, :])),
                    waits=waits, incs=[(s_wslot[i % NST], 16)])
            sch.add("vector",
                    (lambda e, i=i: e.tensor_copy(out=wbf[:, i * 128:(i + 1) * 128],
                                                  in_=wstg[:, (i % NST) * 128:(i % NST + 1) * 128])),
                    waits=[(s_wslot[i % NST], 16 * (i // NST + 1))], incs=[(s_cast, 1)])

        ldy = Counter(); ldx = Counter(); mm = Counter(); oe = Counter()
        tp = Counter(); ra = Counter(); sq = Counter(); nrm = Counter()
        tb = Counter(); fme = Counter(); st = Counter()

        for b in range(B):
            lw = [(s_mm, mm.v)] if b else []
            def ldyf(e, b=b):
                for kc in range(NKC):
                    e.dma_start(out=yosb[:, kc * SL:(kc + 1) * SL],
                                in_=yo[kc, b, :, :]).then_inc(s_ldy, 16)
                return None
            sch.add("sync", ldyf, waits=lw)
            ldy.inc(NKC)
            lw2 = [(s_ra, ra.v)] if b else []
            def ldxf(e, b=b):
                for t in range(NTT):
                    e.dma_start(out=xtk[:, t * D:(t + 1) * D],
                                in_=xr[b, t * 128:(t + 1) * 128, :]).then_inc(s_ldx, 16)
                return None
            sch.add("sync", ldxf, waits=lw2)
            ldx.inc(NTT)

            for mc in range(NKC):
                for blk in range(2):
                    m = (b * NKC + mc) * 2 + blk
                    w = [(s_ldy, 16 * ldy.v), (s_cast, NW)]
                    if m >= 2:
                        w.append((s_oe, m - 1))
                    def mmf(e, mc=mc, blk=blk, m=m):
                        last = None
                        for kc in range(NKC):
                            last = e.matmul(
                                out=o_ps[:, (m % 2) * 512:(m % 2) * 512 + 512],
                                lhsT=wbf[:, (kc * NKC + mc) * 128:(kc * NKC + mc + 1) * 128],
                                rhs=yosb[:, kc * SL + blk * 512: kc * SL + blk * 512 + 512],
                                start=(kc == 0), stop=(kc == NKC - 1))
                        return last
                    sch.add("tensor", mmf, waits=w, incs=[(s_mm, 1)])
                    mm.inc()
                    sch.add("vector",
                            (lambda e, mc=mc, blk=blk, m=m: e.tensor_copy(
                                out=osb[:, mc * SL + blk * 512: mc * SL + blk * 512 + 512],
                                in_=o_ps[:, (m % 2) * 512:(m % 2) * 512 + 512])),
                            waits=[(s_mm, mm.v)], incs=[(s_oe, 1)])
                    oe.inc()

            for t in range(NTT if PP_STOP != 'mm' else 0):
                for mc in range(NKC):
                    j = (b * NTT + t) * NKC + mc
                    w = [(s_oe, oe.v), (s_ldx, 16 * ldx.v), (s_ldx, 16)]
                    if j >= 2:
                        w.append((s_ra, j - 1))
                    sch.add("tensor",
                            (lambda e, t=t, mc=mc, j=j: e.transpose(
                                out=pstb[j % 2][:, 0:128],
                                in_=osb[:, mc * SL + t * 128: mc * SL + (t + 1) * 128],
                                identity=idtb[:])),
                            waits=w, incs=[(s_tp, 1)])
                    tp.inc()
                    sch.add("vector",
                            (lambda e, t=t, mc=mc, j=j: e.scalar_tensor_tensor(
                                out=xres[:, t * D + mc * 128: t * D + (mc + 1) * 128],
                                in0=xtk[:, t * D + mc * 128: t * D + (mc + 1) * 128],
                                scalar=1.0, in1=pstb[j % 2][:, 0:128],
                                op0=ALU.mult, op1=ALU.add)),
                            waits=[(s_tp, tp.v)], incs=[(s_ra, 1)])
                    ra.inc()
                if PP_STOP == 'ra':
                    continue
                def L1(e, t=t):
                    return e.reduce_sum(out=scol[:, 0:1], in_=xres[:, t * D:(t + 1) * D],
                                        axis=mybir.AxisListType.X)
                sch.add("vector", L1, waits=[(s_ra, ra.v)], incs=[(s_nrm, 1)])
                nrm.inc()
                def L2(e, t=t):
                    e.tensor_mul(out=xsq[:, :], in0=xres[:, t * D:(t + 1) * D],
                                 in1=xres[:, t * D:(t + 1) * D])
                    return e.reduce_sum(out=scol[:, 1:2], in_=xsq[:, :],
                                        axis=mybir.AxisListType.X)
                sch.add("vector", L2, waits=[(s_nrm, nrm.v)], incs=[(s_nrm, 1)])
                nrm.inc()
                sch.add("vector", (lambda e: e.tensor_scalar_mul(
                    out=scol[:, 2:3], in0=scol[:, 0:1], scalar1=1.0 / D)),
                    waits=[(s_nrm, nrm.v)], incs=[(s_nrm, 1)])
                nrm.inc()
                sch.add("vector", (lambda e: e.tensor_scalar_mul(
                    out=scol[:, 3:4], in0=scol[:, 1:2], scalar1=1.0 / D)),
                    waits=[(s_nrm, nrm.v)], incs=[(s_nrm, 1)])
                nrm.inc()
                sch.add("vector", (lambda e: e.tensor_mul(
                    out=scol[:, 4:5], in0=scol[:, 2:3], in1=scol[:, 2:3])),
                    waits=[(s_nrm, nrm.v)], incs=[(s_nrm, 1)])
                nrm.inc()
                sch.add("vector", (lambda e: e.tensor_sub(
                    out=scol[:, 5:6], in0=scol[:, 3:4], in1=scol[:, 4:5])),
                    waits=[(s_nrm, nrm.v)], incs=[(s_nrm, 1)])
                nrm.inc()
                sch.add("scalar", (lambda e: e.activation(
                    out=scol[:, 6:7], in_=scol[:, 5:6], func=AF.Ln, bias=np.float32(LN_EPS))),
                    waits=[(s_nrm, nrm.v)], incs=[(s_nrm, 1)])
                nrm.inc()
                sch.add("scalar", (lambda e: e.activation(
                    out=scol[:, 7:8], in_=scol[:, 6:7], func=AF.Exp, scale=np.float32(-0.5))),
                    waits=[(s_nrm, nrm.v)], incs=[(s_nrm, 1)])
                nrm.inc()
                def nzf(e, t=t):
                    e.tensor_scalar(out=xres[:, t * D:(t + 1) * D], in0=xres[:, t * D:(t + 1) * D],
                                    scalar1=scol[:, 2:3], scalar2=scol[:, 7:8],
                                    op0=ALU.subtract, op1=ALU.mult)
                    return e.tensor_copy(out=xnb[:, t * D:(t + 1) * D], in_=xres[:, t * D:(t + 1) * D])
                sch.add("vector", nzf, waits=[(s_nrm, nrm.v)], incs=[(s_nrm, 1)])
                nrm.inc()
            if PP_STOP not in ('mm',):
                sch.add("sync",
                        (lambda e, b=b: e.dma_start(
                            out=xout_t[b].rearrange("(t p) d -> p t d", p=128),
                            in_=xres[:].rearrange("p (t d) -> p t d", d=D))),
                        waits=[(s_nrm, nrm.v), (s_ra, ra.v)], incs=[(s_st, 16)])
                st.inc()
            for t in range(NTT if PP_STOP == 'full' else 0):
                for mc in range(NKC):
                    j2 = (b * NTT + t) * NKC + mc
                    w = [(s_nrm, nrm.v), (s_ldx, 16)]
                    if j2 >= 2:
                        w.append((s_fme, j2 - 1))
                    sch.add("tensor",
                            (lambda e, t=t, mc=mc, j2=j2: e.transpose(
                                out=pstc[j2 % 2][:, 0:128],
                                in_=xnb[:, t * D + mc * 128: t * D + (mc + 1) * 128],
                                identity=idtb[:])),
                            waits=w, incs=[(s_tb, 1)])
                    tb.inc()
                    sch.add("vector",
                            (lambda e, t=t, mc=mc, j2=j2: e.tensor_copy(
                                out=fmout[:, mc * SL + t * 128: mc * SL + (t + 1) * 128],
                                in_=pstc[j2 % 2][:, 0:128])),
                            waits=[(s_tb, tb.v)], incs=[(s_fme, 1)])
                    fme.inc()
            def stf2(e, b=b):
                src = fmout if PP_STOP == 'full' else osb
                for kc in range(NKC):
                    e.dma_start(out=xout_fm[b, kc, :, :],
                                in_=src[:, kc * SL:(kc + 1) * SL]).then_inc(s_st, 16)
                return None
            sch.add("sync", stf2, waits=([(s_fme, fme.v)] if PP_STOP == 'full' else [(s_oe, oe.v)]))
            st.inc(NKC)
            sch.add("vector", lambda e: None, waits=[(s_st, 16 * st.v)])

        sch.add("sync", lambda e: None, waits=[(s_st, 16 * st.v)])
        sch.run()
    return nc


def _ident_consts():
    return {
        "identf": np.eye(128, dtype=f32),
        "identb": np.eye(128, dtype=BF),
        "onesc": np.ones((128, 1), f32),
        "ones1": np.ones((1, 128), f32),
    }


def run_postproj(tag, w, y_hs, xr_tok, transpose_w):
    key = "pp_" + tag
    if key not in _cache:
        _cache[key] = build_postproj(tag)
    nc = _cache[key]
    wch = _wchunks_T(w) if transpose_w else _wchunks(w)
    ident = np.eye(128, dtype=BF)
    maps = []
    for c in range(NC):
        sl = slice(c * SL, (c + 1) * SL)
        maps.append({"yo": np.ascontiguousarray(y_hs[:, :, :, sl]), "wo": wch,
                     "identb": ident, "xr": np.ascontiguousarray(xr_tok[:, sl, :])})
    res = run_bass_kernel_spmd(nc, maps, core_ids=list(range(NC)))
    xt = np.concatenate([r["xout_t"] for r in res.results], axis=1)
    xfm = np.concatenate([r["xout_fm"] for r in res.results], axis=3)
    return xt, xfm


EC = 2048
FF = 4096


def build_moe():
    nc = bass.Bass("TRN2", target_bir_lowering=False, debug=False)
    xall = nc.dram_tensor("xall", [S, D], F32, kind="ExternalInput").ap()
    idxc = nc.dram_tensor("idxc", [128, 16], I32, kind="ExternalInput").ap()
    w1 = nc.dram_tensor("w1", [NKC, 32, 128, 128], F32, kind="ExternalInput").ap()
    w2 = nc.dram_tensor("w2", [32, NKC, 128, 128], F32, kind="ExternalInput").ap()
    b1 = nc.dram_tensor("b1", [128, 32], F32, kind="ExternalInput").ap()
    b2 = nc.dram_tensor("b2", [128, 8], F32, kind="ExternalInput").ap()
    cwe = nc.dram_tensor("cwe", [S, 1], F32, kind="ExternalInput").ap()
    identb = nc.dram_tensor("identb", [128, 128], BF16, kind="ExternalInput").ap()
    hbounce = nc.dram_tensor("hbounce", [32, 128, EC], BF16, kind="Internal")
    moe_o = nc.dram_tensor("moe_o", [EC, D], BF16, kind="ExternalOutput").ap()
    hb_ap = hbounce.ap()

    NST = 4
    es = contextlib.ExitStack()
    with es:
        wstg = es.enter_context(nc.sbuf_tensor("wstg", [128, NST * 128], F32))
        w1b = es.enter_context(nc.sbuf_tensor("w1b", [128, 256 * 128], BF16))
        idxt = es.enter_context(nc.sbuf_tensor("idxt", [128, 16], I32))
        idt = es.enter_context(nc.sbuf_tensor("idt", [128, 128], BF16))
        b1t = es.enter_context(nc.sbuf_tensor("b1t", [128, 32], F32))
        b2t = es.enter_context(nc.sbuf_tensor("b2t", [128, 8], F32))
        wct = es.enter_context(nc.sbuf_tensor("wct", [128, 16], F32))
        xg32 = es.enter_context(nc.sbuf_tensor("xg32", [128, 2 * D], F32))
        xgb = es.enter_context(nc.sbuf_tensor("xgb", [128, 2 * D], BF16))
        xgfm = es.enter_context(nc.sbuf_tensor("xgfm", [128, NKC * EC], BF16))
        hev = es.enter_context(nc.sbuf_tensor("hev", [128, 6 * 512], F32))
        hbt = es.enter_context(nc.sbuf_tensor("hbt", [128, EC], BF16))
        h2 = es.enter_context(nc.sbuf_tensor("h2", [128, 32 * 512], BF16))
        osb = es.enter_context(nc.sbuf_tensor("osb", [128, NKC * 512], BF16))
        otm = es.enter_context(nc.sbuf_tensor("otm", [128, 2 * D], BF16))
        ps_m = es.enter_context(nc.psum_tensor("ps_m", [128, 1024], F32))
        ps_t0 = es.enter_context(nc.psum_tensor("ps_t0", [128, 512], BF16))
        ps_t1 = es.enter_context(nc.psum_tensor("ps_t1", [128, 512], BF16))
        s_wslot = [es.enter_context(nc.semaphore(f"s_wsl{j}")) for j in range(NST)]
        s_cast = es.enter_context(nc.semaphore("s_cast"))
        s_ld = es.enter_context(nc.semaphore("s_ld"))
        s_gat = es.enter_context(nc.semaphore("s_gat"))
        s_gc = es.enter_context(nc.semaphore("s_gc"))
        s_gtp = es.enter_context(nc.semaphore("s_gtp"))
        s_gte = es.enter_context(nc.semaphore("s_gte"))
        s_mm = es.enter_context(nc.semaphore("s_mm"))
        s_ge = es.enter_context(nc.semaphore("s_ge"))
        s_hst = es.enter_context(nc.semaphore("s_hst"))
        s_h2 = es.enter_context(nc.semaphore("s_h2"))
        s_mm2 = es.enter_context(nc.semaphore("s_mm2"))
        s_oe = es.enter_context(nc.semaphore("s_oe"))
        s_otp = es.enter_context(nc.semaphore("s_otp"))
        s_ote = es.enter_context(nc.semaphore("s_ote"))
        s_st = es.enter_context(nc.semaphore("s_st"))

        sch = Sched(nc)
        sch._nopt = wstg

        sch.add("sync", lambda e: e.dma_start(out=idt[:], in_=identb[:, :]), incs=[(s_ld, 16)])
        sch.add("sync", lambda e: e.dma_start(out=idxt[:], in_=idxc[:, :]), incs=[(s_ld, 16)])
        sch.add("sync", lambda e: e.dma_start(out=b1t[:], in_=b1[:, :]), incs=[(s_ld, 16)])
        sch.add("sync", lambda e: e.dma_start(out=b2t[:], in_=b2[:, :]), incs=[(s_ld, 16)])
        NW1 = 256
        for i in range(NW1):
            waits = [(s_cast, i - NST + 1)] if i >= NST else []
            sch.add("sync",
                    (lambda e, i=i: e.dma_start(out=wstg[:, (i % NST) * 128:(i % NST + 1) * 128],
                                                in_=w1[i // 32, i % 32, :, :])),
                    waits=waits, incs=[(s_wslot[i % NST], 16)])
            sch.add("vector",
                    (lambda e, i=i: e.tensor_copy(out=w1b[:, i * 128:(i + 1) * 128],
                                                  in_=wstg[:, (i % NST) * 128:(i % NST + 1) * 128])),
                    waits=[(s_wslot[i % NST], 16 * (i // NST + 1))], incs=[(s_cast, 1)])

        gat = Counter(); gc = Counter(); gtp = Counter(); gte = Counter()
        mm = Counter(); ge = Counter(); hst = Counter(); h2c = Counter()
        mm2 = Counter(); oe = Counter(); otp = Counter(); ote = Counter(); st = Counter()

        for g in range(16):
            w = [(s_ld, 64)]
            if g >= 2:
                w.append((s_gc, 2 * (g - 1)))
            def gf(e, g=g):
                e.indirect_dma_start(
                    out=xg32[:, (g % 2) * D:(g % 2 + 1) * D], out_offset=None,
                    in_=xall[0:128, :], in_offset=IndirectOffsetOnAxis(idxt[:, g:g + 1], 0)
                ).then_inc(s_gat, 16)
                e.indirect_dma_start(
                    out=wct[:, g:g + 1], out_offset=None,
                    in_=cwe[0:128, :], in_offset=IndirectOffsetOnAxis(idxt[:, g:g + 1], 0)
                ).then_inc(s_gat, 16)
                return None
            sch.add("gpsimd", gf, waits=w)
            gat.inc()
            sch.add("vector",
                    (lambda e, g=g: e.tensor_copy(out=xgb[:, (g % 2) * D:(g % 2 + 1) * D],
                                                  in_=xg32[:, (g % 2) * D:(g % 2 + 1) * D])),
                    waits=[(s_gat, 32 * gat.v)], incs=[(s_gc, 1)])
            gc.inc()
            for kc in range(NKC):
                j = g * NKC + kc
                tw = [(s_gc, gc.v)]
                if j >= 2:
                    tw.append((s_gte, j - 1))
                sch.add("tensor",
                        (lambda e, g=g, kc=kc, j=j: e.transpose(
                            out=(ps_t0 if j % 2 == 0 else ps_t1)[:, 0:128],
                            in_=xgb[:, (g % 2) * D + kc * 128:(g % 2) * D + (kc + 1) * 128],
                            identity=idt[:])),
                        waits=tw, incs=[(s_gtp, 1)])
                gtp.inc()
                sch.add("vector",
                        (lambda e, g=g, kc=kc, j=j: e.tensor_copy(
                            out=xgfm[:, kc * EC + g * 128: kc * EC + (g + 1) * 128],
                            in_=(ps_t0 if j % 2 == 0 else ps_t1)[:, 0:128])),
                        waits=[(s_gtp, gtp.v)], incs=[(s_gte, 1)])
                gte.inc()
            sch.add("vector", (lambda e: None), waits=[(s_gte, gte.v)])
            sch.add("vector", (lambda e: e.memset(hev[0:1, 0:1], 0)), incs=[(s_gc, 1)])
            gc.inc()

        for mc in range(32):
            for blk in range(EC // 512):
                m = mc * 4 + blk
                w = [(s_gte, gte.v), (s_cast, NW1)]
                if m >= 1:
                    w.append((s_ge, 6 * m))
                def mmf(e, mc=mc, blk=blk, m=m):
                    last = None
                    for kc in range(NKC):
                        last = e.matmul(
                            out=ps_m[:, (m % 2) * 512:(m % 2) * 512 + 512],
                            lhsT=w1b[:, (kc * 32 + mc) * 128:(kc * 32 + mc + 1) * 128],
                            rhs=xgfm[:, kc * EC + blk * 512: kc * EC + (blk + 1) * 512],
                            start=(kc == 0), stop=(kc == NKC - 1))
                    return last
                sch.add("tensor", mmf, waits=w, incs=[(s_mm, 1)])
                mm.inc()
                def g1(e, mc=mc, m=m):
                    return e.tensor_scalar_add(out=hev[:, 0:512], in0=ps_m[:, (m % 2) * 512:(m % 2) * 512 + 512],
                                               scalar1=b1t[:, mc:mc + 1])
                sch.add("vector", g1, waits=[(s_mm, mm.v)], incs=[(s_ge, 1)])
                def g2(e):
                    return e.tensor_mul(out=hev[:, 512:1024], in0=hev[:, 0:512], in1=hev[:, 0:512])
                sch.add("vector", g2, waits=[(s_ge, ge.v + 1)], incs=[(s_ge, 1)])
                def g3(e):
                    return e.tensor_scalar(out=hev[:, 1024:1536], in0=hev[:, 512:1024],
                                           scalar1=0.044715, scalar2=1.0, op0=ALU.mult, op1=ALU.add)
                sch.add("vector", g3, waits=[(s_ge, ge.v + 2)], incs=[(s_ge, 1)])
                def g4(e):
                    return e.tensor_mul(out=hev[:, 1536:2048], in0=hev[:, 1024:1536], in1=hev[:, 0:512])
                sch.add("vector", g4, waits=[(s_ge, ge.v + 3)], incs=[(s_ge, 1)])
                def g5(e):
                    return e.activation(out=hev[:, 2048:2560], in_=hev[:, 1536:2048],
                                        func=AF.Tanh, scale=np.float32(0.7978845608028654))
                sch.add("scalar", g5, waits=[(s_ge, ge.v + 4)], incs=[(s_ge, 1)])
                def g6(e, blk=blk):
                    e.tensor_scalar(out=hev[:, 2560:3072], in0=hev[:, 2048:2560],
                                    scalar1=1.0, scalar2=0.5, op0=ALU.add, op1=ALU.mult)
                    return e.tensor_mul(out=hbt[:, blk * 512:(blk + 1) * 512],
                                        in0=hev[:, 2560:3072], in1=hev[:, 0:512])
                sch.add("vector", g6, waits=[(s_ge, ge.v + 5)], incs=[(s_ge, 1)])
                ge.inc(6)
            sch.add("sync",
                    (lambda e, mc=mc: e.dma_start(out=hb_ap[mc, :, :], in_=hbt[:, :])),
                    waits=[(s_ge, ge.v)], incs=[(s_hst, 16)])
            hst.inc()
            sch.add("vector", (lambda e: None), waits=[(s_hst, 16 * hst.v)])

        NW2 = 256
        for i in range(NW2):
            waits = [(s_mm, mm.v)] if i < NST else [(s_cast, NW1 + i - NST + 1)]
            sch.add("sync",
                    (lambda e, i=i: e.dma_start(out=wstg[:, (i % NST) * 128:(i % NST + 1) * 128],
                                                in_=w2[i // NKC, i % NKC, :, :])),
                    waits=waits, incs=[(s_wslot[i % NST], 16 * 1)])
            sch.add("vector",
                    (lambda e, i=i: e.tensor_copy(out=w1b[:, i * 128:(i + 1) * 128],
                                                  in_=wstg[:, (i % NST) * 128:(i % NST + 1) * 128])),
                    waits=[(s_wslot[i % NST], 16 * ((NW1 + i) // NST + 1))], incs=[(s_cast, 1)])

        for blk in range(EC // 512):
            def ldh(e, blk=blk):
                for kc2 in range(32):
                    e.dma_start(out=h2[:, kc2 * 512:(kc2 + 1) * 512],
                                in_=hb_ap[kc2, :, blk * 512:(blk + 1) * 512]).then_inc(s_h2, 16)
                return None
            sch.add("sync", ldh, waits=[(s_hst, 16 * hst.v), (s_mm2, mm2.v)])
            h2c.inc(32)
            for mc2 in range(NKC):
                m = blk * NKC + mc2
                w = [(s_h2, 16 * h2c.v), (s_cast, NW1 + NW2)]
                if m >= 2:
                    w.append((s_oe, m - 1))
                def mmf2(e, mc2=mc2, m=m):
                    last = None
                    for kc2 in range(32):
                        last = e.matmul(
                            out=ps_m[:, (m % 2) * 512:(m % 2) * 512 + 512],
                            lhsT=w1b[:, (kc2 * NKC + mc2) * 128:(kc2 * NKC + mc2 + 1) * 128],
                            rhs=h2[:, kc2 * 512:(kc2 + 1) * 512],
                            start=(kc2 == 0), stop=(kc2 == 31))
                    return last
                sch.add("tensor", mmf2, waits=w, incs=[(s_mm2, 1)])
                mm2.inc()
                def oef(e, mc2=mc2, m=m):
                    return e.tensor_scalar_add(out=osb[:, mc2 * 512:(mc2 + 1) * 512],
                                               in0=ps_m[:, (m % 2) * 512:(m % 2) * 512 + 512],
                                               scalar1=b2t[:, mc2:mc2 + 1])
                ow = [(s_mm2, mm2.v)]
                if blk >= 1:
                    ow.append((s_otp, 32 * blk))
                sch.add("vector", oef, waits=ow, incs=[(s_oe, 1)])
                oe.inc()
            for tg in range(4):
                for mc2 in range(NKC):
                    j = (blk * 4 + tg) * NKC + mc2
                    w = [(s_oe, oe.v), (s_ld, 64), (s_gte, 128), (s_gat, 16 * 32)]
                    if j >= 2:
                        w.append((s_ote, j - 1))
                    sch.add("tensor",
                            (lambda e, tg=tg, mc2=mc2, j=j: e.transpose(
                                out=(ps_t0 if j % 2 == 0 else ps_t1)[:, 0:128],
                                in_=osb[:, mc2 * 512 + tg * 128: mc2 * 512 + (tg + 1) * 128],
                                identity=idt[:])),
                            waits=w, incs=[(s_otp, 1)])
                    otp.inc()
                    ewx = [(s_otp, otp.v)]
                    if (blk * 4 + tg) >= 2:
                        ewx.append((s_st, 16 * (blk * 4 + tg - 1)))
                    sch.add("vector",
                            (lambda e, tg=tg, mc2=mc2, j=j, blk=blk: e.tensor_scalar_mul(
                                out=otm[:, (j % 16 // 8) * D + mc2 * 128:(j % 16 // 8) * D + (mc2 + 1) * 128],
                                in0=(ps_t0 if j % 2 == 0 else ps_t1)[:, 0:128],
                                scalar1=wct[:, blk * 4 + tg: blk * 4 + tg + 1])),
                            waits=ewx, incs=[(s_ote, 1)])
                    ote.inc()
                def stf(e, blk=blk, tg=tg):
                    j0 = (blk * 4 + tg) * NKC
                    half = (j0 % 16) // 8
                    return e.dma_start(out=moe_o[(blk * 4 + tg) * 128:(blk * 4 + tg + 1) * 128, :],
                                       in_=otm[:, half * D:(half + 1) * D])
                sch.add("sync", stf, waits=[(s_ote, ote.v)], incs=[(s_st, 16)])
                st.inc()

        sch.add("sync", lambda e: None, waits=[(s_st, 16 * st.v)])
        sch.run()
    return nc


def run_moe(inputs, x1_tok):
    if "moe" not in _cache:
        _cache["moe"] = build_moe()
    nc = _cache["moe"]
    W1 = inputs["L0_ffn_W1"].astype(f32)
    W2 = inputs["L0_ffn_W2"].astype(f32)
    b1 = inputs["L0_ffn_b1"].astype(f32)
    b2 = inputs["L0_ffn_b2"].astype(f32)
    cw = inputs["L0_ffn_combine_weights"].astype(f32)
    ti = inputs["L0_ffn_top_idx"].astype(np.int32)
    ident = np.eye(128, dtype=BF)
    xall = np.ascontiguousarray(x1_tok[0])
    maps = []
    for c in range(NC):
        maps.append({
            "xall": xall,
            "idxc": np.ascontiguousarray(ti[:, c].reshape(16, 128).T),
            "w1": _wchunks(W1[c]), "w2": _wchunks(W2[c]),
            "b1": np.ascontiguousarray(b1[c].reshape(32, 128).T),
            "b2": np.ascontiguousarray(b2[c].reshape(8, 128).T),
            "cwe": np.ascontiguousarray(cw[:, c:c + 1]),
            "identb": ident,
        })
    res = run_bass_kernel_spmd(nc, maps, core_ids=list(range(NC)))
    return np.stack([r["moe_o"] for r in res.results], axis=0)


def build_combine():
    nc = bass.Bass("TRN2", target_bir_lowering=False, debug=False)
    moe = nc.dram_tensor("moe", [8, EC, D], BF16, kind="ExternalInput").ap()
    tokq = nc.dram_tensor("tokq", [128, 128], I32, kind="ExternalInput").ap()
    xr = nc.dram_tensor("xr", [B, SL, D], F32, kind="ExternalInput").ap()
    iotac = nc.dram_tensor("iotac", [128, SL], F32, kind="ExternalInput").ap()
    xout_t = nc.dram_tensor("xout_t", [B, SL, D], F32, kind="ExternalOutput").ap()

    es = contextlib.ExitStack()
    with es:
        toki = es.enter_context(nc.sbuf_tensor("toki", [128, 128], I32))
        tokf = es.enter_context(nc.sbuf_tensor("tokf", [128, 128], F32))
        iot = es.enter_context(nc.sbuf_tensor("iot", [128, SL], F32))
        mtile = [es.enter_context(nc.sbuf_tensor(f"mt{j}", [128, SL], BF16)) for j in range(2)]
        moet = [es.enter_context(nc.sbuf_tensor(f"moet{j}", [128, D], BF16)) for j in range(2)]
        xtk = es.enter_context(nc.sbuf_tensor("xtk", [128, NTT * D], F32))
        xres = es.enter_context(nc.sbuf_tensor("xres", [128, NTT * D], F32))
        xsq = es.enter_context(nc.sbuf_tensor("xsq", [128, D], F32))
        scol = es.enter_context(nc.sbuf_tensor("scol", [128, 16], F32))
        pg = [es.enter_context(nc.psum_tensor(f"pg{j}", [128, 1024], F32)) for j in range(4)]
        s_ld = es.enter_context(nc.semaphore("s_ld"))
        s_ldx = es.enter_context(nc.semaphore("s_ldx"))
        s_mload = es.enter_context(nc.semaphore("s_mload"))
        s_mb = es.enter_context(nc.semaphore("s_mb"))
        s_mm = es.enter_context(nc.semaphore("s_mm"))
        s_ra = es.enter_context(nc.semaphore("s_ra"))
        s_nrm = es.enter_context(nc.semaphore("s_nrm"))
        s_st = es.enter_context(nc.semaphore("s_st"))

        sch = Sched(nc)
        sch._nopt = tokf

        sch.add("sync", lambda e: e.dma_start(out=toki[:], in_=tokq[:, :]), incs=[(s_ld, 16)])
        sch.add("sync", lambda e: e.dma_start(out=iot[:], in_=iotac[:, :]), incs=[(s_ld, 16)])
        sch.add("vector", (lambda e: e.tensor_copy(out=tokf[:], in_=toki[:])),
                waits=[(s_ld, 32)], incs=[(s_mb, 1)])
        mbv = Counter(); mbv.inc()

        ldx = Counter(); mload = Counter(); mm = Counter(); ra = Counter()
        nrm = Counter(); st = Counter()

        def ln_chain(tile_getter, store_fn, extra_w):
            pass

        for b in range(B):
            def ldxf(e, b=b):
                for t in range(NTT):
                    e.dma_start(out=xtk[:, t * D:(t + 1) * D],
                                in_=xr[b, t * 128:(t + 1) * 128, :]).then_inc(s_ldx, 16)
                return None
            sch.add("sync", ldxf, waits=([(s_nrm, nrm.v)] if b else []))
            ldx.inc(NTT)

            if b == 0:
                for grp in range(2):
                    for cc in range(128):
                        ci = grp * 128 + cc
                        w = []
                        if ci >= 2:
                            w.append((s_mm, 8 * (ci - 1)))
                        sch.add("sync",
                                (lambda e, cc=cc, ci=ci: e.dma_start(
                                    out=moet[ci % 2][:], in_=moe[cc // 16, (cc % 16) * 128:(cc % 16 + 1) * 128, :])),
                                waits=w, incs=[(s_mload, 16)])
                        mload.inc()
                        mw = [(s_ld, 32)]
                        if ci >= 2:
                            mw.append((s_mm, 8 * (ci - 1)))
                        sch.add("vector",
                                (lambda e, cc=cc, ci=ci: e.tensor_single_scalar(
                                    out=mtile[ci % 2][:], in_=iot[:],
                                    scalar=tokf[:, cc:cc + 1], op=ALU.is_equal)),
                                waits=mw, incs=[(s_mb, 1)])
                        mbv.inc()
                        def mmf(e, grp=grp, cc=cc, ci=ci):
                            last = None
                            for ntl in range(4):
                                nt = grp * 4 + ntl
                                for fh in range(2):
                                    last = e.matmul(
                                        out=pg[ntl][:, fh * 512:(fh + 1) * 512],
                                        lhsT=mtile[ci % 2][:, nt * 128:(nt + 1) * 128],
                                        rhs=moet[ci % 2][:, fh * 512:(fh + 1) * 512],
                                        start=(cc == 0), stop=(cc == 127))
                            return last
                        mmw = [(s_mload, 16 * mload.v), (s_mb, mbv.v)]
                        if grp == 1 and cc == 0:
                            mmw.append((s_ra, 4))
                        sch.add("tensor", mmf, waits=mmw, incs=[(s_mm, 8)])
                        mm.inc(8)
                    for ntl in range(4):
                        t = grp * 4 + ntl
                        sch.add("vector",
                                (lambda e, t=t, ntl=ntl: e.tensor_add(
                                    out=xres[:, t * D:(t + 1) * D],
                                    in0=xtk[:, t * D:(t + 1) * D], in1=pg[ntl][:])),
                                waits=[(s_mm, mm.v), (s_ldx, 16 * ldx.v)], incs=[(s_ra, 1)])
                        ra.inc()
                        _emit_ln(sch, nc, xres, xsq, scol, t, s_ra, ra.v, s_nrm, nrm, AF, ALU)
                        def stf(e, b=b, t=t):
                            return e.dma_start(
                                out=xout_t[b, t * 128:(t + 1) * 128, :],
                                in_=xres[:, t * D:(t + 1) * D])
                        sch.add("sync", stf, waits=[(s_nrm, nrm.v)], incs=[(s_st, 16)])
                        st.inc()
            else:
                for t in range(NTT):
                    sch.add("vector",
                            (lambda e, t=t: e.tensor_copy(
                                out=xres[:, t * D:(t + 1) * D], in_=xtk[:, t * D:(t + 1) * D])),
                            waits=[(s_ldx, 16 * ldx.v), (s_st, 16 * st.v)], incs=[(s_ra, 1)])
                    ra.inc()
                    _emit_ln(sch, nc, xres, xsq, scol, t, s_ra, ra.v, s_nrm, nrm, AF, ALU)
                    def stf(e, b=b, t=t):
                        return e.dma_start(
                            out=xout_t[b, t * 128:(t + 1) * 128, :],
                            in_=xres[:, t * D:(t + 1) * D])
                    sch.add("sync", stf, waits=[(s_nrm, nrm.v)], incs=[(s_st, 16)])
                    st.inc()

        sch.add("sync", lambda e: None, waits=[(s_st, 16 * st.v)])
        sch.run()
    return nc


def _emit_ln(sch, nc, xres, xsq, scol, t, s_prev, prev_v, s_nrm, nrm, AF, ALU):
    sch.add("vector", (lambda e, t=t: e.reduce_sum(
        out=scol[:, 0:1], in_=xres[:, t * D:(t + 1) * D], axis=mybir.AxisListType.X)),
        waits=[(s_prev, prev_v)], incs=[(s_nrm, 1)])
    nrm.inc()
    def L2(e, t=t):
        e.tensor_mul(out=xsq[:, :], in0=xres[:, t * D:(t + 1) * D],
                     in1=xres[:, t * D:(t + 1) * D])
        return e.reduce_sum(out=scol[:, 1:2], in_=xsq[:, :], axis=mybir.AxisListType.X)
    sch.add("vector", L2, waits=[(s_nrm, nrm.v)], incs=[(s_nrm, 1)])
    nrm.inc()
    sch.add("vector", (lambda e: e.tensor_scalar_mul(
        out=scol[:, 2:3], in0=scol[:, 0:1], scalar1=1.0 / D)),
        waits=[(s_nrm, nrm.v)], incs=[(s_nrm, 1)])
    nrm.inc()
    sch.add("vector", (lambda e: e.tensor_scalar_mul(
        out=scol[:, 3:4], in0=scol[:, 1:2], scalar1=1.0 / D)),
        waits=[(s_nrm, nrm.v)], incs=[(s_nrm, 1)])
    nrm.inc()
    sch.add("vector", (lambda e: e.tensor_mul(
        out=scol[:, 4:5], in0=scol[:, 2:3], in1=scol[:, 2:3])),
        waits=[(s_nrm, nrm.v)], incs=[(s_nrm, 1)])
    nrm.inc()
    sch.add("vector", (lambda e: e.tensor_sub(
        out=scol[:, 5:6], in0=scol[:, 3:4], in1=scol[:, 4:5])),
        waits=[(s_nrm, nrm.v)], incs=[(s_nrm, 1)])
    nrm.inc()
    sch.add("scalar", (lambda e: e.activation(
        out=scol[:, 6:7], in_=scol[:, 5:6], func=AF.Ln, bias=np.float32(LN_EPS))),
        waits=[(s_nrm, nrm.v)], incs=[(s_nrm, 1)])
    nrm.inc()
    sch.add("scalar", (lambda e: e.activation(
        out=scol[:, 7:8], in_=scol[:, 6:7], func=AF.Exp, scale=np.float32(-0.5))),
        waits=[(s_nrm, nrm.v)], incs=[(s_nrm, 1)])
    nrm.inc()
    sch.add("vector", (lambda e, t=t: e.tensor_scalar(
        out=xres[:, t * D:(t + 1) * D], in0=xres[:, t * D:(t + 1) * D],
        scalar1=scol[:, 2:3], scalar2=scol[:, 7:8],
        op0=ALU.subtract, op1=ALU.mult)),
        waits=[(s_nrm, nrm.v)], incs=[(s_nrm, 1)])
    nrm.inc()


def run_combine(inputs, moe_all, x1_tok):
    if "combine" not in _cache:
        _cache["combine"] = build_combine()
    nc = _cache["combine"]
    ti = inputs["L0_ffn_top_idx"].astype(np.int32)
    flat = np.ascontiguousarray(ti.T.reshape(-1))
    tokq = np.ascontiguousarray(flat.reshape(128, 128).T)
    maps = []
    for c in range(NC):
        sl = slice(c * SL, (c + 1) * SL)
        iota = np.broadcast_to(np.arange(c * SL, (c + 1) * SL, dtype=f32)[None, :],
                               (128, SL)).copy()
        maps.append({"moe": moe_all, "tokq": tokq,
                     "xr": np.ascontiguousarray(x1_tok[:, sl, :]), "iotac": iota})
    res = run_bass_kernel_spmd(nc, maps, core_ids=list(range(NC)))
    return np.concatenate([r["xout_t"] for r in res.results], axis=1)


def build_qkv1():
    nc = bass.Bass("TRN2", target_bir_lowering=False, debug=False)
    xr = nc.dram_tensor("xr", [B, SL, D], F32, kind="ExternalInput").ap()
    wqkv = nc.dram_tensor("wqkv", [3, NKC, NKC, 128, 128], F32, kind="ExternalInput").ap()
    identb = nc.dram_tensor("identb", [128, 128], BF16, kind="ExternalInput").ap()
    qk_o = nc.dram_tensor("qk_o", [2, B, NKC, 128, SL], BF16, kind="ExternalOutput").ap()
    v_o = nc.dram_tensor("v_o", [B, SL, D], BF16, kind="ExternalOutput").ap()

    NST = 4
    NW = 3 * NKC * NKC
    es = contextlib.ExitStack()
    with es:
        wstg = es.enter_context(nc.sbuf_tensor("wstg", [128, NST * 128], F32))
        wb = es.enter_context(nc.sbuf_tensor("wb", [128, NW * 128], BF16))
        idt = es.enter_context(nc.sbuf_tensor("idt", [128, 128], BF16))
        xtk = es.enter_context(nc.sbuf_tensor("xtk", [128, 2 * D], F32))
        xtb = es.enter_context(nc.sbuf_tensor("xtb", [128, 2 * D], BF16))
        xfm = es.enter_context(nc.sbuf_tensor("xfm", [128, NKC * SL], BF16))
        qksb = es.enter_context(nc.sbuf_tensor("qksb", [128, 4 * 512], BF16))
        vsb = es.enter_context(nc.sbuf_tensor("vsb", [128, 2 * D], BF16))
        pst0 = es.enter_context(nc.psum_tensor("pst0", [128, 512], BF16))
        pst1 = es.enter_context(nc.psum_tensor("pst1", [128, 512], BF16))
        ps_qk = es.enter_context(nc.psum_tensor("ps_qk", [128, 1024], F32))
        ps_v = es.enter_context(nc.psum_tensor("ps_v", [128, 1024], F32))
        s_wslot = [es.enter_context(nc.semaphore(f"s_wsl{j}")) for j in range(NST)]
        s_cast = es.enter_context(nc.semaphore("s_cast"))
        s_ld = es.enter_context(nc.semaphore("s_ld"))
        s_xslot = [es.enter_context(nc.semaphore(f"s_xsl{j}")) for j in range(2)]
        s_xc = es.enter_context(nc.semaphore("s_xc"))
        s_tp = es.enter_context(nc.semaphore("s_tp"))
        s_tpe = es.enter_context(nc.semaphore("s_tpe"))
        s_mm = es.enter_context(nc.semaphore("s_mm"))
        s_qe = es.enter_context(nc.semaphore("s_qe"))
        s_mv = es.enter_context(nc.semaphore("s_mv"))
        s_ve = es.enter_context(nc.semaphore("s_ve"))
        s_st = es.enter_context(nc.semaphore("s_st"))

        sch = Sched(nc)
        sch._nopt = wstg

        sch.add("sync", lambda e: e.dma_start(out=idt[:], in_=identb[:, :]), incs=[(s_ld, 16)])
        for i in range(NW):
            waits = [(s_cast, i - NST + 1)] if i >= NST else []
            sch.add("sync",
                    (lambda e, i=i: e.dma_start(out=wstg[:, (i % NST) * 128:(i % NST + 1) * 128],
                                                in_=wqkv[i // 64, (i % 64) // NKC, i % NKC, :, :])),
                    waits=waits, incs=[(s_wslot[i % NST], 16)])
            sch.add("vector",
                    (lambda e, i=i: e.tensor_copy(out=wb[:, i * 128:(i + 1) * 128],
                                                  in_=wstg[:, (i % NST) * 128:(i % NST + 1) * 128])),
                    waits=[(s_wslot[i % NST], 16 * (i // NST + 1))], incs=[(s_cast, 1)])

        ldx = Counter(); xc = Counter(); tp = Counter(); tpe = Counter()
        mm = Counter(); qe = Counter(); mv = Counter(); ve = Counter(); st = Counter()

        for b in range(B):
            for t in range(NTT):
                i = b * NTT + t
                w = [(s_xc, i - 1)] if i >= 2 else []
                sch.add("sync",
                        (lambda e, b=b, t=t, i=i: e.dma_start(
                            out=xtk[:, (i % 2) * D:(i % 2 + 1) * D],
                            in_=xr[b, t * 128:(t + 1) * 128, :])),
                        waits=w, incs=[(s_xslot[i % 2], 16)])
                ldx.inc()
                cw = [(s_xslot[i % 2], 16 * (i // 2 + 1))]
                if i >= 2:
                    cw.append((s_tp, 8 * (i - 1)))
                sch.add("vector",
                        (lambda e, i=i: e.tensor_copy(
                            out=xtb[:, (i % 2) * D:(i % 2 + 1) * D],
                            in_=xtk[:, (i % 2) * D:(i % 2 + 1) * D])),
                        waits=cw, incs=[(s_xc, 1)])
                xc.inc()
                for kc in range(NKC):
                    j = i * NKC + kc
                    tw = [(s_xc, xc.v), (s_ld, 16)]
                    if j >= 2:
                        tw.append((s_tpe, j - 1))
                    sch.add("tensor",
                            (lambda e, i=i, kc=kc, j=j: e.transpose(
                                out=(pst0 if j % 2 == 0 else pst1)[:, 0:128],
                                in_=xtb[:, (i % 2) * D + kc * 128:(i % 2) * D + (kc + 1) * 128],
                                identity=idt[:])),
                            waits=tw, incs=[(s_tp, 1)])
                    tp.inc()
                    sch.add("vector",
                            (lambda e, t=t, kc=kc, j=j: e.tensor_copy(
                                out=xfm[:, kc * SL + t * 128: kc * SL + (t + 1) * 128],
                                in_=(pst0 if j % 2 == 0 else pst1)[:, 0:128])),
                            waits=[(s_tp, tp.v)], incs=[(s_tpe, 1)])
                    tpe.inc()

            for t2 in range(2):
                for mc in range(NKC):
                    for blk in range(2):
                        m = ((b * 2 + t2) * NKC + mc) * 2 + blk
                        w = [(s_tpe, tpe.v), (s_cast, NW)]
                        if m >= 2:
                            w.append((s_qe, m - 1))
                        def mmf(e, t2=t2, mc=mc, blk=blk, m=m):
                            last = None
                            for kc in range(NKC):
                                last = e.matmul(
                                    out=ps_qk[:, (m % 2) * 512:(m % 2) * 512 + 512],
                                    lhsT=wb[:, (t2 * 64 + kc * NKC + mc) * 128:(t2 * 64 + kc * NKC + mc + 1) * 128],
                                    rhs=xfm[:, kc * SL + blk * 512: kc * SL + blk * 512 + 512],
                                    start=(kc == 0), stop=(kc == NKC - 1))
                            return last
                        sch.add("tensor", mmf, waits=w, incs=[(s_mm, 1)])
                        mm.inc()
                        ew = [(s_mm, mm.v)]
                        if m >= 4:
                            ew.append((s_st, 16 * (st.v - 3)))
                        sch.add("vector",
                                (lambda e, m=m: e.tensor_copy(
                                    out=qksb[:, (m % 4) * 512:(m % 4 + 1) * 512],
                                    in_=ps_qk[:, (m % 2) * 512:(m % 2) * 512 + 512])),
                                waits=ew, incs=[(s_qe, 1)])
                        qe.inc()
                        sch.add("sync",
                                (lambda e, t2=t2, b=b, mc=mc, blk=blk, m=m: e.dma_start(
                                    out=qk_o[t2, b, mc, :, blk * 512:blk * 512 + 512],
                                    in_=qksb[:, (m % 4) * 512:(m % 4 + 1) * 512])),
                                waits=[(s_qe, qe.v)], incs=[(s_st, 16)])
                        st.inc()

            for t in range(NTT):
                for fh in range(2):
                    m = (b * NTT + t) * 2 + fh
                    w = [(s_tpe, tpe.v), (s_cast, NW)]
                    if m >= 2:
                        w.append((s_ve, m - 1))
                    def mvf(e, t=t, fh=fh, m=m):
                        last = None
                        for kc in range(NKC):
                            last = e.matmul(
                                out=ps_v[:, (m % 2) * 512:(m % 2) * 512 + 512],
                                lhsT=xfm[:, kc * SL + t * 128: kc * SL + (t + 1) * 128],
                                rhs=wb[:, (2 * 64 + kc * NKC + fh * 4) * 128:(2 * 64 + kc * NKC + fh * 4 + 4) * 128],
                                start=(kc == 0), stop=(kc == NKC - 1))
                        return last
                    sch.add("tensor", mvf, waits=w, incs=[(s_mv, 1)])
                    mv.inc()
                    ew = [(s_mv, mv.v)]
                    if m >= 4:
                        ew.append((s_st, 16 * st.v - 16 * 3))
                    sch.add("vector",
                            (lambda e, m=m, fh=fh: e.tensor_copy(
                                out=vsb[:, (m % 4) * 512:(m % 4 + 1) * 512],
                                in_=ps_v[:, (m % 2) * 512:(m % 2) * 512 + 512])),
                            waits=ew, incs=[(s_ve, 1)])
                    ve.inc()
                    sch.add("sync",
                            (lambda e, b=b, t=t, fh=fh, m=m: e.dma_start(
                                out=v_o[b, t * 128:(t + 1) * 128, fh * 512:fh * 512 + 512],
                                in_=vsb[:, (m % 4) * 512:(m % 4 + 1) * 512])),
                            waits=[(s_ve, ve.v)], incs=[(s_st, 16)])
                    st.inc()

        sch.add("sync", lambda e: None, waits=[(s_st, 16 * st.v)])
        sch.run()
    return nc


def run_qkv1(inputs, x2_tok):
    if "qkv1" not in _cache:
        _cache["qkv1"] = build_qkv1()
    nc = _cache["qkv1"]
    wq = _wchunks(inputs["L1_attn_w_q"].astype(f32))
    wk = _wchunks(inputs["L1_attn_w_k"].astype(f32))
    wv = _wchunks(inputs["L1_attn_w_v"].astype(f32))
    wqkv = np.stack([wq, wk, wv])
    ident = np.eye(128, dtype=BF)
    maps = []
    for c in range(NC):
        sl = slice(c * SL, (c + 1) * SL)
        maps.append({"xr": np.ascontiguousarray(x2_tok[:, sl, :]), "wqkv": wqkv, "identb": ident})
    res = run_bass_kernel_spmd(nc, maps, core_ids=list(range(NC)))
    qk = np.concatenate([r["qk_o"] for r in res.results], axis=4)
    v = np.concatenate([r["v_o"] for r in res.results], axis=1)
    return qk, v


CSH = 20.0


def build_attn1():
    nc = bass.Bass("TRN2", target_bir_lowering=False, debug=False)
    q_in = nc.dram_tensor("q_in", [B, 128, S], BF16, kind="ExternalInput").ap()
    k_in = nc.dram_tensor("k_in", [B, 128, S], BF16, kind="ExternalInput").ap()
    v_in = nc.dram_tensor("v_in", [2, B, S, DK], BF16, kind="ExternalInput").ap()
    sel64 = nc.dram_tensor("sel64", [65, 64], BF16, kind="ExternalInput").ap()
    o_o = nc.dram_tensor("o_o", [B, 128, S], BF16, kind="ExternalOutput").ap()

    NQB = S // 512
    NKCH = S // 128
    es = contextlib.ExitStack()
    with es:
        qsb = es.enter_context(nc.sbuf_tensor("qsb", [128, S], BF16))
        ksb = es.enter_context(nc.sbuf_tensor("ksb", [128, S], BF16))
        vsb = es.enter_context(nc.sbuf_tensor("vsb", [128, NKCH * 65], BF16))
        psb = [es.enter_context(nc.sbuf_tensor(f"psb{j}", [128, 1024], BF16)) for j in range(3)]
        osb = es.enter_context(nc.sbuf_tensor("osb", [128, S], BF16))
        oev = es.enter_context(nc.sbuf_tensor("oev", [128, 2 * 512], BF16))
        slt = es.enter_context(nc.sbuf_tensor("slt", [65, 64], BF16))
        sps = [es.enter_context(nc.psum_tensor(f"sps{j}", [128, 1024], F32)) for j in range(3)]
        ops = [es.enter_context(nc.psum_tensor(f"ops{j}", [128, 512], F32)) for j in range(1)]
        lbp = [es.enter_context(nc.psum_tensor(f"lbp{j}", [64, 512], F32)) for j in range(1)]
        s_ld = es.enter_context(nc.semaphore("s_ld"))
        s_vs = es.enter_context(nc.semaphore("s_vs"))
        s_smm = es.enter_context(nc.semaphore("s_smm"))
        s_ex = es.enter_context(nc.semaphore("s_ex"))
        s_pv = es.enter_context(nc.semaphore("s_pv"))
        s_oe = es.enter_context(nc.semaphore("s_oe"))
        s_rc = es.enter_context(nc.semaphore("s_rc"))
        s_bc = es.enter_context(nc.semaphore("s_bc"))
        s_on = es.enter_context(nc.semaphore("s_on"))
        s_st = es.enter_context(nc.semaphore("s_st"))

        sch = Sched(nc)
        sch._nopt = slt

        sch.add("sync", lambda e: e.dma_start(out=slt[:], in_=sel64[:, :]), incs=[(s_ld, 16)])

        ld = Counter(); vs = Counter(); smm = Counter(); ex = Counter()
        pv = Counter(); oe = Counter(); rc = Counter(); bc = Counter()
        on = Counter(); st = Counter()
        prg = Counter()
        qg = Counter()

        for b in range(B):
            lw = []
            if b:
                lw = [(s_smm, smm.v), (s_pv, pv.v)]
            def ldf(e, b=b):
                e.dma_start(out=qsb[:], in_=q_in[b, :, :]).then_inc(s_ld, 16)
                e.dma_start(out=ksb[:], in_=k_in[b, :, :]).then_inc(s_ld, 16)
                return None
            sch.add("sync", ldf, waits=lw)
            ld.inc(2)
            for hl in range(2):
                hr = slice(hl * 64, hl * 64 + 64)
                vw = [(s_pv, pv.v)] if (b or hl) else []
                sch.add("sync",
                        (lambda e, b=b, hl=hl: e.dma_start(
                            out=vsb[:].rearrange("p (c j) -> p c j", j=65)[:, :, 0:64],
                            in_=v_in[hl, b].rearrange("(c p) j -> p c j", p=128))),
                        waits=vw, incs=[(s_vs, 16)])
                vs.inc()
                sch.add("vector",
                        (lambda e: e.memset(
                            vsb[:].rearrange("p (c j) -> p c j", j=65)[:, :, 64:65], 1.0)),
                        waits=[(s_vs, 17 * (2 * b + hl) + 16)], incs=[(s_vs, 1)])
                vs.inc()

                NPR = NQB * (NKCH // 2)
                pr0 = prg.v

                def emit_smf(p, b=b, hl=hl):
                    qb, pr = divmod(p - pr0, NKCH // 2)
                    w = [(s_ld, 16 + 32 * (b + 1))]
                    if p >= 3:
                        w.append((s_ex, p - 2))
                    def smf(e, hl=hl, qb=qb, pr=pr, p=p):
                        hr = slice(hl * 64, hl * 64 + 64)
                        last = None
                        for half in range(2):
                            kc = pr * 2 + half
                            last = e.matmul(
                                out=sps[p % 3][:, half * 512:(half + 1) * 512],
                                lhsT=ksb[hr, kc * 128:(kc + 1) * 128],
                                rhs=qsb[hr, qb * 512:(qb + 1) * 512],
                                start=True, stop=True)
                        return last
                    sch.add("tensor", smf, waits=w, incs=[(s_smm, 1)])
                    smm.inc()

                for p in range(3):
                    emit_smf(pr0 + p)

                for px in range(NPR):
                    p = pr0 + px
                    qb, pr = divmod(px, NKCH // 2)
                    qgi = qg.v + qb
                    w = [(s_smm, p + 1)]
                    if p >= 3:
                        w.append((s_pv, 2 * (p - 2)))
                    sch.add("scalar",
                            (lambda e, p=p: e.activation(
                                out=psb[p % 3][:], in_=sps[p % 3][:],
                                func=AF.Exp, scale=np.float32(0.125), bias=np.float32(-CSH))),
                            waits=w, incs=[(s_ex, 1)])
                    ex.inc()
                    for half in range(2):
                        kc = pr * 2 + half
                        w = [(s_ex, ex.v), (s_vs, 17 * (2 * b + hl + 1))]
                        if pr == 0 and half == 0 and qgi >= 1:
                            w.append((s_oe, qgi))
                        def pvf(e, kc=kc, half=half, p=p):
                            return e.matmul(
                                out=ops[0][0:65, :],
                                lhsT=vsb[:, kc * 65:(kc + 1) * 65],
                                rhs=psb[p % 3][:, half * 512:(half + 1) * 512],
                                start=(kc == 0), stop=(kc == NKCH - 1))
                        sch.add("tensor", pvf, waits=w, incs=[(s_pv, 1)])
                        pv.inc()
                    if px + 3 < NPR:
                        emit_smf(p + 3)
                    prg.inc()
                    if pr == NKCH // 2 - 1:
                        ew = [(s_pv, pv.v)]
                        if qgi >= 1:
                            ew.append((s_on, qgi))
                        sch.add("vector",
                                (lambda e: e.tensor_copy(
                                    out=oev[0:65, 0:512], in_=ops[0][0:65, :])),
                                waits=ew, incs=[(s_oe, 1)])
                        oe.inc()
                        def rcf(e):
                            with nc.allow_low_precision(reason="softmax 1/l in bf16"):
                                return e.reciprocal(out=oev[64:65, 0:512], in_=oev[64:65, 0:512])
                        sch.add("vector", rcf, waits=[(s_oe, oe.v)], incs=[(s_rc, 1)])
                        rc.inc()
                        w = [(s_rc, rc.v), (s_ld, 16)]
                        if qgi >= 1:
                            w.append((s_on, qgi))
                        sch.add("tensor",
                                (lambda e: e.matmul(
                                    out=lbp[0][:, :], lhsT=slt[:, :],
                                    rhs=oev[0:65, 0:512], start=True, stop=True)),
                                waits=w, incs=[(s_bc, 1)])
                        bc.inc()
                        sch.add("vector",
                                (lambda e, hl=hl, qb=qb: e.tensor_mul(
                                    out=osb[hl * 64:hl * 64 + 64, qb * 512:(qb + 1) * 512],
                                    in0=oev[0:64, 0:512], in1=lbp[0][:, :])),
                                waits=[(s_bc, bc.v)], incs=[(s_on, 1)])
                        on.inc()
                qg.inc(NQB)
            sch.add("sync", (lambda e, b=b: e.dma_start(out=o_o[b, :, :], in_=osb[:])),
                    waits=[(s_on, on.v)], incs=[(s_st, 16)])
            st.inc()
            sch.add("vector", lambda e: None, waits=[(s_st, 16 * st.v)])

        sch.add("sync", lambda e: None, waits=[(s_st, 16 * st.v)])
        sch.run()
    return nc


def run_attn1(qk_hs, v_full):
    if "attn1" not in _cache:
        _cache["attn1"] = build_attn1()
    nc = _cache["attn1"]
    sel = np.zeros((65, 64), dtype=BF)
    sel[64, :] = 1
    maps = []
    for c in range(NC):
        vv = np.stack([np.ascontiguousarray(v_full[:, :, (2 * c + hl) * 64:(2 * c + hl + 1) * 64])
                       for hl in range(2)])
        maps.append({"q_in": np.ascontiguousarray(qk_hs[0, :, c]),
                     "k_in": np.ascontiguousarray(qk_hs[1, :, c]),
                     "v_in": vv, "sel64": sel})
    res = run_bass_kernel_spmd(nc, maps, core_ids=list(range(NC)))
    return np.stack([r["o_o"] for r in res.results], axis=0)


def build_ffn1():
    nc = bass.Bass("TRN2", target_bir_lowering=False, debug=False)
    xfm_in = nc.dram_tensor("xfm_in", [B, NKC, 128, SL], BF16, kind="ExternalInput").ap()
    xr = nc.dram_tensor("xr", [B, SL, D], F32, kind="ExternalInput").ap()
    wfc = nc.dram_tensor("wfc", [NKC, 32, 128, 128], F32, kind="ExternalInput").ap()
    wpj = nc.dram_tensor("wpj", [32, NKC, 128, 128], F32, kind="ExternalInput").ap()
    identb = nc.dram_tensor("identb", [128, 128], BF16, kind="ExternalInput").ap()
    out_t = nc.dram_tensor("out_t", [B, SL, D], F32, kind="ExternalOutput").ap()

    NST = 4
    NW = 512
    es = contextlib.ExitStack()
    with es:
        wstg = es.enter_context(nc.sbuf_tensor("wstg", [128, NST * 128], F32))
        wfb = es.enter_context(nc.sbuf_tensor("wfb", [128, 256 * 128], BF16))
        wpb = es.enter_context(nc.sbuf_tensor("wpb", [128, 256 * 128], BF16))
        idt = es.enter_context(nc.sbuf_tensor("idt", [128, 128], BF16))
        xfm = es.enter_context(nc.sbuf_tensor("xfm", [128, NKC * 512], BF16))
        hb = es.enter_context(nc.sbuf_tensor("hb", [128, 32 * 512], BF16))
        hrl = es.enter_context(nc.sbuf_tensor("hrl", [128, 512], BF16))
        yt = es.enter_context(nc.sbuf_tensor("yt", [128, 512], F32))
        mem = es.enter_context(nc.sbuf_tensor("mem", [128, 512], F32))
        spk = es.enter_context(nc.sbuf_tensor("spk", [128, 512], F32))
        m1 = es.enter_context(nc.sbuf_tensor("m1", [128, 512], F32))
        accfm = es.enter_context(nc.sbuf_tensor("accfm", [128, NKC * 512], BF16))
        xres = es.enter_context(nc.sbuf_tensor("xres", [128, 4 * D], F32))
        xsq = es.enter_context(nc.sbuf_tensor("xsq", [128, D], F32))
        scol = es.enter_context(nc.sbuf_tensor("scol", [128, 16], F32))
        ps_h = es.enter_context(nc.psum_tensor("ps_h", [128, 1024], F32))
        ps_y = es.enter_context(nc.psum_tensor("ps_y", [128, 1024], F32))
        pst0 = es.enter_context(nc.psum_tensor("pst0", [128, 512], BF16))
        pst1 = es.enter_context(nc.psum_tensor("pst1", [128, 512], BF16))
        s_wslot = [es.enter_context(nc.semaphore(f"s_wsl{j}")) for j in range(NST)]
        s_cast = es.enter_context(nc.semaphore("s_cast"))
        s_ld = es.enter_context(nc.semaphore("s_ld"))
        s_ldx = es.enter_context(nc.semaphore("s_ldx"))
        s_ldr = es.enter_context(nc.semaphore("s_ldr"))
        s_mh = es.enter_context(nc.semaphore("s_mh"))
        s_he = es.enter_context(nc.semaphore("s_he"))
        s_my = es.enter_context(nc.semaphore("s_my"))
        s_sn = es.enter_context(nc.semaphore("s_sn"))
        s_tp = es.enter_context(nc.semaphore("s_tp"))
        s_ra = es.enter_context(nc.semaphore("s_ra"))
        s_nrm = es.enter_context(nc.semaphore("s_nrm"))
        s_st = es.enter_context(nc.semaphore("s_st"))

        sch = Sched(nc)
        sch._nopt = wstg

        sch.add("sync", lambda e: e.dma_start(out=idt[:], in_=identb[:, :]), incs=[(s_ld, 16)])
        for i in range(NW):
            waits = [(s_cast, i - NST + 1)] if i >= NST else []
            def ldw(e, i=i):
                if i < 256:
                    src = wfc[i // 32, i % 32, :, :]
                else:
                    j = i - 256
                    src = wpj[j // NKC, j % NKC, :, :]
                return e.dma_start(out=wstg[:, (i % NST) * 128:(i % NST + 1) * 128], in_=src)
            sch.add("sync", ldw, waits=waits, incs=[(s_wslot[i % NST], 16)])
            def cst(e, i=i):
                dst = wfb if i < 256 else wpb
                j = i if i < 256 else i - 256
                return e.tensor_copy(out=dst[:, j * 128:(j + 1) * 128],
                                     in_=wstg[:, (i % NST) * 128:(i % NST + 1) * 128])
            sch.add("vector", cst, waits=[(s_wslot[i % NST], 16 * (i // NST + 1))], incs=[(s_cast, 1)])

        ldx = Counter(); ldr = Counter(); mh = Counter(); he = Counter()
        my = Counter(); sn = Counter(); tp = Counter(); ra = Counter()
        nrm = Counter(); st = Counter()

        for b in range(B):
            for hf in range(2):
                hx = b * 2 + hf
                tp_h = tp.v
                st_h = st.v
                lw = [(s_mh, mh.v)] if hx else []
                def ldxf(e, b=b, hf=hf):
                    for kc in range(NKC):
                        e.dma_start(out=xfm[:, kc * 512:(kc + 1) * 512],
                                    in_=xfm_in[b, kc, :, hf * 512:(hf + 1) * 512]).then_inc(s_ldx, 16)
                    return None
                sch.add("sync", ldxf, waits=lw)
                ldx.inc(NKC)
                lw2 = [(s_ra, ra.v)] if hx else []
                def ldrf(e, b=b, hf=hf):
                    for t in range(4):
                        tt = hf * 4 + t
                        e.dma_start(out=xres[:, t * D:(t + 1) * D],
                                    in_=xr[b, tt * 128:(tt + 1) * 128, :]).then_inc(s_ldr, 16)
                    return None
                if hx:
                    lw2.append((s_st, 16 * st_h))
                sch.add("sync", ldrf, waits=lw2)
                ldr.inc(4)

                for mc in range(32):
                    m = hx * 32 + mc
                    w = [(s_ldx, 16 * ldx.v), (s_cast, 256)]
                    if m >= 2:
                        w.append((s_he, 2 * (m - 1)))
                    def mhf(e, mc=mc, m=m):
                        last = None
                        for kc in range(NKC):
                            last = e.matmul(
                                out=ps_h[:, (m % 2) * 512:(m % 2) * 512 + 512],
                                lhsT=wfb[:, (kc * 32 + mc) * 128:(kc * 32 + mc + 1) * 128],
                                rhs=xfm[:, kc * 512:(kc + 1) * 512],
                                start=(kc == 0), stop=(kc == NKC - 1))
                        return last
                    sch.add("tensor", mhf, waits=w, incs=[(s_mh, 1)])
                    mh.inc()
                    def ev1(e, m=m):
                        return e.tensor_scalar_max(out=hrl[:], in0=ps_h[:, (m % 2) * 512:(m % 2) * 512 + 512],
                                                   scalar1=0.0)
                    sch.add("vector", ev1, waits=[(s_mh, mh.v)], incs=[(s_he, 1)])
                    he.inc()
                    def ev2(e, mc=mc):
                        return e.tensor_mul(out=hb[:, mc * 512:(mc + 1) * 512], in0=hrl[:], in1=hrl[:])
                    sch.add("vector", ev2, waits=[(s_he, he.v)], incs=[(s_he, 1)])
                    he.inc()

                for mc2 in range(NKC):
                    m = hx * NKC + mc2
                    w = [(s_he, he.v), (s_cast, NW)]
                    if m >= 2:
                        w.append((s_sn, 19 * (m - 1)))
                    def myf(e, mc2=mc2, m=m):
                        last = None
                        for kc2 in range(32):
                            last = e.matmul(
                                out=ps_y[:, (m % 2) * 512:(m % 2) * 512 + 512],
                                lhsT=wpb[:, (kc2 * NKC + mc2) * 128:(kc2 * NKC + mc2 + 1) * 128],
                                rhs=hb[:, kc2 * 512:(kc2 + 1) * 512],
                                start=(kc2 == 0), stop=(kc2 == 31))
                        return last
                    sch.add("tensor", myf, waits=w, incs=[(s_my, 1)])
                    my.inc()
                    def snn(e, mc2=mc2, m=m):
                        ps = ps_y[:, (m % 2) * 512:(m % 2) * 512 + 512]
                        e.tensor_copy(out=yt[:], in_=ps)
                        e.tensor_copy(out=mem[:], in_=yt[:])
                        e.tensor_single_scalar(out=spk[:], in_=mem[:], scalar=1.0, op=ALU.is_gt)
                        e.tensor_copy(out=accfm[:, mc2 * 512:(mc2 + 1) * 512], in_=spk[:])
                        for it in range(4):
                            e.scalar_tensor_tensor(out=m1[:], in0=mem[:], scalar=0.9, in1=yt[:],
                                                   op0=ALU.mult, op1=ALU.add)
                            e.tensor_sub(out=mem[:], in0=m1[:], in1=spk[:])
                            e.tensor_single_scalar(out=spk[:], in_=mem[:], scalar=1.0, op=ALU.is_gt)
                            e.tensor_tensor(out=accfm[:, mc2 * 512:(mc2 + 1) * 512],
                                            in0=accfm[:, mc2 * 512:(mc2 + 1) * 512], in1=spk[:],
                                            op=ALU.add)
                        return e.tensor_copy(out=accfm[:, mc2 * 512:(mc2 + 1) * 512],
                                             in_=accfm[:, mc2 * 512:(mc2 + 1) * 512])
                    sch.add("vector", snn, waits=[(s_my, my.v), (s_tp, tp_h)], incs=[(s_sn, 19)])
                    sn.inc(19)

                for t in range(4):
                    tt = hf * 4 + t
                    for mc2 in range(NKC):
                        j = (hx * 4 + t) * NKC + mc2
                        w = [(s_sn, sn.v), (s_ld, 16), (s_ldr, 16 * ldr.v)]
                        if j >= 2:
                            w.append((s_ra, j - 1))
                        sch.add("tensor",
                                (lambda e, t=t, mc2=mc2, j=j: e.transpose(
                                    out=(pst0 if j % 2 == 0 else pst1)[:, 0:128],
                                    in_=accfm[:, mc2 * 512 + t * 128: mc2 * 512 + (t + 1) * 128],
                                    identity=idt[:])),
                                waits=w, incs=[(s_tp, 1)])
                        tp.inc()
                        sch.add("vector",
                                (lambda e, t=t, mc2=mc2, j=j: e.scalar_tensor_tensor(
                                    out=xres[:, t * D + mc2 * 128: t * D + (mc2 + 1) * 128],
                                    in0=(pst0 if j % 2 == 0 else pst1)[:, 0:128],
                                    scalar=0.2, in1=xres[:, t * D + mc2 * 128: t * D + (mc2 + 1) * 128],
                                    op0=ALU.mult, op1=ALU.add)),
                                waits=[(s_tp, tp.v)], incs=[(s_ra, 1)])
                        ra.inc()
                    _emit_ln(sch, nc, xres, xsq, scol, t, s_ra, ra.v, s_nrm, nrm, AF, ALU)
                    sch.add("sync",
                            (lambda e, b=b, tt=tt, t=t: e.dma_start(
                                out=out_t[b, tt * 128:(tt + 1) * 128, :],
                                in_=xres[:, t * D:(t + 1) * D])),
                            waits=[(s_nrm, nrm.v)], incs=[(s_st, 16)])
                    st.inc()

        sch.add("sync", lambda e: None, waits=[(s_st, 16 * st.v)])
        sch.run()
    return nc


def run_ffn1(inputs, x3_fm, x3_tok):
    if "ffn1" not in _cache:
        _cache["ffn1"] = build_ffn1()
    nc = _cache["ffn1"]
    wfc = _wchunks_T(inputs["L1_ffn_W_fc"].astype(f32))
    wpj = _wchunks_T(inputs["L1_ffn_W_proj"].astype(f32))
    ident = np.eye(128, dtype=BF)
    maps = []
    for c in range(NC):
        sl = slice(c * SL, (c + 1) * SL)
        maps.append({"xfm_in": np.ascontiguousarray(x3_fm[:, :, :, sl]),
                     "xr": np.ascontiguousarray(x3_tok[:, sl, :]),
                     "wfc": wfc, "wpj": wpj, "identb": ident})
    res = run_bass_kernel_spmd(nc, maps, core_ids=list(range(NC)))
    return np.concatenate([r["out_t"] for r in res.results], axis=1)
